# version 06
# speedup vs baseline: 1.3007x; 1.1285x over previous
import contextlib
import numpy as np
import ml_dtypes
import concourse.bass as bass
import concourse.mybir as mybir
import bass_rust
from concourse.bass_utils import run_bass_kernel_spmd

F32 = mybir.dt.float32
BF16 = mybir.dt.bfloat16
I32 = mybir.dt.int32
U32 = mybir.dt.uint32
AF = mybir.ActivationFunctionType
ALU = mybir.AluOpType
AXL = mybir.AxisListType

NCORES = 8
BLOC = 2
T = 2048
D = 2048
TOK = BLOC * T
NTT = TOK // 128
NKD = D // 128

bf = ml_dtypes.bfloat16


def perm_deint(n=128):
    return np.concatenate([np.arange(0, n, 2), np.arange(1, n, 2)])


class K:

    def __init__(self, nc, block, es, prefix=""):
        self.nc = nc
        self.block = block
        self.es = es
        self.sems = {}
        self.cnt = {}
        self.prefix = prefix

    @staticmethod
    def sub(parent, es, prefix=None):
        kk = K.__new__(K)
        kk.nc, kk.block, kk.es = parent.nc, parent.block, es
        kk.sems, kk.cnt = parent.sems, parent.cnt
        kk.prefix = parent.prefix if prefix is None else prefix
        return kk

    def sem(self, name):
        if name not in self.sems:
            self.sems[name] = self.es.enter_context(self.nc.semaphore(name))
            self.cnt[name] = 0
        return self.sems[name]

    def inc(self, name, inst, n=1):
        inst.then_inc(self.sem(name), n)
        self.cnt[name] += n
        return self.cnt[name]

    def dma(self, eng, out, in_, name):
        return self.inc(name, eng.dma_start(out=out, in_=in_), 16)

    def sb(self, name, shape, dtype):
        return self.es.enter_context(
            self.nc.sbuf_tensor(self.prefix + name, shape, dtype))

    def ps(self, name, shape, dtype=F32):
        return self.es.enter_context(
            self.nc.psum_tensor(self.prefix + name, shape, dtype))

    def barrier(self):
        block = self.block
        snap = [(n, self.cnt[n]) for n in self.sems if self.cnt[n] > 0]

        @block.tensor
        def _(t):
            for n, c in snap:
                t.wait_ge(self.sems[n], c)

        @block.vector
        def _(v):
            for n, c in snap:
                v.wait_ge(self.sems[n], c)

        @block.scalar
        def _(s):
            for n, c in snap:
                s.wait_ge(self.sems[n], c)

        @block.gpsimd
        def _(g):
            for n, c in snap:
                g.wait_ge(self.sems[n], c)

        @block.sync
        def _(sy):
            for n, c in snap:
                sy.wait_ge(self.sems[n], c)


def phase_ln(k, x_hbm, ident=None, xnT_sb=None, xn_hbm=None, rowsum_sb=None,
             eps=1e-5, tag="ln", in_dtype=F32, lnw_rep=None, lnb_rep=None,
             out_bf_hbm=None):
    nc, block = k.nc, k.block
    x_t = x_hbm.rearrange("(i p) d -> i p d", p=128)
    xs = [k.sb(f"{tag}_xs{j}", [128, D], in_dtype) for j in range(2)]
    xnb = [k.sb(f"{tag}_xnb{j}", [128, D], BF16) for j in range(2)]
    xnm = k.sb(f"{tag}_xnm", [128, D], F32) if lnw_rep is not None else None
    xnf = [k.sb(f"{tag}_xnf{j}", [128, D], F32) for j in range(2)] if xn_hbm is not None else None
    sq = k.sb(f"{tag}_sq", [128, D], F32)
    mneg = k.sb(f"{tag}_mneg", [128, 1], F32)
    mneg2 = k.sb(f"{tag}_mneg2", [128, 1], F32)
    red_s = k.sb(f"{tag}_reds", [128, 1], F32)
    ssq = k.sb(f"{tag}_ssq", [128, 1], F32)
    veps = k.sb(f"{tag}_veps", [128, 1], F32)
    lncol = k.sb(f"{tag}_lncol", [128, 1], F32)
    lnh = k.sb(f"{tag}_lnh", [128, 1], F32)
    rstd = k.sb(f"{tag}_rstd", [128, 1], F32)
    if xnT_sb is not None:
        tpA = k.ps(f"{tag}_tpA", [128, 1024], BF16)
        tpB = k.ps(f"{tag}_tpB", [128, 1024], BF16)

    hist = {}
    for i in range(NTT):
        buf = i % 2

        @block.sync
        def _(sy, i=i, buf=buf):
            if i >= 2:
                sy.wait_ge(k.sem("mv"), hist[("apply", i - 2)])
            k.dma(sy, xs[buf][:], x_t[i], "qs")
        ld_done = k.cnt["qs"]

        @block.scalar
        def _(s, i=i, buf=buf):
            s.wait_ge(k.sem("qs"), ld_done)
            k.inc("ms", s.activation(sq[:], xs[buf][:], AF.Copy, accum_out=red_s[:, 0:1]))
        red_done = k.cnt["ms"]

        @block.vector
        def _(v, i=i, buf=buf):
            v.wait_ge(k.sem("ms"), red_done)
            k.inc("mv", v.tensor_scalar(mneg[:, 0:1], red_s[:, 0:1], -1.0 / D, None, ALU.mult))
        mneg_done = k.cnt["mv"]

        @block.scalar
        def _(s, i=i, buf=buf):
            s.wait_ge(k.sem("mv"), mneg_done)
            k.inc("ms", s.activation(sq[:], xs[buf][:], AF.Square, bias=mneg[:, 0:1],
                                     accum_out=ssq[:, 0:1]))
        ssq_done = k.cnt["ms"]

        @block.vector
        def _(v, i=i, buf=buf):
            v.wait_ge(k.sem("ms"), ssq_done)
            k.inc("mv", v.tensor_scalar(veps[:, 0:1], ssq[:, 0:1], 1.0 / D, eps, ALU.mult, ALU.add))
        veps_done = k.cnt["mv"]

        @block.scalar
        def _(s, i=i, buf=buf):
            s.wait_ge(k.sem("mv"), veps_done)
            k.inc("ms", s.activation(lncol[:, 0:1], veps[:, 0:1], AF.Ln))
        ln_done = k.cnt["ms"]

        @block.vector
        def _(v, i=i, buf=buf):
            v.wait_ge(k.sem("ms"), ln_done)
            k.inc("mv", v.tensor_scalar(lnh[:, 0:1], lncol[:, 0:1], -0.5, None, ALU.mult))
        lnh_done = k.cnt["mv"]

        @block.scalar
        def _(s, i=i, buf=buf):
            s.wait_ge(k.sem("mv"), lnh_done)
            s.activation(rstd[:, 0:1], lnh[:, 0:1], AF.Exp)
            k.inc("ms", s.copy(mneg2[:, 0:1], mneg[:, 0:1]))
        rstd_done = k.cnt["ms"]

        @block.vector
        def _(v, i=i, buf=buf):
            v.wait_ge(k.sem("ms"), rstd_done)
            if xnT_sb is not None and i >= 2:
                v.wait_ge(k.sem("mt"), hist[("tp", i - 2)])
            if xnf is not None and i >= 2:
                v.wait_ge(k.sem("qg"), hist[("xnf", i - 2)])
            if out_bf_hbm is not None and i >= 2:
                v.wait_ge(k.sem("qs"), hist[("xnbst", i - 2)])
            if lnw_rep is not None:
                v.tensor_scalar(xnm[:], xs[buf][:], mneg2[:, 0:1], rstd[:, 0:1],
                                ALU.add, ALU.mult)
                v.tensor_tensor(xnm[:], xnm[:], lnw_rep[:], ALU.mult)
                k.inc("mv", v.tensor_tensor(xnb[buf][:], xnm[:], lnb_rep[:], ALU.add))
            else:
                k.inc("mv", v.tensor_scalar(xnb[buf][:], xs[buf][:], mneg2[:, 0:1], rstd[:, 0:1],
                                            ALU.add, ALU.mult))
            if xnf is not None:
                k.inc("mv", v.tensor_scalar(xnf[buf][:], xs[buf][:], mneg2[:, 0:1], rstd[:, 0:1],
                                            ALU.add, ALU.mult))
            if rowsum_sb is not None:
                k.inc("mv", v.tensor_scalar(sq[:], xnb[buf][:], 1.0, None, ALU.mult,
                                            ALU.add, accum_out=rowsum_sb[:, i:i + 1]))
        hist[("apply", i)] = k.cnt["mv"]
        xn_done = k.cnt["mv"]

        if xnf is not None:
            @block.gpsimd
            def _(g, i=i, buf=buf):
                g.wait_ge(k.sem("mv"), xn_done)
                k.dma(g, xn_hbm.rearrange("(i p) d -> i p d", p=128)[i], xnf[buf][:], "qg")
            hist[("xnf", i)] = k.cnt["qg"]

        if out_bf_hbm is not None:
            @block.sync
            def _(sy, i=i, buf=buf):
                sy.wait_ge(k.sem("mv"), xn_done)
                k.dma(sy, out_bf_hbm.rearrange("(i p) d -> i p d", p=128)[i], xnb[buf][:], "qs")
            hist[("xnbst", i)] = k.cnt["qs"]

        if xnT_sb is not None:
            @block.tensor
            def _(t, i=i, buf=buf):
                t.wait_ge(k.sem("mv"), xn_done)
                if i >= 1:
                    t.wait_ge(k.sem("ms"), hist[("tpcopy", i - 1)])
                for j in range(8):
                    mm = t.transpose(tpA[:, j * 128:(j + 1) * 128],
                                     xnb[buf][:, j * 128:(j + 1) * 128], ident[:])
                k.inc("mt", mm)
                for j in range(8, NKD):
                    mm = t.transpose(tpB[:, (j - 8) * 128:(j - 8) * 128 + 128],
                                     xnb[buf][:, j * 128:(j + 1) * 128], ident[:])
                k.inc("mt", mm)
            hist[("tp", i)] = k.cnt["mt"]

            @block.scalar
            def _(s, i=i, buf=buf):
                s.wait_ge(k.sem("mt"), hist[("tp", i)] - 1)
                for j in range(8):
                    s.copy(xnT_sb[:, j * TOK + i * 128: j * TOK + (i + 1) * 128],
                           tpA[:, j * 128:(j + 1) * 128])
                s.wait_ge(k.sem("mt"), hist[("tp", i)])
                for j in range(8, NKD):
                    inst = s.copy(xnT_sb[:, j * TOK + i * 128: j * TOK + (i + 1) * 128],
                                  tpB[:, (j - 8) * 128:(j - 8) * 128 + 128])
                k.inc("ms", inst)
            hist[("tpcopy", i)] = k.cnt["ms"]

    return hist


def phase_proj(k, xn_hbm, w_hbm, W, evac_cb, tag, evac_sem="ms", wprep_cb=None,
               KSTR=NKD, HT=T, PASSW=2048, nbuf=None, NTOK=TOK):
    nc, block = k.nc, k.block
    kdim = KSTR * 128
    npanels = NTOK // HT
    MT = HT // 128
    npass = (W + PASSW - 1) // PASSW
    if nbuf is None:
        nbuf = 2 if npass > 1 else 1
    NCH_MAX = PASSW // 512
    wsb = [k.sb(f"{tag}_w{j}", [128, KSTR * PASSW], BF16) for j in range(nbuf)]
    xnT = k.sb(f"{tag}_xnT", [128, KSTR * HT], BF16)
    pss = [k.ps(f"{tag}_ps{j}", [128, 512]) for j in range(8)]
    hist = {}

    for panel in range(npanels):
        @block.sync
        def _(sy, panel=panel):
            if panel >= 1:
                sy.wait_ge(k.sem("mt"), hist[("panelmm", panel - 1)])
            for j in range(KSTR):
                k.inc("qs", sy.dma_start_transpose(
                    xnT[:, j * HT:(j + 1) * HT],
                    xn_hbm[panel * HT:(panel + 1) * HT, j * 128:(j + 1) * 128]), 16)
        xn_ld = k.cnt["qs"]

        for p in range(npass):
            pw = min(PASSW, W - p * PASSW)
            nch = pw // 512
            pbuf = p % nbuf

            if panel == 0 or npass > 1:
                @block.gpsimd
                def _(g, p=p, pbuf=pbuf, pw=pw, panel=panel):
                    if ("wmm", pbuf) in hist:
                        g.wait_ge(k.sem("mt"), hist[("wmm", pbuf)])
                    for kk in range(KSTR):
                        k.dma(g, wsb[pbuf][:, kk * PASSW: kk * PASSW + pw],
                              w_hbm[kk * 128:(kk + 1) * 128, p * PASSW: p * PASSW + pw], "qg")
                wld = k.cnt["qg"]
                wready = ("qg", wld)
                if wprep_cb is not None:
                    wready = wprep_cb(k, p, pbuf, wsb[pbuf], pw, wld)
                hist[("wready", p)] = wready
            wready = hist[("wready", p)]

            for m in range(MT):
                gi = panel * MT + m
                grp = m % 2
                ngrp = min(4, nch)

                @block.tensor
                def _(t, p=p, m=m, gi=gi, grp=grp, pbuf=pbuf, nch=nch, wready=wready,
                      xn_ld=xn_ld, panel=panel, ngrp=ngrp):
                    if m == 0:
                        t.wait_ge(k.sem(wready[0]), wready[1])
                        t.wait_ge(k.sem("qs"), xn_ld)
                    gidx = (panel * npass + p) * MT + m
                    if gidx >= 2:
                        t.wait_ge(k.sem(evac_sem), hist[("evac", gidx - 2)])
                    for kk in range(KSTR):
                        for n in range(nch):
                            mm = t.matmul(
                                pss[grp * 4 + n][:],
                                xnT[:, kk * HT + m * 128: kk * HT + (m + 1) * 128],
                                wsb[pbuf][:, kk * PASSW + n * 512: kk * PASSW + (n + 1) * 512],
                                start=(kk == 0), stop=(kk == KSTR - 1))
                    k.inc("mt", mm)
                mm_cnt = k.cnt["mt"]
                hist[("wmm", pbuf)] = k.cnt["mt"]

                group_pss = [pss[grp * 4 + n] for n in range(nch)]
                for n in range(nch):
                    evac_cb(k, gi, p * NCH_MAX + n, pss[grp * 4 + n], mm_cnt, group_pss)
                hist[("evac", (panel * npass + p) * MT + m)] = k.cnt[evac_sem]
        hist[("panelmm", panel)] = k.cnt["mt"]
    return hist


def make_qk_evac(k, wrep, cos_hbm, sin_hbm, out_hbm, eps, tag, rope=True, do_rms=True,
                 nh_total=16, brep=None):
    nc, block = k.nc, k.block
    csb = [k.sb(tag + f"_cs{j}", [128, 1024], BF16) for j in range(2)] if rope else None
    ssb = [k.sb(tag + f"_sn{j}", [128, 1024], BF16) for j in range(2)] if rope else None
    tbl_hist = {}
    ssqb = k.sb(tag + "_ssqb", [128, 16], F32)
    sqd = k.sb(tag + "_sqd", [128, 128], F32)
    veps = k.sb(tag + "_veps", [128, 16], F32)
    lnc = k.sb(tag + "_lnc", [128, 16], F32)
    lnh = k.sb(tag + "_lnh", [128, 16], F32)
    rstd = k.sb(tag + "_rstd", [128, 16], F32)
    qn = k.sb(tag + "_qn", [128, D], F32)
    qw = k.sb(tag + "_qw", [128, D], BF16)
    t1 = k.sb(tag + "_t1", [128, 1024], F32)
    t2 = k.sb(tag + "_t2", [128, 1024], F32)
    qrot = [k.sb(tag + f"_qrot{j}", [128, D], BF16) for j in range(2)]
    out_t = out_hbm.rearrange("(i p) d -> i p d", p=128)
    st_hist = {}

    def evac(k, m, ncg, ps, mm_cnt, group_pss):
        nc4 = ncg % 4
        if rope and nc4 == 0:
            trow = m % 16

            @block.gpsimd
            def _(g):
                if m >= 2 and (m - 2) in tbl_hist:
                    g.wait_ge(k.sem("mv"), tbl_hist[m - 2])
                k.dma(g, csb[m % 2][:, 0:nh_total * 64],
                      cos_hbm[trow * 128:(trow + 1) * 128, 0:nh_total * 64], "qg")
                k.dma(g, ssb[m % 2][:, 0:nh_total * 64],
                      sin_hbm[trow * 128:(trow + 1) * 128, 0:nh_total * 64], "qg")
            tbl_hist[("ld", m)] = k.cnt["qg"]

        if do_rms:
            @block.scalar
            def _(s):
                s.wait_ge(k.sem("mt"), mm_cnt)
                for hh in range(4):
                    inst = s.activation(sqd[:], ps[:, hh * 128:(hh + 1) * 128], AF.Square,
                                        accum_out=ssqb[:, nc4 * 4 + hh: nc4 * 4 + hh + 1])
                k.inc("ms", inst)
        sq_done = k.cnt["ms"]

        if nc4 != (3 if len(group_pss) == 4 else len(group_pss) - 1):
            return

        if do_rms:
            @block.vector
            def _(v):
                v.wait_ge(k.sem("ms"), sq_done)
                k.inc("mv", v.tensor_scalar(veps[:], ssqb[:], 1.0 / 128, eps, ALU.mult, ALU.add))
            c1 = k.cnt["mv"]

            @block.scalar
            def _(s):
                s.wait_ge(k.sem("mv"), c1)
                k.inc("ms", s.activation(lnc[:], veps[:], AF.Ln))
            c2 = k.cnt["ms"]

            @block.vector
            def _(v):
                v.wait_ge(k.sem("ms"), c2)
                k.inc("mv", v.tensor_scalar(lnh[:], lnc[:], -0.5, None, ALU.mult))
            c3 = k.cnt["mv"]

            @block.scalar
            def _(s):
                s.wait_ge(k.sem("mv"), c3)
                k.inc("ms", s.activation(rstd[:], lnh[:], AF.Exp))
        c4 = k.cnt["ms"]

        buf = m % 2
        trow = m % 16

        nheads = len(group_pss) * 4

        @block.vector
        def _(v):
            v.wait_ge(k.sem("ms"), c4)
            v.wait_ge(k.sem("mt"), mm_cnt)
            if m >= 2:
                v.wait_ge(k.sem("qs"), st_hist[m - 2])
            for h in range(nheads):
                if do_rms:
                    v.tensor_scalar(qn[:, h * 128:(h + 1) * 128],
                                    group_pss[h // 4][:, (h % 4) * 128:(h % 4) * 128 + 128],
                                    rstd[:, h:h + 1], None, ALU.mult)
                else:
                    v.tensor_scalar(qn[:, h * 128:(h + 1) * 128],
                                    group_pss[h // 4][:, (h % 4) * 128:(h % 4) * 128 + 128],
                                    1.0, None, ALU.mult)
            if brep is not None:
                last = v.tensor_tensor(qw[:, 0:nheads * 128], qn[:, 0:nheads * 128],
                                       brep[:, 0:nheads * 128], ALU.add)
            elif wrep is not None:
                last = v.tensor_tensor(qw[:, 0:nheads * 128], qn[:, 0:nheads * 128],
                                       wrep[:, 0:nheads * 128], ALU.mult)
            else:
                last = v.tensor_copy(qw[:, 0:nheads * 128], qn[:, 0:nheads * 128])
            if rope:
                v.wait_ge(k.sem("qg"), tbl_hist[("ld", m)])
                nh = nheads
                qwv = qw.ap()[:, 0:nh * 128].rearrange("p (h x) -> p h x", h=nh)
                qrv = qrot[buf].ap()[:, 0:nh * 128].rearrange("p (h x) -> p h x", h=nh)
                x1, x2 = qwv[:, :, 0:64], qwv[:, :, 64:128]
                cs = csb[m % 2].ap()[:, 0:nh * 64].rearrange("p (h x) -> p h x", h=nh)
                sn = ssb[m % 2].ap()[:, 0:nh * 64].rearrange("p (h x) -> p h x", h=nh)
                t1v = t1.ap()[:, 0:nh * 64].rearrange("p (h x) -> p h x", h=nh)
                t2v = t2.ap()[:, 0:nh * 64].rearrange("p (h x) -> p h x", h=nh)
                v.tensor_tensor(t1v, x1, cs, ALU.mult)
                v.tensor_tensor(t2v, x2, sn, ALU.mult)
                v.tensor_tensor(qrv[:, :, 0:64], t1v, t2v, ALU.subtract)
                v.tensor_tensor(t1v, x1, sn, ALU.mult)
                v.tensor_tensor(t2v, x2, cs, ALU.mult)
                last = v.tensor_tensor(qrv[:, :, 64:128], t1v, t2v, ALU.add)
            else:
                last = v.tensor_copy(qrot[buf][:, 0:nheads * 128], qw[:, 0:nheads * 128])
            k.inc("mv", last)
        c5 = k.cnt["mv"]
        tbl_hist[m] = c5

        @block.sync
        def _(sy):
            sy.wait_ge(k.sem("mv"), c5)
            k.dma(sy, out_t[m][:, 0:nheads * 128], qrot[buf][:, 0:nheads * 128], "qs")
        st_hist[m] = k.cnt["qs"]

    return evac


def make_v_evac(k, out_hbm, tag, nch_total=4):
    nc, block = k.nc, k.block
    vst = [k.sb(tag + f"_vst{j}", [128, D], BF16) for j in range(2)]
    out_t = out_hbm.rearrange("(i p) d -> i p d", p=128)
    st_hist = {}

    def evac(k, m, ncg, ps, mm_cnt, group_pss):
        nc4 = ncg % 4
        buf = m % 2

        @block.scalar
        def _(s):
            s.wait_ge(k.sem("mt"), mm_cnt)
            if nc4 == 0 and m >= 2:
                s.wait_ge(k.sem("qs"), st_hist[m - 2])
            k.inc("ms", s.copy(vst[buf][:, nc4 * 512:(nc4 + 1) * 512], ps[:]))
        cp_done = k.cnt["ms"]

        if nc4 == nch_total - 1:
            @block.sync
            def _(sy):
                sy.wait_ge(k.sem("ms"), cp_done)
                k.dma(sy, out_t[m][:, 0:512 * nch_total], vst[buf][:, 0:512 * nch_total], "qs")
            st_hist[m] = k.cnt["qs"]

    return evac


def phase_attn_oproj(k, cfg):
    nc, block = k.nc, k.block
    nh = 16
    kv_rep = cfg.get("kv_rep", 1)
    KW = (nh // kv_rep) * 128
    softcap = cfg.get("softcap", None)
    scale = cfg["scale"]
    ident_bf = cfg["ident_bf"]
    ident_f32 = cfg["ident_f32"]
    ones1 = cfg["ones1"]
    ones_row1 = cfg["ones_row1"]
    q_hbm, k_hbm, v_hbm = cfg["q_hbm"], cfg["k_hbm"], cfg["v_hbm"]
    v_t = v_hbm.rearrange("(i p) d -> i p d", p=128)

    oes = contextlib.ExitStack()
    ko_ = K.__new__(K)
    ko_.nc, ko_.block, ko_.es = nc, block, oes
    ko_.sems, ko_.cnt = k.sems, k.cnt
    ko_.prefix = k.prefix
    ob_sb = ko_.sb("atob", [128, 16 * T], BF16)
    nmask = cfg["masks_hbm"].shape[0] // 128
    masks = [ko_.sb(f"atmk{j}", [128, 512], cfg["mask_dtype"]) for j in range(nmask)]
    masks2 = None
    if cfg.get("masks2_hbm") is not None:
        masks2 = [ko_.sb(f"atmk2{j}", [128, 512], BF16) for j in range(4)]

    @block.sync
    def _(sy):
        for j in range(nmask):
            ko_.dma(sy, masks[j][:], cfg["masks_hbm"][j * 128:(j + 1) * 128, :], "qs")
        if masks2 is not None:
            for j in range(4):
                ko_.dma(sy, masks2[j][:], cfg["masks2_hbm"][j * 128:(j + 1) * 128, :], "qs")

    for b in range(BLOC):
        k.barrier()
        with contextlib.ExitStack() as pes:
            ka = K.__new__(K)
            ka.nc, ka.block, ka.es = nc, block, pes
            ka.sems, ka.cnt = k.sems, k.cnt
            ka.prefix = k.prefix

            v_sb = ka.sb(f"at{b}_v", [128, 16 * KW], BF16)
            qt_sb = [ka.sb(f"at{b}_qt{j}", [128, T], BF16) for j in range(2)]
            kt_sb = [ka.sb(f"at{b}_kt{j}", [128, T], BF16) for j in range(2)]
            p_sb = [ka.sb(f"at{b}_p{j}", [128, 512], BF16) for j in range(2)]
            t_sb = ka.sb(f"at{b}_t", [128, 512], F32)
            t2_sb = ka.sb(f"at{b}_t2", [128, 512], F32)
            drow = ka.sb(f"at{b}_drow", [128, 512], F32)
            negm = ka.sb(f"at{b}_negm", [128, 512], BF16)
            mcol = ka.sb(f"at{b}_mcol", [128, 4], F32)
            lnd = ka.sb(f"at{b}_lnd", [128, 4], F32)
            mbig = ka.sb(f"at{b}_mbig", [128, 4], F32)
            mbig2 = ka.sb(f"at{b}_mbig2", [128, 4], F32)
            dhat = ka.sb(f"at{b}_dhat", [128, 4], F32)
            on_sb = ka.sb(f"at{b}_on", [128, 512], BF16)
            sdump = ka.sb(f"at{b}_sdump", [128, 512], F32)
            ps_s = [ka.ps(f"at{b}_pss{j}", [128, 512]) for j in range(2)]
            ps_o = [ka.ps(f"at{b}_pso{j}", [128, 512]) for j in range(4)]
            ps_d = ka.ps(f"at{b}_psd", [128, 512])
            ps_tp = ka.ps(f"at{b}_pstp", [128, 1024], BF16)
            ps_sp = ps_o

            @block.sync
            def _(sy):
                for m in range(16):
                    ka.dma(sy, v_sb[:, m * KW:(m + 1) * KW],
                           v_t[b * 16 + m][:, 0:KW], "qs")
            vld = k.cnt["qs"]

            hist = {}
            for h in range(nh):
                hb = h % 2
                kvh = h // kv_rep

                @block.sync
                def _(sy, h=h, hb=hb, kvh=kvh):
                    if h >= 2:
                        sy.wait_ge(k.sem("mt"), hist[("hmm", h - 2)])
                    k.inc("qs", sy.dma_start_transpose(
                        qt_sb[hb][:], q_hbm[b * T:(b + 1) * T, h * 128:(h + 1) * 128]), 16)
                    k.inc("qs", sy.dma_start_transpose(
                        kt_sb[hb][:], k_hbm[b * T:(b + 1) * T, kvh * 128:(kvh + 1) * 128]), 16)
                qk_ld = k.cnt["qs"]

                for qc in range(4):
                    ktop = 4 * (qc + 1)

                    if softcap is None:
                        for qsub in range(4):
                            for kc in range(qc + 1):
                                pb = (qsub * (qc + 1) + kc) % 4

                                @block.tensor
                                def _(t, h=h, hb=hb, qc=qc, qsub=qsub, kc=kc, pb=pb):
                                    if h == 0 and qc == 0 and qsub == 0 and kc == 0:
                                        t.wait_ge(k.sem("qs"), vld)
                                    t.wait_ge(k.sem("qs"), qk_ld)
                                    if qsub == 0 and kc == 0 and ("ofree",) in hist:
                                        t.wait_ge(k.sem("mv"), hist[("ofree",)])
                                    if ("spfree", pb) in hist:
                                        t.wait_ge(k.sem("mv"), hist[("spfree", pb)])
                                    diag = (kc == qc)
                                    mm = t.matmul(ps_sp[pb][:, 0:512],
                                                  qt_sb[hb][:, (qc * 4 + qsub) * 128:(qc * 4 + qsub + 1) * 128],
                                                  kt_sb[hb][:, kc * 512:(kc + 1) * 512],
                                                  start=True, stop=not diag, skip_group_check=True)
                                    if diag:
                                        mm = t.matmul(ps_sp[pb][:, 0:512], ident_bf[:], masks2[qsub][:],
                                                      start=False, stop=True, skip_group_check=True)
                                    k.inc("mt", mm)
                                sp_done = k.cnt["mt"]

                                @block.vector
                                def _(v, h=h, qsub=qsub, kc=kc, pb=pb):
                                    v.wait_ge(k.sem("mt"), sp_done)
                                    k.inc("mv", v.tensor_reduce(
                                        mbig[:, kc:kc + 1], ps_sp[pb][:, 0:512], AXL.X, ALU.max))
                                hist[("spfree", pb)] = k.cnt["mv"]
                            mb_done = k.cnt["mv"]

                            @block.scalar
                            def _(s, qc=qc, qsub=qsub):
                                s.wait_ge(k.sem("mv"), mb_done)
                                k.inc("ms", s.copy(mbig2[:, 0:qc + 1], mbig[:, 0:qc + 1]))
                            mb2_done = k.cnt["ms"]

                            @block.vector
                            def _(v, qc=qc, qsub=qsub):
                                v.wait_ge(k.sem("ms"), mb2_done)
                                k.inc("mv", v.tensor_reduce(
                                    mcol[:, qsub:qsub + 1], mbig2[:, 0:qc + 1],
                                    AXL.X, ALU.max))
                        mcol_done = k.cnt["mv"]

                        @block.tensor
                        def _(t, qc=qc):
                            t.wait_ge(k.sem("mv"), mcol_done)
                            if ("dtfree",) in hist:
                                t.wait_ge(k.sem("ms"), hist[("dtfree",)])
                            if ("dtfree2",) in hist:
                                t.wait_ge(k.sem("ms"), hist[("dtfree2",)])
                            for j in range(4):
                                mm = t.transpose(ps_d[0:1, j * 128:(j + 1) * 128],
                                                 mcol[:, j:j + 1], ident_f32[:])
                            k.inc("mt", mm)
                        mt_done = k.cnt["mt"]

                        @block.scalar
                        def _(s, qc=qc):
                            s.wait_ge(k.sem("mt"), mt_done)
                            k.inc("ms", s.activation(negm[0:1, 0:512], ps_d[0:1, 0:512],
                                                     AF.Copy, scale=-1.0))
                        hist[("dtfree",)] = k.cnt["ms"]
                        negm_done = k.cnt["ms"]

                    for kt in range(ktop):
                        pb = kt % 2

                        @block.tensor
                        def _(t, h=h, hb=hb, qc=qc, kt=kt, pb=pb):
                            if softcap is not None and h == 0 and qc == 0 and kt == 0:
                                t.wait_ge(k.sem("qs"), vld)
                            t.wait_ge(k.sem("qs"), qk_ld)
                            if ("sfree", pb) in hist:
                                t.wait_ge(k.sem("ms"), hist[("sfree", pb)])
                            diag = qc * 4 <= kt
                            if softcap is not None:
                                mm = t.matmul(ps_s[pb][:], kt_sb[hb][:, kt * 128:(kt + 1) * 128],
                                              qt_sb[hb][:, qc * 512:(qc + 1) * 512],
                                              start=True, stop=True, skip_group_check=True)
                            else:
                                t.wait_ge(k.sem("ms"), negm_done)
                                mm = t.matmul(ps_s[pb][:], kt_sb[hb][:, kt * 128:(kt + 1) * 128],
                                              qt_sb[hb][:, qc * 512:(qc + 1) * 512],
                                              start=True, stop=False, skip_group_check=True)
                                if diag:
                                    mm = t.matmul(ps_s[pb][:], ident_bf[:], masks[kt - qc * 4][:],
                                                  start=False, stop=False, skip_group_check=True)
                                mm = t.matmul(ps_s[pb][:], ones_row1[0:1, :], negm[0:1, 0:512],
                                              start=False, stop=True, skip_group_check=True)
                            k.inc("mt", mm)
                        sbar_done = k.cnt["mt"]

                        if softcap is not None:
                            diag = qc * 4 <= kt

                            @block.scalar
                            def _(s, kt=kt, pb=pb):
                                s.wait_ge(k.sem("mt"), sbar_done)
                                if ("pfree", pb) in hist:
                                    s.wait_ge(k.sem("mt"), hist[("pfree", pb)])
                                k.inc("ms", s.activation(t_sb[:], ps_s[pb][:], AF.Tanh,
                                                         scale=1.0 / (softcap / scale)))
                            tanh_done = k.cnt["ms"]
                            if diag:
                                @block.vector
                                def _(v, qc=qc, kt=kt):
                                    v.wait_ge(k.sem("ms"), tanh_done)
                                    k.inc("mv", v.tensor_tensor(t2_sb[:], t_sb[:],
                                                                masks[kt - qc * 4][:], ALU.add))
                                madd_done = k.cnt["mv"]

                                @block.scalar
                                def _(s, kt=kt, pb=pb):
                                    s.wait_ge(k.sem("mv"), madd_done)
                                    k.inc("ms", s.activation(p_sb[pb][:], t2_sb[:], AF.Exp,
                                                             scale=softcap))
                            else:
                                @block.scalar
                                def _(s, kt=kt, pb=pb):
                                    k.inc("ms", s.activation(p_sb[pb][:], t_sb[:], AF.Exp,
                                                             scale=softcap))
                        else:
                            @block.scalar
                            def _(s, kt=kt, pb=pb):
                                s.wait_ge(k.sem("mt"), sbar_done)
                                if ("pfree", pb) in hist:
                                    s.wait_ge(k.sem("mt"), hist[("pfree", pb)])
                                k.inc("ms", s.activation(p_sb[pb][:], ps_s[pb][:], AF.Exp,
                                                         scale=scale))
                        hist[("sfree", pb)] = k.cnt["ms"]
                        p_done = k.cnt["ms"]

                        @block.tensor
                        def _(t, qc=qc, kt=kt, pb=pb, ktop=ktop, kvh=kvh):
                            t.wait_ge(k.sem("ms"), p_done)
                            if kt == 0 and ("ofree",) in hist:
                                t.wait_ge(k.sem("mv"), hist[("ofree",)])
                            if kt == 0 and ("dfree",) in hist:
                                t.wait_ge(k.sem("ms"), hist[("dfree",)])
                            if kt == 0 and ("dtfree2",) in hist:
                                t.wait_ge(k.sem("ms"), hist[("dtfree2",)])
                            mm = t.matmul(ps_d[0:1, :], ones1[:, 0:1], p_sb[pb][:],
                                          start=(kt == 0), stop=(kt == ktop - 1),
                                          skip_group_check=True)
                            for sub in range(4):
                                if kt <= qc * 4 + sub:
                                    mm = t.matmul(ps_o[sub][:, 0:128],
                                                  p_sb[pb][:, sub * 128:(sub + 1) * 128],
                                                  v_sb[:, kt * KW + kvh * 128: kt * KW + (kvh + 1) * 128],
                                                  start=(kt == 0), stop=(kt == qc * 4 + sub),
                                                  skip_group_check=True)
                            k.inc("mt", mm)
                        hist[("pfree", pb)] = k.cnt["mt"]
                    pv_done = k.cnt["mt"]

                    @block.scalar
                    def _(s):
                        s.wait_ge(k.sem("mt"), pv_done)
                        k.inc("ms", s.copy(drow[0:1, :], ps_d[0:1, :]))
                    hist[("dfree",)] = k.cnt["ms"]
                    drow_done = k.cnt["ms"]

                    @block.tensor
                    def _(t):
                        t.wait_ge(k.sem("ms"), drow_done)
                        for j in range(4):
                            mm = t.transpose(ps_d[:, 448 + j: 449 + j],
                                             drow[0:1, j * 128:(j + 1) * 128],
                                             ident_f32[0:1, 0:1])
                        k.inc("mt", mm)
                    dt_done = k.cnt["mt"]

                    @block.scalar
                    def _(s):
                        s.wait_ge(k.sem("mt"), dt_done)
                        s.activation(lnd[:, 0:4], ps_d[:, 448:452], AF.Ln)
                        k.inc("ms", s.activation(dhat[:, 0:4], lnd[:, 0:4], AF.Exp,
                                                 scale=-1.0))
                    hist[("dtfree2",)] = k.cnt["ms"]
                    dhat_done = k.cnt["ms"]

                    @block.vector
                    def _(v):
                        v.wait_ge(k.sem("ms"), dhat_done)
                        if ("tpfree",) in hist:
                            v.wait_ge(k.sem("mt"), hist[("tpfree",)])
                        for sub in range(4):
                            inst = v.tensor_scalar(on_sb[:, sub * 128:(sub + 1) * 128],
                                                   ps_o[sub][:, 0:128],
                                                   dhat[:, sub:sub + 1], None, ALU.mult)
                        k.inc("mv", inst)
                    hist[("ofree",)] = k.cnt["mv"]
                    on_done = k.cnt["mv"]

                    @block.tensor
                    def _(t):
                        t.wait_ge(k.sem("mv"), on_done)
                        if ("obcp",) in hist:
                            t.wait_ge(k.sem("ms"), hist[("obcp",)])
                        for sub in range(4):
                            mm = t.transpose(ps_tp[:, sub * 128:(sub + 1) * 128],
                                             on_sb[:, sub * 128:(sub + 1) * 128], ident_bf[:])
                        k.inc("mt", mm)
                    hist[("tpfree",)] = k.cnt["mt"]
                    tp_done = k.cnt["mt"]

                    @block.scalar
                    def _(s, h=h, qc=qc):
                        s.wait_ge(k.sem("mt"), tp_done)
                        k.inc("ms", s.copy(ob_sb[:, h * T + qc * 512: h * T + (qc + 1) * 512],
                                           ps_tp[:, 0:512]))
                    hist[("obcp",)] = k.cnt["ms"]

                hist[("hmm", h)] = k.cnt["mt"]

        k.barrier()
        with contextlib.ExitStack() as pes:
            ko = K.__new__(K)
            ko.nc, ko.block, ko.es = nc, block, pes
            ko.sems, ko.cnt = k.sems, k.cnt
            ko.prefix = k.prefix
            phase_oproj_resid(ko, cfg, b, ob_sb)
        k.barrier()
    oes.close()


def phase_oproj_resid(k, cfg, b, ob_sb):
    nc, block = k.nc, k.block
    woT = cfg["woT_hbm"]
    g_t = cfg["g_hbm"].rearrange("(i p) d -> i p d", p=128)
    x_t = cfg["x_hbm"].rearrange("(i p) d -> i p d", p=128)
    xo_t = cfg["xout_hbm"].rearrange("(i p) d -> i p d", p=128)

    wo_sb = k.sb(f"op{b}_wo", [128, NKD * D], BF16)
    g_sb = [k.sb(f"op{b}_g{j}", [128, D], BF16) for j in range(2)]
    x_sb = [k.sb(f"op{b}_x{j}", [128, D], F32) for j in range(2)]
    xo_sb = [k.sb(f"op{b}_xo{j}", [128, D], F32) for j in range(2)]
    sg_sb = k.sb(f"op{b}_sg", [128, 512], F32)
    pso = [k.ps(f"op{b}_ps{j}", [128, 512]) for j in range(8)]

    @block.gpsimd
    def _(g):
        for kk in range(NKD):
            k.dma(g, wo_sb[:, kk * D:(kk + 1) * D], woT[kk * 128:(kk + 1) * 128, :], "qg")
    wo_ld = k.cnt["qg"]

    hist = {}
    for m in range(16):
        gm = b * 16 + m
        buf = m % 2
        grp = m % 2

        @block.sync
        def _(sy, m=m, gm=gm, buf=buf):
            if m >= 2:
                sy.wait_ge(k.sem("mv"), hist[("resid", m - 2)])
            k.dma(sy, g_sb[buf][:], g_t[gm], "qs")
            k.dma(sy, x_sb[buf][:], x_t[gm], "qs")
        gx_ld = k.cnt["qs"]

        @block.tensor
        def _(t, m=m, grp=grp):
            if m == 0:
                t.wait_ge(k.sem("qg"), wo_ld)
            if m >= 2:
                t.wait_ge(k.sem("mv"), hist[("evac", m - 2)])
            for kk in range(NKD):
                for n in range(4):
                    mm = t.matmul(pso[grp * 4 + n][:],
                                  ob_sb[:, kk * T + m * 128: kk * T + (m + 1) * 128],
                                  wo_sb[:, kk * D + n * 512: kk * D + (n + 1) * 512],
                                  start=(kk == 0), stop=(kk == NKD - 1))
            k.inc("mt", mm)
        mm_done = k.cnt["mt"]

        @block.vector
        def _(v, m=m, gm=gm, buf=buf, grp=grp):
            v.wait_ge(k.sem("mt"), mm_done)
            v.wait_ge(k.sem("qs"), gx_ld)
            if m >= 2:
                v.wait_ge(k.sem("qs"), hist[("xout", m - 2)])
            for n in range(4):
                v.tensor_tensor(sg_sb[:], pso[grp * 4 + n][:],
                                g_sb[buf][:, n * 512:(n + 1) * 512], ALU.mult)
                inst = v.tensor_tensor(xo_sb[buf][:, n * 512:(n + 1) * 512], sg_sb[:],
                                       x_sb[buf][:, n * 512:(n + 1) * 512], ALU.add)
            k.inc("mv", inst)
        hist[("evac", m)] = k.cnt["mv"]
        hist[("resid", m)] = k.cnt["mv"]
        ev_done = k.cnt["mv"]

        @block.sync
        def _(sy, m=m, gm=gm, buf=buf):
            sy.wait_ge(k.sem("mv"), ev_done)
            k.dma(sy, xo_t[gm], xo_sb[buf][:], "qs")
        hist[("xout", m)] = k.cnt["qs"]


def scoped(k):
    class _S:
        def __enter__(self):
            self.pes = contextlib.ExitStack()
            self.pes.__enter__()
            kk = K.__new__(K)
            kk.nc, kk.block, kk.es = k.nc, k.block, self.pes
            kk.sems, kk.cnt = k.sems, k.cnt
            kk.prefix = k.prefix
            return kk

        def __exit__(self, *a):
            return self.pes.__exit__(*a)
    return _S()


def make_sign_wprep(k, tag):
    tA = k.sb(tag + "_sgA", [128, 2048], F32)
    tB = k.sb(tag + "_sgB", [128, 2048], F32)

    def wprep(k, p, pbuf, wsb_buf, pw, wld):
        nc, block = k.nc, k.block

        @block.vector
        def _(v):
            v.wait_ge(k.sem("qg"), wld)
            for kk in range(NKD):
                sl = wsb_buf[:, kk * 2048: kk * 2048 + pw]
                v.tensor_scalar(tA[:, 0:pw], sl, 0.0, None, ALU.is_gt)
                v.tensor_scalar(tB[:, 0:pw], sl, 0.0, None, ALU.is_lt)
                inst = v.tensor_tensor(sl, tA[:, 0:pw], tB[:, 0:pw], ALU.subtract)
            k.inc("mv", inst)
        return ("mv", k.cnt["mv"])

    return wprep


def phase_absmax(k, w_hbm, W, wmax_col, tag, ident_f32=None, ones_rowf=None):
    nc, block = k.nc, k.block
    wt = [k.sb(tag + f"_aw{j}", [128, 2048], F32) for j in range(2)]
    nstrip = w_hbm.shape[0] // 128
    ncol = (W + 2047) // 2048
    ntile = nstrip * ncol
    cols = k.sb(tag + "_cols", [128, ntile], F32)
    cols2 = k.sb(tag + "_cols2", [128, ntile], F32)
    hist = {}
    for kk in range(nstrip):
        for c in range(ncol):
            i = kk * ncol + c
            cw = min(2048, W - c * 2048)
            buf = i % 2

            @block.sync
            def _(sy, kk=kk, c=c, buf=buf, cw=cw, i=i):
                if i >= 2:
                    sy.wait_ge(k.sem("mv"), hist[i - 2])
                k.dma(sy, wt[buf][:, 0:cw],
                      w_hbm[kk * 128:(kk + 1) * 128, c * 2048:c * 2048 + cw], "qs")
            ld = k.cnt["qs"]

            @block.vector
            def _(v, buf=buf, cw=cw, i=i):
                v.wait_ge(k.sem("qs"), ld)
                k.inc("mv", v.tensor_reduce(cols[:, i:i + 1], wt[buf][:, 0:cw], AXL.X,
                                            ALU.max, apply_absolute_value=True))
            hist[i] = k.cnt["mv"]
    red_done = k.cnt["mv"]

    @block.scalar
    def _(s):
        s.wait_ge(k.sem("mv"), red_done)
        k.inc("ms", s.copy(cols2[:], cols[:]))
    cp_done = k.cnt["ms"]

    @block.vector
    def _(v):
        v.wait_ge(k.sem("ms"), cp_done)
        k.inc("mv", v.tensor_reduce(wmax_col[:, 0:1], cols2[:], AXL.X, ALU.max))
    red2 = k.cnt["mv"]

    ps_x = k.ps(tag + "_psx", [128, 512])
    rowb = k.sb(tag + "_rowb", [128, 128], F32)
    rowb2 = k.sb(tag + "_rowb2", [128, 128], F32)
    val = k.sb(tag + "_val", [128, 1], F32)
    val2 = k.sb(tag + "_val2", [128, 1], F32)

    @block.tensor
    def _(t):
        t.wait_ge(k.sem("mv"), red2)
        k.inc("mt", t.transpose(ps_x[0:1, 0:128], wmax_col[:, 0:1], ident_f32[:]))
    c1 = k.cnt["mt"]

    @block.scalar
    def _(s):
        s.wait_ge(k.sem("mt"), c1)
        k.inc("ms", s.copy(rowb[0:1, 0:128], ps_x[0:1, 0:128]))
    c2 = k.cnt["ms"]

    @block.vector
    def _(v):
        v.wait_ge(k.sem("ms"), c2)
        k.inc("mv", v.tensor_reduce(val[0:1, 0:1], rowb[0:1, 0:128], AXL.X, ALU.max))
    c3 = k.cnt["mv"]

    @block.scalar
    def _(s):
        s.wait_ge(k.sem("mv"), c3)
        k.inc("ms", s.copy(val2[0:1, 0:1], val[0:1, 0:1]))
    c3b = k.cnt["ms"]

    @block.tensor
    def _(t):
        t.wait_ge(k.sem("ms"), c3b)
        k.inc("mt", t.matmul(ps_x[:, 256:257], ones_rowf[0:1, :], val2[0:1, 0:1],
                             start=True, stop=True))
    c4 = k.cnt["mt"]

    @block.scalar
    def _(s):
        s.wait_ge(k.sem("mt"), c4)
        k.inc("ms", s.copy(wmax_col[:, 0:1], ps_x[:, 256:257]))
    return k.cnt["ms"]


def make_ffn_evac(k, gamma_rep, beta_rep, wmax_col, g_hbm, tag):
    nc, block = k.nc, k.block
    tf = k.sb(tag + "_tf", [128, 512], F32)
    t2 = [k.sb(tag + f"_t2{j}", [128, 2048], F32) for j in range(2)]
    gst = [k.sb(tag + f"_g{j}", [128, 2048], BF16) for j in range(2)]
    g_t = g_hbm.rearrange("(i p) d -> i p d", p=128)
    st_hist = {}

    def evac(k, gi, ncg, ps, mm_cnt, group_pss):
        nc4 = ncg % 4
        buf = gi % 2

        @block.vector
        def _(v):
            v.wait_ge(k.sem("mt"), mm_cnt)
            if ("silu", buf, nc4) in st_hist:
                v.wait_ge(k.sem("ms"), st_hist[("silu", buf, nc4)])
            v.tensor_tensor(tf[:], ps[:], gamma_rep[:, nc4 * 512:(nc4 + 1) * 512], ALU.mult)
            k.inc("mv", v.scalar_tensor_tensor(t2[buf][:, nc4 * 512:(nc4 + 1) * 512], tf[:],
                                               wmax_col[:, 0:1],
                                               beta_rep[:, nc4 * 512:(nc4 + 1) * 512],
                                               ALU.mult, ALU.add))
        c1 = k.cnt["mv"]

        @block.scalar
        def _(s):
            s.wait_ge(k.sem("mv"), c1)
            if nc4 == 0 and gi >= 2:
                s.wait_ge(k.sem("qs"), st_hist[gi - 2])
            k.inc("ms", s.activation(gst[buf][:, nc4 * 512:(nc4 + 1) * 512],
                                     t2[buf][:, nc4 * 512:(nc4 + 1) * 512], AF.Silu))
        c2 = k.cnt["ms"]
        st_hist[("silu", buf, nc4)] = c2

        if nc4 == 3:
            @block.sync
            def _(sy):
                sy.wait_ge(k.sem("ms"), c2)
                k.dma(sy, g_t[gi], gst[buf][:], "qs")
            st_hist[gi] = k.cnt["qs"]

    return evac


def build_l0(nc, phases=(1,2,3,4,5), dbg=False, k=None, io=None):
    ext = {}

    def ein(name, shape, dt=F32):
        if io is not None:
            ext[name] = io[name]
            return ext[name]
        ext[name] = nc.dram_tensor(name, shape, dt, kind="ExternalInput").ap()
        return ext[name]

    def edecl(name, shape, dt=F32, kind="Internal"):
        if io is not None:
            return io[name]
        return nc.dram_tensor(name, shape, dt, kind=kind).ap()

    x = ein("x", [TOK, D])
    wqT = ein("wqT", [D, 2048])
    wkT = ein("wkT", [D, 2048])
    wvT = ein("wvT", [D, 2048])
    woT = ein("woT", [2048, D])
    fwT = ein("fwT", [D, 2048])
    qnw_rep = ein("qnw_rep", [128, 2048])
    knw_rep = ein("knw_rep", [128, 2048])
    cos_rep = ein("cos_rep", [T, 1024])
    sin_rep = ein("sin_rep", [T, 1024])
    lnw_rep = ein("lnw_rep", [128, D])
    lnb_rep = ein("lnb_rep", [128, D])
    gamma_rep = ein("gamma_rep", [128, D])
    beta_rep = ein("beta_rep", [128, D])
    ident_bf_h = ein("ident_bf", [128, 128], BF16)
    ident_f32_h = ein("ident_f32", [128, 128])
    ones1_h = ein("ones1", [128, 1], BF16)
    ones_row1_h = ein("ones_row1", [1, 128], BF16)
    ones_rowf_h = ein("ones_rowf", [1, 128])
    masks_h = ein("masks_f32", [4 * 128, 512])
    xout = edecl("xout", [TOK, D], F32, "ExternalOutput")

    ikind = "ExternalOutput" if dbg else "Internal"
    xn_bf = edecl("xn_bf", [TOK, D], BF16, ikind)
    xn2_bf = edecl("xn2_bf", [TOK, D], BF16, ikind)
    q_hbm = edecl("q_sc", [TOK, 2048], BF16, ikind)
    k_hbm = edecl("k_sc", [TOK, 2048], BF16, ikind)
    v_hbm = edecl("v_sc", [TOK, 2048], BF16, ikind)
    g_hbm = edecl("g_sc", [TOK, D], BF16, ikind)

    es = contextlib.ExitStack()
    with es:
        if k is None:
            block = es.enter_context(nc.Block())
            k = K(nc, block, es)
            for s_ in ["qs", "qg", "mt", "mv", "ms", "mg"]:
                k.sem(s_)
        else:
            block = k.block
            k = K.sub(k, es)

        ident_bf = k.sb("c_identbf", [128, 128], BF16)
        ident_f32 = k.sb("c_identf32", [128, 128], F32)
        ones1 = k.sb("c_ones1", [128, 1], BF16)
        ones_row1 = k.sb("c_onesrow", [1, 128], BF16)
        ones_rowf = k.sb("c_onesrowf", [1, 128], F32)
        wmax_col = k.sb("c_wmax", [128, 1], F32)

        @block.sync
        def _(sy):
            k.dma(sy, ident_bf[:], ident_bf_h, "qs")
            k.dma(sy, ident_f32[:], ident_f32_h, "qs")
            k.dma(sy, ones1[:], ones1_h, "qs")
            k.dma(sy, ones_row1[:], ones_row1_h, "qs")
            k.dma(sy, ones_rowf[:], ones_rowf_h, "qs")
        k.barrier()

        if 1 in phases:
            with scoped(k) as kp:
                phase_ln(kp, x, out_bf_hbm=xn_bf, eps=1e-5, tag="ln1")
            k.barrier()

        for name, wT, out_h, wrep_h, rope in [] if 2 not in phases else [
                ("q", wqT, q_hbm, qnw_rep, True),
                ("k", wkT, k_hbm, knw_rep, True),
                ("v", wvT, v_hbm, None, False)]:
            with scoped(k) as kp:
                if name == "v":
                    evac = make_v_evac(kp, out_h, "pv")
                    phase_proj(kp, xn_bf, wT, 2048, evac, "pv", evac_sem="ms")
                else:
                    wrep_sb = kp.sb("c_wrep" + name, [128, 2048], F32)

                    @block.sync
                    def _(sy):
                        kp.dma(sy, wrep_sb[:], wrep_h, "qs")
                    evac = make_qk_evac(kp, wrep_sb, cos_rep, sin_rep, out_h,
                                        1e-6, "p" + name, rope=rope)
                    phase_proj(kp, xn_bf, wT, 2048, evac, "p" + name, evac_sem="mv")
            k.barrier()

        if 3 in phases:
          with scoped(k) as kp:
            lnwr = kp.sb("c_lnw", [128, D], F32)
            lnbr = kp.sb("c_lnb", [128, D], F32)

            @block.sync
            def _(sy):
                kp.dma(sy, lnwr[:], lnw_rep, "qs")
                kp.dma(sy, lnbr[:], lnb_rep, "qs")
            import os as _os
            if _os.environ.get("LN2_NOLNW"):
                phase_ln(kp, xn_bf, out_bf_hbm=xn2_bf, eps=1e-5, tag="ln2",
                         in_dtype=BF16)
            else:
                phase_ln(kp, xn_bf, out_bf_hbm=xn2_bf, eps=1e-5, tag="ln2",
                         in_dtype=BF16, lnw_rep=lnwr, lnb_rep=lnbr)
          k.barrier()

        if 4 in phases or 6 in phases:
          with scoped(k) as kp:
            phase_absmax(kp, fwT, 2048, wmax_col, "amx", ident_f32=ident_f32,
                         ones_rowf=ones_rowf)
          k.barrier()

        if 4 in phases or 7 in phases:
          with scoped(k) as kp:
            gammar = kp.sb("c_gamma", [128, D], F32)
            betar = kp.sb("c_beta", [128, D], F32)

            @block.sync
            def _(sy):
                kp.dma(sy, gammar[:], gamma_rep, "qs")
                kp.dma(sy, betar[:], beta_rep, "qs")
            wprep = make_sign_wprep(kp, "ffs")
            evac = make_ffn_evac(kp, gammar, betar, wmax_col, g_hbm, "ffe")
            phase_proj(kp, xn2_bf, fwT, 2048, evac, "ff", evac_sem="mv", wprep_cb=wprep)
          k.barrier()

        cfg = dict(q_hbm=q_hbm, k_hbm=k_hbm, v_hbm=v_hbm, woT_hbm=woT,
                   g_hbm=g_hbm, x_hbm=x, xout_hbm=xout,
                   ident_bf=ident_bf, ident_f32=ident_f32, ones1=ones1,
                   ones_row1=ones_row1, masks_hbm=masks_h, mask_dtype=F32,
                   softcap=50.0, scale=1.0 / float(np.sqrt(128.0)), kv_rep=1)
        if 5 in phases:
            phase_attn_oproj(k, cfg)
        k.barrier()
    return nc


def prep_l0_inputs(xc, w):
    P = perm_deint(128)
    out = {"x": np.ascontiguousarray(xc, dtype=np.float32)}

    def permrows(m):
        return np.ascontiguousarray(m.reshape(16, 128, -1)[:, P, :].reshape(2048, -1))

    out["wqT"] = np.ascontiguousarray(permrows(w["L0_attn_wq"]).T)
    out["wkT"] = np.ascontiguousarray(permrows(w["L0_attn_wk"]).T)
    out["wvT"] = np.ascontiguousarray(w["L0_attn_wv"].T)
    out["woT"] = np.ascontiguousarray(w["L0_attn_wo"].T)
    out["fwT"] = np.ascontiguousarray(w["L0_ffn_weight"].T)
    qn = w["L0_attn_q_norm_w"][P]
    kn = w["L0_attn_k_norm_w"][P]
    out["qnw_rep"] = np.ascontiguousarray(np.tile(qn, (128, 16)))
    out["knw_rep"] = np.ascontiguousarray(np.tile(kn, (128, 16)))
    out["cos_rep"] = np.ascontiguousarray(
        np.tile(np.asarray(w["L0_attn_rope_cos"])[:16, :].reshape(1, 1024), (T, 1)))
    out["sin_rep"] = np.ascontiguousarray(
        np.tile(np.asarray(w["L0_attn_rope_sin"])[:16, :].reshape(1, 1024), (T, 1)))
    out["lnw_rep"] = np.ascontiguousarray(np.tile(w["L0_ffn_ln_weight"], (128, 1)))
    out["lnb_rep"] = np.ascontiguousarray(np.tile(w["L0_ffn_ln_bias"], (128, 1)))
    out["gamma_rep"] = np.ascontiguousarray(np.tile(w["L0_ffn_gamma"], (128, 1)))
    out["beta_rep"] = np.ascontiguousarray(np.tile(w["L0_ffn_beta"], (128, 1)))
    out["ident_bf"] = np.eye(128, dtype=bf)
    out["ident_f32"] = np.eye(128, dtype=np.float32)
    out["ones1"] = np.ones((128, 1), dtype=bf)
    out["ones_row1"] = np.ones((1, 128), dtype=bf)
    out["ones_rowf"] = np.ones((1, 128), dtype=np.float32)
    mk = np.zeros((4 * 128, 512), dtype=np.float32)
    for dlt in range(4):
        p = np.arange(128)[:, None]
        j = np.arange(512)[None, :]
        mk[dlt * 128:(dlt + 1) * 128] = np.where(j >= p + dlt * 128, 0.0, -1e30)
    out["masks_f32"] = mk
    return out


def make_gate_evac(k, n128rs, sgate_rep, gate_hbm, tag):
    nc, block = k.nc, k.block
    tf = k.sb(tag + "_tf", [128, 512], F32)
    t2 = [k.sb(tag + f"_t2{j}", [128, 512], F32) for j in range(2)]
    gb = [k.sb(tag + f"_gb{j}", [128, 512], BF16) for j in range(2)]
    st_hist = {}

    def evac(k, gi, ncg, ps, mm_cnt, group_pss):
        ib = (gi * 11 + ncg) % 2

        @block.vector
        def _(v):
            v.wait_ge(k.sem("mt"), mm_cnt)
            if ("silu", ib) in st_hist:
                v.wait_ge(k.sem("ms"), st_hist[("silu", ib)])
            v.tensor_scalar(tf[:], ps[:], n128rs[:, gi:gi + 1], None, ALU.add)
            k.inc("mv", v.tensor_tensor(t2[ib][:], tf[:],
                                        sgate_rep[:, ncg * 512:(ncg + 1) * 512], ALU.mult))
        c1 = k.cnt["mv"]

        @block.scalar
        def _(s):
            s.wait_ge(k.sem("mv"), c1)
            if ("st", ib) in st_hist:
                s.wait_ge(k.sem("qs"), st_hist[("st", ib)])
            k.inc("ms", s.activation(gb[ib][:], t2[ib][:], AF.Silu))
        c2 = k.cnt["ms"]
        st_hist[("silu", ib)] = c2

        @block.sync
        def _(sy):
            sy.wait_ge(k.sem("ms"), c2)
            k.dma(sy, gate_hbm[gi * 128:(gi + 1) * 128, ncg * 512:(ncg + 1) * 512],
                  gb[ib][:], "qs")
        st_hist[("st", ib)] = k.cnt["qs"]

    return evac


def make_up_evac(k, n128rs, sud_rep, gate_hbm, guT_hbm, gusum_sb, tag):
    nc, block = k.nc, k.block
    tf = k.sb(tag + "_tf", [128, 512], F32)
    t2 = k.sb(tag + "_t2", [128, 512], F32)
    gsl = [k.sb(tag + f"_gsl{j}", [128, 512], BF16) for j in range(2)]
    gu = [k.sb(tag + f"_gu{j}", [128, 512], BF16) for j in range(2)]
    st_hist = {}

    def evac(k, gi, ncg, ps, mm_cnt, group_pss):
        ib = (gi * 11 + ncg) % 2

        @block.sync
        def _(sy):
            if ("st", ib) in st_hist:
                sy.wait_ge(k.sem("mv"), st_hist[("gsl", ib)])
            k.dma(sy, gsl[ib][:],
                  gate_hbm[gi * 128:(gi + 1) * 128, ncg * 512:(ncg + 1) * 512], "qs")
        g_ld = k.cnt["qs"]

        @block.vector
        def _(v):
            v.wait_ge(k.sem("mt"), mm_cnt)
            v.wait_ge(k.sem("qs"), g_ld)
            if ("st", ib) in st_hist:
                v.wait_ge(k.sem("qs"), st_hist[("st", ib)])
            v.tensor_scalar(tf[:], ps[:], n128rs[:, gi:gi + 1], None, ALU.add)
            v.tensor_tensor(t2[:], tf[:], sud_rep[:, ncg * 512:(ncg + 1) * 512], ALU.mult)
            v.tensor_tensor(gu[ib][:], t2[:], gsl[ib][:], ALU.mult)
            k.inc("mv", v.tensor_scalar(tf[:], gu[ib][:], 1.0, None, ALU.mult, ALU.add,
                                        accum_out=gusum_sb[:, gi * 11 + ncg: gi * 11 + ncg + 1]))
        c1 = k.cnt["mv"]
        st_hist[("gsl", ib)] = k.cnt["mv"]

        @block.sync
        def _(sy):
            sy.wait_ge(k.sem("mv"), c1)
            k.dma(sy, guT_hbm[gi * 128:(gi + 1) * 128, ncg * 512:(ncg + 1) * 512],
                  gu[ib][:], "qs")
        st_hist[("st", ib)] = k.cnt["qs"]

    return evac


def make_down_evac(k, n128gs, sdown_rep, g_hbm, tag):
    nc, block = k.nc, k.block
    tf = [k.sb(tag + f"_tf{j}", [128, 512], F32) for j in range(2)]
    gb = [k.sb(tag + f"_gb{j}", [128, 512], BF16) for j in range(2)]
    st_hist = {}

    def evac(k, gi, ncg, ps, mm_cnt, group_pss):
        ib = (gi + ncg) % 2

        @block.vector
        def _(v):
            v.wait_ge(k.sem("mt"), mm_cnt)
            if ("silu", ib) in st_hist:
                v.wait_ge(k.sem("ms"), st_hist[("silu", ib)])
            v.tensor_scalar(tf[ib][:], ps[:], n128gs[:, gi:gi + 1], None, ALU.add)
            k.inc("mv", v.tensor_tensor(tf[ib][:], tf[ib][:],
                                        sdown_rep[:, ncg * 512:(ncg + 1) * 512], ALU.mult))
        c1 = k.cnt["mv"]

        @block.scalar
        def _(s):
            s.wait_ge(k.sem("mv"), c1)
            if ("st", ib) in st_hist:
                s.wait_ge(k.sem("qs"), st_hist[("st", ib)])
            k.inc("ms", s.activation(gb[ib][:], tf[ib][:], AF.Silu))
        c2 = k.cnt["ms"]
        st_hist[("silu", ib)] = c2

        @block.sync
        def _(sy):
            sy.wait_ge(k.sem("ms"), c2)
            k.dma(sy, g_hbm[gi * 128:(gi + 1) * 128, ncg * 512:(ncg + 1) * 512],
                  gb[ib][:], "qs")
        st_hist[("st", ib)] = k.cnt["qs"]

    return evac


def build_l1(nc, dbg=False, k=None, io=None):
    ext = {}

    def ein(name, shape, dt=F32):
        if io is not None:
            ext[name] = io[name]
            return ext[name]
        ext[name] = nc.dram_tensor(name, shape, dt, kind="ExternalInput").ap()
        return ext[name]

    def edecl(name, shape, dt=F32, kind="Internal"):
        if io is not None:
            return io[name]
        return nc.dram_tensor(name, shape, dt, kind=kind).ap()

    x = ein("x", [TOK, D])
    wqT = ein("wqT", [D, 2048])
    wkT = ein("wkT", [D, 2048])
    wvT = ein("wvT", [D, 2048])
    woT = ein("woT", [2048, D])
    wgT_i = ein("wgT", [D, 5632], I32)
    wuT_i = ein("wuT", [D, 5632], I32)
    wdT_i = ein("wdT", [5632, D], I32)
    sgate_rep = ein("sgate_rep", [128, 5632])
    sup_rep = ein("sup_rep", [128, 5632])
    sdown_rep = ein("sdown_rep", [128, 2048])
    qnw_rep = ein("qnw_rep", [128, 2048])
    knw_rep = ein("knw_rep", [128, 2048])
    ident_bf_h = ein("ident_bf", [128, 128], BF16)
    ident_f32_h = ein("ident_f32", [128, 128])
    ones1_h = ein("ones1", [128, 1], BF16)
    ones_row1_h = ein("ones_row1", [1, 128], BF16)
    masks_h = ein("masks_bf", [4 * 128, 512], BF16)
    masks2_h = ein("masks2_bf", [4 * 128, 512], BF16)
    xout = edecl("xout", [TOK, D], F32, "ExternalOutput")

    ikind = "ExternalOutput" if dbg else "Internal"
    xn_bf = edecl("xn_bf", [TOK, D], BF16, ikind)
    q_hbm = edecl("q_sc", [TOK, 2048], BF16, ikind)
    k_hbm = edecl("k_sc", [TOK, 2048], BF16, ikind)
    v_hbm = edecl("v_sc", [TOK, 2048], BF16, ikind)
    g_hbm = edecl("g_sc", [TOK, D], BF16, ikind)
    gate_hbm = edecl("gate_sc", [TOK, 5632], BF16, ikind)
    guT_hbm = edecl("guT_sc", [TOK, 5632], BF16, ikind)

    es = contextlib.ExitStack()
    with es:
        if k is None:
            block = es.enter_context(nc.Block())
            k = K(nc, block, es)
            for s_ in ["qs", "qg", "mt", "mv", "ms", "mg"]:
                k.sem(s_)
        else:
            block = k.block
            k = K.sub(k, es)

        ident_bf = k.sb("c_identbf", [128, 128], BF16)
        ident_f32 = k.sb("c_identf32", [128, 128], F32)
        ones1 = k.sb("c_ones1", [128, 1], BF16)
        ones_row1 = k.sb("c_onesrow", [1, 128], BF16)
        rowsum_sb = k.sb("c_rowsum", [128, NTT], F32)
        rowsum2 = k.sb("c_rowsum2", [128, NTT], F32)
        n128rs = k.sb("c_n128rs", [128, NTT], F32)
        gusum_sb = k.sb("c_gusum", [128, NTT * 11], F32)
        gusum2 = k.sb("c_gusum2", [128, NTT * 11], F32)
        n128gs = k.sb("c_n128gs", [128, NTT], F32)

        @block.sync
        def _(sy):
            k.dma(sy, ident_bf[:], ident_bf_h, "qs")
            k.dma(sy, ident_f32[:], ident_f32_h, "qs")
            k.dma(sy, ones1[:], ones1_h, "qs")
            k.dma(sy, ones_row1[:], ones_row1_h, "qs")
        k.barrier()

        with scoped(k) as kp:
            phase_ln(kp, x, out_bf_hbm=xn_bf, eps=1e-5, tag="ln1",
                     rowsum_sb=rowsum_sb)
        k.barrier()

        @block.scalar
        def _(s):
            k.inc("ms", s.copy(rowsum2[:], rowsum_sb[:]))
        c_rs = k.cnt["ms"]

        @block.vector
        def _(v):
            v.wait_ge(k.sem("ms"), c_rs)
            k.inc("mv", v.tensor_scalar(n128rs[:], rowsum2[:], -128.0, None, ALU.mult))
        k.barrier()

        for name, wT, out_h, wrep_h in [("q", wqT, q_hbm, qnw_rep),
                                        ("k", wkT, k_hbm, knw_rep),
                                        ("v", wvT, v_hbm, None)]:
            with scoped(k) as kp:
                if name == "v":
                    evac = make_v_evac(kp, out_h, "pv")
                    phase_proj(kp, xn_bf, wT, 2048, evac, "pv", evac_sem="ms")
                else:
                    wrep_sb = kp.sb("c_wrep" + name, [128, 2048], F32)

                    @block.sync
                    def _(sy):
                        kp.dma(sy, wrep_sb[:], wrep_h, "qs")
                    evac = make_qk_evac(kp, wrep_sb, None, None, out_h,
                                        1e-5, "p" + name, rope=False)
                    phase_proj(kp, xn_bf, wT, 2048, evac, "p" + name, evac_sem="mv")
            k.barrier()

        with scoped(k) as kp:
            sg_sb = kp.sb("c_sg", [128, 5632], F32)

            @block.sync
            def _(sy):
                kp.dma(sy, sg_sb[:], sgate_rep, "qs")
            evac = make_gate_evac(kp, n128rs, sg_sb, gate_hbm, "pg")
            phase_proj(kp, xn_bf, wgT_i, 5632, evac, "pg", evac_sem="mv",
                       PASSW=1024, nbuf=2)
        k.barrier()

        with scoped(k) as kp:
            su_sb = kp.sb("c_su", [128, 5632], F32)

            @block.sync
            def _(sy):
                kp.dma(sy, su_sb[:], sup_rep, "qs")
            k.barrier()
            evac = make_up_evac(kp, n128rs, su_sb, gate_hbm, guT_hbm, gusum_sb, "pu")
            phase_proj(kp, xn_bf, wuT_i, 5632, evac, "pu", evac_sem="mv",
                       PASSW=1024, nbuf=2)
        k.barrier()

        @block.scalar
        def _(s):
            k.inc("ms", s.copy(gusum2[:], gusum_sb[:]))
        c_gs = k.cnt["ms"]

        @block.vector
        def _(v):
            v.wait_ge(k.sem("ms"), c_gs)
            for i in range(NTT):
                inst = v.tensor_reduce(gusum_sb[:, i:i + 1], gusum2[:, i * 11:(i + 1) * 11],
                                       AXL.X, ALU.add)
            k.inc("mv", inst)
        k.barrier()

        @block.scalar
        def _(s):
            k.inc("ms", s.copy(gusum2[:, 0:NTT], gusum_sb[:, 0:NTT]))
        c_gs2 = k.cnt["ms"]

        @block.vector
        def _(v):
            v.wait_ge(k.sem("ms"), c_gs2)
            k.inc("mv", v.tensor_scalar(n128gs[:], gusum2[:, 0:NTT], -128.0, None, ALU.mult))
        k.barrier()

        with scoped(k) as kp:
            sdn_sb = kp.sb("c_sdn", [128, 2048], F32)

            @block.sync
            def _(sy):
                kp.dma(sy, sdn_sb[:], sdown_rep, "qs")
            k.barrier()
            evac = make_down_evac(kp, n128gs, sdn_sb, g_hbm, "pd")
            phase_proj(kp, guT_hbm, wdT_i, 2048, evac, "pd", evac_sem="mv",
                       KSTR=44, HT=1024, PASSW=512, nbuf=2)
        k.barrier()

        cfg = dict(q_hbm=q_hbm, k_hbm=k_hbm, v_hbm=v_hbm, woT_hbm=woT,
                   g_hbm=g_hbm, x_hbm=x, xout_hbm=xout,
                   ident_bf=ident_bf, ident_f32=ident_f32, ones1=ones1,
                   ones_row1=ones_row1, masks_hbm=masks_h, mask_dtype=BF16,
                   masks2_hbm=masks2_h,
                   softcap=None, scale=1.0 / float(np.sqrt(128.0)), kv_rep=1)
        phase_attn_oproj(k, cfg)
        k.barrier()
    return nc


def prep_l1_inputs(xc, w):
    out = {"x": np.ascontiguousarray(xc, dtype=np.float32)}
    out["wqT"] = np.ascontiguousarray(w["L1_attn_Wq"].T)
    out["wkT"] = np.ascontiguousarray(w["L1_attn_Wk"].T)
    out["wvT"] = np.ascontiguousarray(w["L1_attn_Wv"].T)
    out["woT"] = np.ascontiguousarray(w["L1_attn_Wo"].T)
    out["wgT"] = np.ascontiguousarray(w["L1_ffn_wq_gate"].T)
    out["wuT"] = np.ascontiguousarray(w["L1_ffn_wq_up"].T)
    out["wdT"] = np.ascontiguousarray(w["L1_ffn_wq_down"].T)
    out["sgate_rep"] = np.ascontiguousarray(np.tile(w["L1_ffn_s_gate"], (128, 1)))
    out["sup_rep"] = np.ascontiguousarray(np.tile(w["L1_ffn_s_up"], (128, 1)))
    out["sdown_rep"] = np.ascontiguousarray(np.tile(w["L1_ffn_s_down"], (128, 1)))
    out["qnw_rep"] = np.ascontiguousarray(np.tile(w["L1_attn_gq"], (128, 16)))
    out["knw_rep"] = np.ascontiguousarray(np.tile(w["L1_attn_gk"], (128, 16)))
    out["ident_bf"] = np.eye(128, dtype=bf)
    out["ident_f32"] = np.eye(128, dtype=np.float32)
    out["ones1"] = np.ones((128, 1), dtype=bf)
    out["ones_row1"] = np.ones((1, 128), dtype=bf)
    mk = np.zeros((4 * 128, 512), dtype=np.float32)
    for dlt in range(4):
        p = np.arange(128)[:, None]
        j = np.arange(512)[None, :]
        mk[dlt * 128:(dlt + 1) * 128] = np.where(j >= p + dlt * 128, 0.0, -1e30)
    out["masks_bf"] = mk.astype(bf)
    mk2 = np.zeros((4 * 128, 512), dtype=np.float32)
    for qs_ in range(4):
        p = np.arange(128)[:, None]
        j = np.arange(512)[None, :]
        mk2[qs_ * 128:(qs_ + 1) * 128] = np.where(j <= p + qs_ * 128, 0.0, -1e30)
    out["masks2_bf"] = mk2.astype(bf)
    return out


def phase_l3ffn(k, h_hbm, bwT_h, g_hbm, ident_bf, hlg_hbm=None, tag="l3f", dbg_hbm=None):
    nc, block = k.nc, k.block
    bw_sb = k.sb(tag + "_bw", [128, 4 * D], BF16)
    hs = [k.sb(tag + f"_h{j}", [128, 1024], BF16) for j in range(2)]
    lg_sb = k.sb(tag + "_lg", [128, 256], F32)
    mtree = k.sb(tag + "_mtree", [128, 256], F32)
    lmaxall = k.sb(tag + "_lmax", [128, 32], F32)
    lmaxall2 = k.sb(tag + "_lmax2", [128, 32], F32)
    suball = k.sb(tag + "_sub", [128, 256], F32)
    expall = k.sb(tag + "_exp", [128, 256], F32)
    sumtree = k.sb(tag + "_sumtree", [128, 256], F32)
    sumeall = k.sb(tag + "_sume", [128, 32], F32)
    pselall = k.sb(tag + "_psel", [128, 32], F32)
    sc4all = k.sb(tag + "_sc4", [128, 32], F32)
    maskall = k.sb(tag + "_maska", [128, 256], F32)
    maskall2 = k.sb(tag + "_maskb", [128, 256], F32)
    hm = k.sb(tag + "_hm", [128, 512], BF16)
    hmT = k.sb(tag + "_hmT", [128, 512], BF16)
    gst = [k.sb(tag + f"_g{j}", [128, D], BF16) for j in range(2)]
    zb = [k.sb(tag + f"_zb{j}", [128, D], F32) for j in range(2)]
    ps_tp = k.ps(tag + "_pstp", [128, 1024], BF16)
    pso = [k.ps(tag + f"_pso{j}", [128, 512]) for j in range(4)]
    h_t = h_hbm.rearrange("(i p) d -> i p d", p=128)
    g_t = g_hbm.rearrange("(i p) d -> i p d", p=128)

    @block.gpsimd
    def _(g):
        for kk in range(4):
            k.dma(g, bw_sb[:, kk * D:(kk + 1) * D],
                  bwT_h[kk * 128:(kk + 1) * 128, :], "qg")
    bw_ld = k.cnt["qg"]

    @block.sync
    def _(sy):
        for i in range(NTT):
            k.dma(sy, lg_sb[:, i * 8:(i + 1) * 8], hlg_hbm[i * 128:(i + 1) * 128, :], "qs")
    lg_ld = k.cnt["qs"]

    lgv = lg_sb.ap().rearrange("p (i e) -> p i e", e=8)
    mtv = mtree.ap().rearrange("p (i e) -> p i e", e=8)
    stv = sumtree.ap().rearrange("p (i e) -> p i e", e=8)
    exv = expall.ap().rearrange("p (i e) -> p i e", e=8)

    @block.vector
    def _(v):
        v.wait_ge(k.sem("qs"), lg_ld)
        v.tensor_tensor(mtv[:, :, 0:4], lgv[:, :, 0:4], lgv[:, :, 4:8], ALU.max)
        v.tensor_tensor(mtv[:, :, 4:6], mtv[:, :, 0:2], mtv[:, :, 2:4], ALU.max)
        k.inc("mv", v.tensor_tensor(lmaxall[:], mtv[:, :, 4:5], mtv[:, :, 5:6], ALU.max))
    c1 = k.cnt["mv"]

    @block.scalar
    def _(s):
        s.wait_ge(k.sem("mv"), c1)
        k.inc("ms", s.copy(lmaxall2[:], lmaxall[:]))
    c2 = k.cnt["ms"]

    @block.vector
    def _(v):
        v.wait_ge(k.sem("ms"), c2)
        for i in range(NTT):
            v.tensor_scalar(suball[:, i * 8:(i + 1) * 8], lg_sb[:, i * 8:(i + 1) * 8],
                            lmaxall2[:, i:i + 1], None, ALU.subtract)
            inst = v.tensor_scalar(maskall[:, i * 8:(i + 1) * 8], lg_sb[:, i * 8:(i + 1) * 8],
                                   lmaxall2[:, i:i + 1], None, ALU.is_ge)
        k.inc("mv", inst)
    c3 = k.cnt["mv"]

    @block.scalar
    def _(s):
        s.wait_ge(k.sem("mv"), c3)
        s.activation(expall[:], suball[:], AF.Exp)
        k.inc("ms", s.copy(maskall2[:], maskall[:]))
    c4 = k.cnt["ms"]

    @block.vector
    def _(v):
        v.wait_ge(k.sem("ms"), c4)
        v.tensor_tensor(stv[:, :, 0:4], exv[:, :, 0:4], exv[:, :, 4:8], ALU.add)
        v.tensor_tensor(stv[:, :, 4:6], stv[:, :, 0:2], stv[:, :, 2:4], ALU.add)
        k.inc("mv", v.tensor_tensor(sumeall[:], stv[:, :, 4:5], stv[:, :, 5:6], ALU.add))
    crt0 = k.cnt["mv"]

    @block.scalar
    def _(s):
        s.wait_ge(k.sem("mv"), crt0)
        s.activation(pselall[:], sumeall[:], AF.Ln, scale=0.25)
        k.inc("ms", s.activation(sc4all[:], pselall[:], AF.Exp, scale=-1.0))
    crt = k.cnt["ms"]
    hist = {}

    for i in range(NTT):
        buf = i % 2

        @block.sync
        def _(sy, i=i, buf=buf):
            if i >= 2:
                sy.wait_ge(k.sem("mv"), hist[("hmdone", i - 2)])
            k.dma(sy, hs[buf][:], h_t[i], "qs")
        h_ld = k.cnt["qs"]

        @block.vector
        def _(v, i=i, buf=buf):
            v.wait_ge(k.sem("qs"), h_ld)
            v.wait_ge(k.sem("ms"), c4)
            if i >= 1:
                v.wait_ge(k.sem("mt"), hist[("hmtp", i - 1)])
            for e in range(8):
                inst = v.tensor_scalar(hm[:, e * 64:(e + 1) * 64],
                                       hs[buf][:, e * 64:(e + 1) * 64],
                                       maskall2[:, i * 8 + e:i * 8 + e + 1], None, ALU.mult)
            k.inc("mv", inst)
        hist[("hmdone", i)] = k.cnt["mv"]
        c5 = k.cnt["mv"]

        @block.tensor
        def _(t, i=i):
            t.wait_ge(k.sem("mv"), c5)
            if i >= 1:
                t.wait_ge(k.sem("ms"), hist[("hmTcp", i - 1)])
            for j in range(4):
                mm = t.transpose(ps_tp[:, j * 128:(j + 1) * 128],
                                 hm[:, j * 128:(j + 1) * 128], ident_bf[:])
            k.inc("mt", mm)
        hist[("hmtp", i)] = k.cnt["mt"]
        c6 = k.cnt["mt"]

        @block.scalar
        def _(s, i=i):
            s.wait_ge(k.sem("mt"), c6)
            k.inc("ms", s.copy(hmT[:], ps_tp[:, 0:512]))
        hist[("hmTcp", i)] = k.cnt["ms"]
        c7 = k.cnt["ms"]

        @block.tensor
        def _(t, i=i):
            t.wait_ge(k.sem("ms"), c7)
            if i == 0:
                t.wait_ge(k.sem("qg"), bw_ld)
            if i >= 1:
                t.wait_ge(k.sem("mv"), hist[("gdone", i - 1)])
            for n in range(4):
                for kk in range(4):
                    mm = t.matmul(pso[n][:], hmT[:, kk * 128:(kk + 1) * 128],
                                  bw_sb[:, kk * D + n * 512: kk * D + (n + 1) * 512],
                                  start=(kk == 0), stop=(kk == 3))
            k.inc("mt", mm)
        c8 = k.cnt["mt"]

        @block.vector
        def _(v, i=i, buf=buf):
            v.wait_ge(k.sem("mt"), c8)
            v.wait_ge(k.sem("ms"), crt)
            if i >= 2:
                v.wait_ge(k.sem("ms"), hist[("zdone", i - 2)])
            for n in range(4):
                inst = v.tensor_scalar(zb[i % 2][:, n * 512:(n + 1) * 512], pso[n][:],
                                       sc4all[:, i:i + 1], None, ALU.mult)
            k.inc("mv", inst)
        hist[("gdone", i)] = k.cnt["mv"]
        c8b = k.cnt["mv"]

        @block.scalar
        def _(s, i=i, buf=buf):
            s.wait_ge(k.sem("mv"), c8b)
            if i >= 2:
                s.wait_ge(k.sem("qs"), hist[("gst", i - 2)])
            inst = s.activation(gst[buf][:], zb[i % 2][:], AF.Silu)
            k.inc("ms", inst)
        hist[("zdone", i)] = k.cnt["ms"]
        c9 = k.cnt["ms"]

        @block.sync
        def _(sy, i=i, buf=buf):
            sy.wait_ge(k.sem("ms"), c9)
            k.dma(sy, g_t[i], gst[buf][:], "qs")
        hist[("gst", i)] = k.cnt["qs"]


def make_store_evac(k, out_hbm, W, tag, f32_chunk=None, f32_hbm=None, f32_w=512):
    nc, block = k.nc, k.block
    st = [k.sb(tag + f"_st{j}", [128, 512], BF16) for j in range(2)]
    stf = [k.sb(tag + f"_stf{j}", [128, f32_w], F32) for j in range(2)] if f32_hbm is not None else None
    st_hist = {}

    def evac(k, gi, ncg, ps, mm_cnt, group_pss):
        ib = (gi * 16 + ncg) % 2

        @block.scalar
        def _(s):
            s.wait_ge(k.sem("mt"), mm_cnt)
            if ("st", ib) in st_hist:
                s.wait_ge(k.sem("qs"), st_hist[("st", ib)])
            inst = s.copy(st[ib][:], ps[:])
            if f32_hbm is not None and ncg == f32_chunk:
                inst = s.copy(stf[ib][:], ps[:, 0:f32_w])
            k.inc("ms", inst)
        c1 = k.cnt["ms"]

        @block.sync
        def _(sy):
            sy.wait_ge(k.sem("ms"), c1)
            k.dma(sy, out_hbm[gi * 128:(gi + 1) * 128, ncg * 512:(ncg + 1) * 512],
                  st[ib][:], "qs")
            if f32_hbm is not None and ncg == f32_chunk:
                k.dma(sy, f32_hbm[gi * 128:(gi + 1) * 128, :], stf[ib][:], "qs")
        st_hist[("st", ib)] = k.cnt["qs"]

    return evac


def build_l3(nc, dbg=False, k=None, io=None):
    ext = {}

    def ein(name, shape, dt=F32):
        if io is not None:
            ext[name] = io[name]
            return ext[name]
        ext[name] = nc.dram_tensor(name, shape, dt, kind="ExternalInput").ap()
        return ext[name]

    def edecl(name, shape, dt=F32, kind="Internal"):
        if io is not None:
            return io[name]
        return nc.dram_tensor(name, shape, dt, kind=kind).ap()

    x = ein("x", [TOK, D])
    wqT = ein("wqT", [D, 2048])
    wkT = ein("wkT", [D, 512])
    wvT = ein("wvT", [D, 512])
    woT = ein("woT", [2048, D])
    awT = ein("awT", [D, 1024])
    bwT_h = ein("bwT", [512, D])
    cos_repq = ein("cos_repq", [T, 1024])
    sin_repq = ein("sin_repq", [T, 1024])
    cos_repk = ein("cos_repk", [T, 256])
    sin_repk = ein("sin_repk", [T, 256])
    ident_bf_h = ein("ident_bf", [128, 128], BF16)
    ident_f32_h = ein("ident_f32", [128, 128])
    ones1_h = ein("ones1", [128, 1], BF16)
    ones_row1_h = ein("ones_row1", [1, 128], BF16)
    masks_h = ein("masks_bf", [4 * 128, 512], BF16)
    masks2_h = ein("masks2_bf", [4 * 128, 512], BF16)
    xout = edecl("xout", [TOK, D], F32, "ExternalOutput")

    ikind = "ExternalOutput" if dbg else "Internal"
    xn_bf = edecl("xn_bf", [TOK, D], BF16, ikind)
    q_hbm = edecl("q_sc", [TOK, 2048], BF16, ikind)
    k_hbm = edecl("k_sc", [TOK, 512], BF16, ikind)
    v_hbm = edecl("v_sc", [TOK, 512], BF16, ikind)
    g_hbm = edecl("g_sc", [TOK, D], BF16, ikind)
    h_hbm = edecl("h_sc", [TOK, 1024], BF16, ikind)
    hlg_hbm = edecl("hlg_sc", [TOK, 8], F32, ikind)

    es = contextlib.ExitStack()
    with es:
        if k is None:
            block = es.enter_context(nc.Block())
            k = K(nc, block, es)
            for s_ in ["qs", "qg", "mt", "mv", "ms", "mg"]:
                k.sem(s_)
        else:
            block = k.block
            k = K.sub(k, es)

        ident_bf = k.sb("c_identbf", [128, 128], BF16)
        ident_f32 = k.sb("c_identf32", [128, 128], F32)
        ones1 = k.sb("c_ones1", [128, 1], BF16)
        ones_row1 = k.sb("c_onesrow", [1, 128], BF16)

        @block.sync
        def _(sy):
            k.dma(sy, ident_bf[:], ident_bf_h, "qs")
            k.dma(sy, ident_f32[:], ident_f32_h, "qs")
            k.dma(sy, ones1[:], ones1_h, "qs")
            k.dma(sy, ones_row1[:], ones_row1_h, "qs")
        k.barrier()

        with scoped(k) as kp:
            phase_ln(kp, x, out_bf_hbm=xn_bf, eps=1e-5, tag="ln1")
        k.barrier()

        with scoped(k) as kp:
            evac = make_qk_evac(kp, None, cos_repq, sin_repq, q_hbm, 0.0, "pq",
                                rope=True, do_rms=False, nh_total=16)
            phase_proj(kp, xn_bf, wqT, 2048, evac, "pq", evac_sem="mv")
        k.barrier()
        with scoped(k) as kp:
            evac = make_qk_evac(kp, None, cos_repk, sin_repk, k_hbm, 0.0, "pk",
                                rope=True, do_rms=False, nh_total=4)
            phase_proj(kp, xn_bf, wkT, 512, evac, "pk", evac_sem="mv")
        k.barrier()
        with scoped(k) as kp:
            evac = make_v_evac(kp, v_hbm, "pv", nch_total=1)
            phase_proj(kp, xn_bf, wvT, 512, evac, "pv", evac_sem="ms")
        k.barrier()

        with scoped(k) as kp:
            evac = make_store_evac(kp, h_hbm, 1024, "ph", f32_chunk=1,
                                   f32_hbm=hlg_hbm, f32_w=8)
            phase_proj(kp, xn_bf, awT, 1024, evac, "ph", evac_sem="ms")
        k.barrier()

        dbg3_hbm = nc.dram_tensor("dbg3", [TOK, 12], F32, kind=ikind).ap() if dbg else None
        with scoped(k) as kp:
            phase_l3ffn(kp, h_hbm, bwT_h, g_hbm, ident_bf, hlg_hbm=hlg_hbm,
                        dbg_hbm=dbg3_hbm)
        k.barrier()

        cfg = dict(q_hbm=q_hbm, k_hbm=k_hbm, v_hbm=v_hbm, woT_hbm=woT,
                   g_hbm=g_hbm, x_hbm=x, xout_hbm=xout,
                   ident_bf=ident_bf, ident_f32=ident_f32, ones1=ones1,
                   ones_row1=ones_row1, masks_hbm=masks_h, mask_dtype=BF16,
                   masks2_hbm=masks2_h,
                   softcap=None, scale=1.0 / float(np.sqrt(128.0)), kv_rep=4)
        phase_attn_oproj(k, cfg)
        k.barrier()
    return nc


def prep_l3_inputs(xc, w):
    P = perm_deint(128)
    out = {"x": np.ascontiguousarray(xc, dtype=np.float32)}

    def permrows(m, nh):
        return np.ascontiguousarray(m.reshape(nh, 128, -1)[:, P, :].reshape(nh * 128, -1))

    out["wqT"] = np.ascontiguousarray(permrows(w["L3_attn_Wq"], 16).T)
    out["wkT"] = np.ascontiguousarray(permrows(w["L3_attn_Wk"], 4).T)
    out["wvT"] = np.ascontiguousarray(w["L3_attn_Wv"].T)
    out["woT"] = np.ascontiguousarray(w["L3_attn_Wo"].T)
    A = w["L3_ffn_A"]
    Wg = w["L3_ffn_Wg"]
    awT = np.zeros((2048, 1024), dtype=np.float32)
    awT[:, 0:512] = A.reshape(512, 2048).T
    awT[:, 512:520] = Wg.T
    out["awT"] = awT
    out["bwT"] = np.ascontiguousarray(
        w["L3_ffn_Bw"].transpose(0, 2, 1).reshape(512, 2048))
    freqs = 1.0 / 10000.0 ** (np.arange(0, 128, 2, dtype=np.float32) / 128)
    ang = np.arange(T, dtype=np.float32)[:, None] * freqs[None, :]
    cos, sin = np.cos(ang), np.sin(ang)
    out["cos_repq"] = np.ascontiguousarray(np.tile(cos, (1, 16)))
    out["sin_repq"] = np.ascontiguousarray(np.tile(sin, (1, 16)))
    out["cos_repk"] = np.ascontiguousarray(np.tile(cos, (1, 4)))
    out["sin_repk"] = np.ascontiguousarray(np.tile(sin, (1, 4)))
    out["ident_bf"] = np.eye(128, dtype=bf)
    out["ident_f32"] = np.eye(128, dtype=np.float32)
    out["ones1"] = np.ones((128, 1), dtype=bf)
    out["ones_row1"] = np.ones((1, 128), dtype=bf)
    mk = np.zeros((4 * 128, 512), dtype=np.float32)
    for dlt in range(4):
        p = np.arange(128)[:, None]
        j = np.arange(512)[None, :]
        mk[dlt * 128:(dlt + 1) * 128] = np.where(j >= p + dlt * 128, 0.0, -1e30)
    out["masks_bf"] = mk.astype(bf)
    mk2 = np.zeros((4 * 128, 512), dtype=np.float32)
    for qs_ in range(4):
        p = np.arange(128)[:, None]
        j = np.arange(512)[None, :]
        mk2[qs_ * 128:(qs_ + 1) * 128] = np.where(j <= p + qs_ * 128, 0.0, -1e30)
    out["masks2_bf"] = mk2.astype(bf)
    return out


def phase_proj_rev(k, xnT_b, w_hbm_tiled, PTILES, evac_cb, tag, evac_sem="ms"):
    nc, block = k.nc, k.block
    wsb = [k.sb(f"{tag}_w{j}", [128, NKD * 128], BF16) for j in range(2)]
    pss = [k.ps(f"{tag}_ps{j}", [128, 512]) for j in range(8)]
    hist = {}
    for pt in range(PTILES):
        wbuf = pt % 2

        @block.gpsimd
        def _(g, pt=pt, wbuf=wbuf):
            if pt >= 2:
                g.wait_ge(k.sem("mt"), hist[("mm", pt - 2)])
            k.dma(g, wsb[wbuf][:], w_hbm_tiled[pt], "qg")
        wld = k.cnt["qg"]

        grp = pt % 2

        @block.tensor
        def _(t, pt=pt, wbuf=wbuf, grp=grp):
            t.wait_ge(k.sem("qg"), wld)
            if pt >= 2:
                t.wait_ge(k.sem(evac_sem), hist[("evac", pt - 2)])
            for kk in range(NKD):
                for n in range(4):
                    mm = t.matmul(pss[grp * 4 + n][:],
                                  wsb[wbuf][:, kk * 128:(kk + 1) * 128],
                                  xnT_b[:, kk * T + n * 512: kk * T + (n + 1) * 512],
                                  start=(kk == 0), stop=(kk == NKD - 1))
            k.inc("mt", mm)
        hist[("mm", pt)] = k.cnt["mt"]
        mm_cnt = k.cnt["mt"]

        for n in range(4):
            evac_cb(k, pt, n, pss[grp * 4 + n], mm_cnt)
        hist[("evac", pt)] = k.cnt[evac_sem]
    return hist


def build_l2a_mamba(k, b, io, cst):
    nc, block = k.nc, k.block
    NDT = 32

    with scoped(k) as kp:
        xnT_b = kp.sb(f"m{b}_xnT", [128, NKD * T], BF16)

        @block.sync
        def _(sy):
            for j in range(NKD):
                k.inc("qs", sy.dma_start_transpose(
                    xnT_b[:, j * T:(j + 1) * T],
                    io["xn_bf"][b * T:(b + 1) * T, j * 128:(j + 1) * 128]), 16)
        k.barrier()

        u_t = io["u_hbm"].rearrange("(i p) l -> i p l", p=128)
        g_t = io["gate_hbm"].rearrange("(i p) l -> i p l", p=128)
        st = [kp.sb(f"m{b}_ipst{j}", [128, T], BF16) for j in range(2)]
        st_hist = {}

        def ip_evac(k, pt, lc, ps, mm_cnt):
            ib = pt % 2

            @block.scalar
            def _(s):
                s.wait_ge(k.sem("mt"), mm_cnt)
                if lc == 0 and ("st", ib) in st_hist:
                    s.wait_ge(k.sem("qs"), st_hist[("st", ib)])
                k.inc("ms", s.copy(st[ib][:, lc * 512:(lc + 1) * 512], ps[:]))
            c1 = k.cnt["ms"]

            if lc == 3:
                @block.sync
                def _(sy):
                    sy.wait_ge(k.sem("ms"), c1)
                    tgt = u_t[pt] if pt < NDT else g_t[pt - NDT]
                    k.dma(sy, tgt, st[ib][:], "qs")
                st_hist[("st", ib)] = k.cnt["qs"]

        phase_proj_rev(kp, xnT_b, io["ipT_tiled"], 64, ip_evac, f"m{b}ip")
    k.barrier()

    with scoped(k) as kp:
        us = [kp.sb(f"m{b}_us{j}", [128, T + 4], BF16) for j in range(2)]
        u2s = [kp.sb(f"m{b}_u2s{j}", [128, T], BF16) for j in range(2)]
        dw = kp.sb(f"m{b}_dw", [128, 128], F32)
        diag = kp.sb(f"m{b}_diag", [128, 4 * 128], BF16)
        psc = [kp.ps(f"m{b}_psc{j}", [128, 512]) for j in range(4)]
        u_t = io["u_hbm"].rearrange("(i p) l -> i p l", p=128)
        u2_t = io["u2_hbm"].rearrange("(i p) l -> i p l", p=128)

        @block.sync
        def _(sy):
            k.dma(sy, dw[:, 0:NDT * 4], io["convw_tiled"], "qs")
        k.barrier()
        hist = {}
        for dt_ in range(NDT):
            buf = dt_ % 2

            @block.sync
            def _(sy, dt_=dt_, buf=buf):
                if dt_ >= 2:
                    sy.wait_ge(k.sem("mt"), hist[("cmm", dt_ - 2)])
                k.dma(sy, us[buf][:, 3:T + 3], u_t[dt_], "qs")
                k.dma(sy, us[buf][:, 0:3], io["zeros4"][0:128, 0:3], "qs")
            u_ld = k.cnt["qs"]

            @block.vector
            def _(v, dt_=dt_, buf=buf):
                if dt_ >= 1:
                    v.wait_ge(k.sem("mt"), hist[("cmm", dt_ - 1)])
                for tp in range(4):
                    inst = v.tensor_scalar(diag[:, tp * 128:(tp + 1) * 128],
                                           cst["ident_f32"][:],
                                           dw[:, dt_ * 4 + tp:dt_ * 4 + tp + 1], None, ALU.mult)
                k.inc("mv", inst)
            dg_done = k.cnt["mv"]

            @block.tensor
            def _(t, dt_=dt_, buf=buf):
                t.wait_ge(k.sem("mv"), dg_done)
                t.wait_ge(k.sem("qs"), u_ld)
                if dt_ >= 2:
                    t.wait_ge(k.sem("ms"), hist[("cevac", dt_ - 2)])
                for lc in range(4):
                    for tp in range(4):
                        mm = t.matmul(psc[lc][:], diag[:, tp * 128:(tp + 1) * 128],
                                      us[buf][:, lc * 512 + tp: lc * 512 + tp + 512],
                                      start=(tp == 0), stop=(tp == 3))
                k.inc("mt", mm)
            hist[("cmm", dt_)] = k.cnt["mt"]
            cm = k.cnt["mt"]

            @block.scalar
            def _(s, dt_=dt_, buf=buf):
                s.wait_ge(k.sem("mt"), cm)
                if dt_ >= 2:
                    s.wait_ge(k.sem("qs"), hist[("cst", dt_ - 2)])
                for lc in range(4):
                    inst = s.activation(u2s[buf][:, lc * 512:(lc + 1) * 512],
                                        psc[lc][:], AF.Silu, bias=cst["convb"][:, dt_:dt_ + 1])
                k.inc("ms", inst)
            hist[("cevac", dt_)] = k.cnt["ms"]
            ce = k.cnt["ms"]

            @block.sync
            def _(sy, dt_=dt_, buf=buf):
                sy.wait_ge(k.sem("ms"), ce)
                k.dma(sy, u2_t[dt_], u2s[buf][:], "qs")
            hist[("cst", dt_)] = k.cnt["qs"]
    k.barrier()

    with scoped(k) as kouter:
      ssm1 = kouter.sb(f"m{b}_ssm1", [128, T], BF16)
      ssm2 = kouter.sb(f"m{b}_ssm2", [128, T], BF16)
      with scoped(k) as kp:
        xpw = kp.sb(f"m{b}_xpw", [128, NDT * 256], BF16)
        u2sx = [kp.sb(f"m{b}_u2x{j}", [128, T], BF16) for j in range(2)]
        psx = [kp.ps(f"m{b}_psx{j}", [128, 512]) for j in range(8)]
        u2_t = io["u2_hbm"].rearrange("(i p) l -> i p l", p=128)

        @block.gpsimd
        def _(g):
            k.dma(g, xpw[:], io["xpT_tiled"], "qg")
        w_ld = k.cnt["qg"]
        hist = {}
        for kk in range(NDT):
            buf = kk % 2

            @block.sync
            def _(sy, kk=kk, buf=buf):
                if kk >= 2:
                    sy.wait_ge(k.sem("mt"), hist[("xmm", kk - 2)])
                k.dma(sy, u2sx[buf][:], u2_t[kk], "qs")
            u_ld = k.cnt["qs"]

            @block.tensor
            def _(t, kk=kk, buf=buf):
                t.wait_ge(k.sem("qs"), u_ld)
                if kk == 0:
                    t.wait_ge(k.sem("qg"), w_ld)
                for blk in range(2):
                    for lc in range(4):
                        mm = t.matmul(psx[blk * 4 + lc][:],
                                      xpw[:, kk * 256 + blk * 128: kk * 256 + (blk + 1) * 128],
                                      u2sx[buf][:, lc * 512:(lc + 1) * 512],
                                      start=(kk == 0), stop=(kk == NDT - 1))
                k.inc("mt", mm)
            hist[("xmm", kk)] = k.cnt["mt"]
        xp_done = k.cnt["mt"]

        @block.scalar
        def _(s):
            s.wait_ge(k.sem("mt"), xp_done)
            for lc in range(4):
                s.copy(ssm1[:, lc * 512:(lc + 1) * 512], psx[lc][:])
                inst = s.copy(ssm2[:, lc * 512:(lc + 1) * 512], psx[4 + lc][:])
            k.inc("ms", inst)
      k.barrier()

      with scoped(k) as kp2:
        phase_scan(k, kp2, b, io, cst, ssm1, ssm2)
    k.barrier()


def phase_scan(k, kp, b, io, cst, ssm1, ssm2):
    nc, block = k.nc, k.block
    NDT = 32
    Bb = kp.sb(f"s{b}_Bb", [128, 16 * 512], BF16)
    Cb = kp.sb(f"s{b}_Cb", [128, 16 * 512], BF16)
    dts = kp.sb(f"s{b}_dts", [128, 512], F32)
    esb = kp.sb(f"s{b}_esb", [128, 512], F32)
    dtu = kp.sb(f"s{b}_dtu", [128, 512], BF16)
    dAb = [kp.sb(f"s{b}_dAb{j}", [128, 512], BF16) for j in range(2)]
    beta = kp.sb(f"s{b}_beta", [128, 512], BF16)
    S_all = kp.sb(f"s{b}_S", [128, 16 * 512], BF16)
    P_all = kp.sb(f"s{b}_P", [128, 16 * 512], BF16)
    carry = kp.sb(f"s{b}_carry", [128, NDT * 16], F32)
    u2sl = [kp.sb(f"s{b}_u2sl{j}", [128, 512], BF16) for j in range(2)]
    gsl = [kp.sb(f"s{b}_gsl{j}", [128, 512], BF16) for j in range(2)]
    yt = kp.sb(f"s{b}_yt", [128, 512], F32)
    yg_all = kp.sb(f"s{b}_yg", [128, NDT * 512], BF16)
    opw = [kp.sb(f"s{b}_opw{j}", [128, NDT * 128], BF16) for j in range(2)]
    sgst = [kp.sb(f"s{b}_sgst{j}", [128, 512], BF16) for j in range(2)]
    sstg = [kp.sb(f"s{b}_sstg{j}", [128, 512], BF16) for j in range(2)]
    ps_bc = [kp.ps(f"s{b}_psbc{j}", [128, 512]) for j in range(2)]
    ps_dt = [kp.ps(f"s{b}_psdt{j}", [128, 512]) for j in range(2)]
    ps_y = [kp.ps(f"s{b}_psy{j}", [128, 512]) for j in range(2)]
    ps_o = [kp.ps(f"s{b}_pso{j}", [128, 512]) for j in range(2)]
    u2_tl = io["u2_hbm"].rearrange("(i p) (c f) -> i c p f", p=128, f=512)
    g_tl = io["gate_hbm"].rearrange("(i p) (c f) -> i c p f", p=128, f=512)
    s_hbm = io["s_hbm"]

    hist = {}

    for lc in range(4):
        for n in range(16):
            pb = n % 2

            @block.tensor
            def _(t, lc=lc, n=n, pb=pb):
                if ("bcf", pb) in hist:
                    t.wait_ge(k.sem("ms"), hist[("bcf", pb)])
                mm = t.matmul(ps_bc[pb][:], cst["ohB"][:, n * 128:(n + 1) * 128],
                              ssm2[0:32, lc * 512:(lc + 1) * 512], start=True, stop=True,
                              skip_group_check=True)
                k.inc("mt", mm)
            c1 = k.cnt["mt"]

            @block.scalar
            def _(s, lc=lc, n=n, pb=pb):
                s.wait_ge(k.sem("mt"), c1)
                k.inc("ms", s.copy(Bb[:, n * 512:(n + 1) * 512], ps_bc[pb][:]))
            hist[("bcf", pb)] = k.cnt["ms"]

            @block.tensor
            def _(t, lc=lc, n=n, pb=pb):
                t.wait_ge(k.sem("ms"), hist[("bcf", pb)])
                mm = t.matmul(ps_bc[pb][:], cst["ohC"][:, n * 128:(n + 1) * 128],
                              ssm2[0:32, lc * 512:(lc + 1) * 512], start=True, stop=True,
                              skip_group_check=True)
                k.inc("mt", mm)
            c2 = k.cnt["mt"]

            @block.scalar
            def _(s, lc=lc, n=n, pb=pb):
                s.wait_ge(k.sem("mt"), c2)
                k.inc("ms", s.copy(Cb[:, n * 512:(n + 1) * 512], ps_bc[pb][:]))
            hist[("bcf", pb)] = k.cnt["ms"]
        bc_done = k.cnt["ms"]

        for dt_ in range(NDT):
            pb = dt_ % 2
            ub = dt_ % 2

            @block.sync
            def _(sy, lc=lc, dt_=dt_, ub=ub):
                if ("uld", ub) in hist:
                    sy.wait_ge(k.sem("mv"), hist[("uld", ub)])
                k.dma(sy, u2sl[ub][:], u2_tl[dt_, lc], "qs")
                if ("gld", ub) in hist:
                    sy.wait_ge(k.sem("ms"), hist[("gld", ub)])
                k.dma(sy, gsl[ub][:], g_tl[dt_, lc], "qs")
            u_ld = k.cnt["qs"]

            @block.tensor
            def _(t, lc=lc, dt_=dt_, pb=pb):
                if ("dtf", pb) in hist:
                    t.wait_ge(k.sem("ms"), hist[("dtf", pb)])
                mm = t.matmul(ps_dt[pb][:], cst["dtwT"][:, dt_ * 128:(dt_ + 1) * 128],
                              ssm1[:, lc * 512:(lc + 1) * 512], start=True, stop=True)
                k.inc("mt", mm)
            c3 = k.cnt["mt"]

            @block.scalar
            def _(s, lc=lc, dt_=dt_, pb=pb):
                s.wait_ge(k.sem("mt"), c3)
                s.activation(esb[:], ps_dt[pb][:], AF.Exp, bias=cst["dtb"][:, dt_:dt_ + 1])
                k.inc("ms", s.activation(dts[:], esb[:], AF.Ln, bias=1.0))
            hist[("dtf", pb)] = k.cnt["ms"]
            c4 = k.cnt["ms"]

            @block.vector
            def _(v, lc=lc, dt_=dt_, ub=ub):
                v.wait_ge(k.sem("ms"), c4)
                v.wait_ge(k.sem("qs"), u_ld)
                k.inc("mv", v.tensor_tensor(dtu[:], dts[:], u2sl[ub][:], ALU.mult))
            c5 = k.cnt["mv"]

            for n in range(16):
                nb = n % 2

                @block.scalar
                def _(s, lc=lc, dt_=dt_, n=n, nb=nb):
                    if n == 0:
                        s.wait_ge(k.sem("mv"), c5)
                    if ("scanuse", nb) in hist:
                        s.wait_ge(k.sem("mv"), hist[("scanuse", nb)])
                    k.inc("ms", s.activation(dAb[nb][:], dts[:], AF.Exp,
                                             scale=cst["A_sb"][:, dt_ * 16 + n:dt_ * 16 + n + 1]))
                c6 = k.cnt["ms"]

                @block.vector
                def _(v, lc=lc, dt_=dt_, n=n, nb=nb):
                    v.wait_ge(k.sem("ms"), c6)
                    if n == 0 and ("ylast",) in hist:
                        v.wait_ge(k.sem("mt"), hist[("ylast",)])
                    v.tensor_tensor(beta[:], dtu[:], Bb[:, n * 512:(n + 1) * 512], ALU.mult)
                    if lc == 0:
                        init = 0.0
                    else:
                        init = carry[:, dt_ * 16 + n:dt_ * 16 + n + 1]
                    v.tensor_tensor_scan(S_all[:, n * 512:(n + 1) * 512], dAb[nb][:], beta[:],
                                         init, ALU.mult, ALU.add)
                    k.inc("mv", v.tensor_tensor(P_all[:, n * 512:(n + 1) * 512],
                                                S_all[:, n * 512:(n + 1) * 512],
                                                Cb[:, n * 512:(n + 1) * 512], ALU.mult))
                hist[("scanuse", nb)] = k.cnt["mv"]
                hist[("lastscan",)] = k.cnt["mv"]
                c7 = k.cnt["mv"]

                @block.tensor
                def _(t, lc=lc, dt_=dt_, n=n, pb=pb):
                    t.wait_ge(k.sem("mv"), c7)
                    if n == 0 and ("yfree", pb) in hist:
                        t.wait_ge(k.sem("mv"), hist[("yfree", pb)])
                    mm = t.matmul(ps_y[pb][:], cst["ident_bf"][:],
                                  P_all[:, n * 512:(n + 1) * 512],
                                  start=(n == 0), stop=(n == 15), skip_group_check=True)
                    k.inc("mt", mm)
            y_done = k.cnt["mt"]

            @block.scalar
            def _(s, lc=lc, dt_=dt_):
                s.wait_ge(k.sem("mt"), y_done)
                k.inc("ms", s.copy(carry[:, dt_ * 16:(dt_ + 1) * 16],
                                   S_all.ap()[:, 511:16 * 512:512]))
            carry_done = k.cnt["ms"]

            @block.scalar
            def _(s, lc=lc, dt_=dt_, ub=ub):
                s.wait_ge(k.sem("qs"), u_ld)
                k.inc("ms", s.activation(sgst[ub][:], gsl[ub][:], AF.Silu))
            hist[("gld", ub)] = k.cnt["ms"]
            sg_done = k.cnt["ms"]

            @block.vector
            def _(v, lc=lc, dt_=dt_, ub=ub, pb=pb):
                v.wait_ge(k.sem("mt"), y_done)
                v.wait_ge(k.sem("ms"), sg_done)
                if ("ygf",) in hist:
                    v.wait_ge(k.sem("mt"), hist[("ygf",)])
                v.tensor_scalar(yt[:], u2sl[ub][:], cst["Dp"][:, dt_:dt_ + 1], None, ALU.mult)
                v.tensor_tensor(yt[:], yt[:], ps_y[pb][:], ALU.add)
                k.inc("mv", v.tensor_tensor(yg_all[:, dt_ * 512:(dt_ + 1) * 512], yt[:],
                                            sgst[ub][:], ALU.mult))
            hist[("yfree", pb)] = k.cnt["mv"]
            hist[("uld", ub)] = k.cnt["mv"]
            hist[("ylast",)] = y_done
        yg_done = k.cnt["mv"]

        for Dmt in range(16):
            pb = Dmt % 2

            @block.gpsimd
            def _(g, lc=lc, Dmt=Dmt, pb=pb):
                if ("omm", pb) in hist:
                    g.wait_ge(k.sem("mt"), hist[("omm", pb)])
                k.dma(g, opw[pb][:], io["opT_tiled"][Dmt], "qg")
            ow_ld = k.cnt["qg"]

            @block.tensor
            def _(t, lc=lc, Dmt=Dmt, pb=pb):
                t.wait_ge(k.sem("qg"), ow_ld)
                t.wait_ge(k.sem("mv"), yg_done)
                if ("sof", pb) in hist:
                    t.wait_ge(k.sem("ms"), hist[("sof", pb)])
                for kk in range(NDT):
                    mm = t.matmul(ps_o[pb][:],
                                  opw[pb][:, kk * 128:(kk + 1) * 128],
                                  yg_all[:, kk * 512:(kk + 1) * 512],
                                  start=(kk == 0), stop=(kk == NDT - 1))
                k.inc("mt", mm)
            hist[("omm", pb)] = k.cnt["mt"]
            if Dmt == 15:
                hist[("ygf",)] = k.cnt["mt"]
            c8 = k.cnt["mt"]

            @block.scalar
            def _(s, lc=lc, Dmt=Dmt, pb=pb):
                s.wait_ge(k.sem("mt"), c8)
                if ("sost", pb) in hist:
                    s.wait_ge(k.sem("qs"), hist[("sost", pb)])
                k.inc("ms", s.copy(sstg[pb][:], ps_o[pb][:]))
            hist[("sof", pb)] = k.cnt["ms"]
            c9 = k.cnt["ms"]

            @block.sync
            def _(sy, lc=lc, Dmt=Dmt, pb=pb):
                sy.wait_ge(k.sem("ms"), c9)
                k.dma(sy, s_hbm[Dmt * 128:(Dmt + 1) * 128, lc * 512:(lc + 1) * 512],
                      sstg[pb][:], "qs")
            hist[("sost", pb)] = k.cnt["qs"]


def build_l2_mamba_only(nc, dbg=False, xn_out=False, k=None, io=None):
    ext = {}

    def ein(name, shape, dt=F32):
        if io is not None:
            ext[name] = io[name]
            return ext[name]
        ext[name] = nc.dram_tensor(name, shape, dt, kind="ExternalInput").ap()
        return ext[name]

    def edecl(name, shape, dt=F32, kind="Internal"):
        if io is not None:
            return io[name]
        return nc.dram_tensor(name, shape, dt, kind=kind).ap()

    x = ein("x", [TOK, D])
    ipT_tiled = ein("ipT_tiled", [64, 128, 2048])
    convw_tiled = ein("convw_tiled", [128, 128])
    xpT_tiled = ein("xpT_tiled", [128, 32 * 256])
    dtwT = ein("dtwT_h", [128, 4096])
    dtb = ein("dtb_h", [128, 32])
    alog = ein("alog_h", [128, 512])
    dp = ein("dp_h", [128, 32])
    convb = ein("convb_h", [128, 32])
    opT_tiled = ein("opT_tiled", [16, 128, 4096])
    ohB_h = ein("ohB", [32, 16 * 128], BF16)
    ohC_h = ein("ohC", [32, 16 * 128], BF16)
    zeros4 = ein("zeros4", [128, 4], BF16)
    ident_bf_h = ein("ident_bf", [128, 128], BF16)
    ident_f32_h = ein("ident_f32", [128, 128])
    s_out = edecl("s_out", [2 * D, T], BF16, "ExternalOutput")

    ikind = "ExternalOutput" if dbg else "Internal"
    xn_bf = edecl("xn_bf", [TOK, D], BF16,
                  "ExternalOutput" if (dbg or xn_out) else "Internal")
    u_hbm = edecl("u_sc", [4096, T], BF16, ikind)
    gate_hbm = edecl("gate_sc", [4096, T], BF16, ikind)
    u2_hbm = edecl("u2_sc", [4096, T], BF16, ikind)

    es = contextlib.ExitStack()
    with es:
        if k is None:
            block = es.enter_context(nc.Block())
            k = K(nc, block, es)
            for s_ in ["qs", "qg", "mt", "mv", "ms", "mg"]:
                k.sem(s_)
        else:
            block = k.block
            k = K.sub(k, es)
        cst = {}
        cst["ident_bf"] = k.sb("c_identbf", [128, 128], BF16)
        cst["ident_f32"] = k.sb("c_identf32", [128, 128], F32)
        cst["ohB"] = k.sb("c_ohB", [32, 16 * 128], BF16)
        cst["ohC"] = k.sb("c_ohC", [32, 16 * 128], BF16)
        cst["dtwT"] = k.sb("c_dtwT", [128, 4096], BF16)
        cst["dtb"] = k.sb("c_dtb", [128, 32], F32)
        cst["Dp"] = k.sb("c_dp", [128, 32], F32)
        cst["convb"] = k.sb("c_convb", [128, 32], F32)
        cst["A_sb"] = k.sb("c_asb", [128, 512], F32)
        atmp = k.sb("c_atmp", [128, 512], F32)

        @block.sync
        def _(sy):
            k.dma(sy, cst["ident_bf"][:], ident_bf_h, "qs")
            k.dma(sy, cst["ident_f32"][:], ident_f32_h, "qs")
            k.dma(sy, cst["ohB"][:], ohB_h, "qs")
            k.dma(sy, cst["ohC"][:], ohC_h, "qs")
            k.dma(sy, cst["dtb"][:], dtb, "qs")
            k.dma(sy, cst["Dp"][:], dp, "qs")
            k.dma(sy, cst["convb"][:], convb, "qs")
            k.dma(sy, atmp[:], alog, "qs")

        @block.gpsimd
        def _(g):
            k.dma(g, cst["dtwT"][:], dtwT, "qg")
        k.barrier()

        @block.scalar
        def _(s):
            k.inc("ms", s.activation(cst["A_sb"][:], atmp[:], AF.Exp))
        ca = k.cnt["ms"]

        @block.vector
        def _(v):
            v.wait_ge(k.sem("ms"), ca)
            k.inc("mv", v.tensor_scalar(cst["A_sb"][:], cst["A_sb"][:], -1.0, None, ALU.mult))
        k.barrier()

        with scoped(k) as kp:
            phase_ln(kp, x, out_bf_hbm=xn_bf, eps=1e-5, tag="ln1")
        k.barrier()

        for b in range(2):
            pio = dict(xn_bf=xn_bf, u_hbm=u_hbm, gate_hbm=gate_hbm, u2_hbm=u2_hbm,
                       ipT_tiled=ipT_tiled, convw_tiled=convw_tiled,
                       xpT_tiled=xpT_tiled, opT_tiled=opT_tiled, zeros4=zeros4,
                       s_hbm=s_out[b * D:(b + 1) * D, :])
            build_l2a_mamba(k, b, pio, cst)
    return nc


def prep_l2_mamba(xc, w):
    out = {"x": np.ascontiguousarray(xc, dtype=np.float32)}
    ip = w["L2_attn_in_proj_w"]
    ipT = ip.T
    out["ipT_tiled"] = np.ascontiguousarray(
        ipT.reshape(16, 128, 64, 128).transpose(2, 1, 0, 3).reshape(64, 128, 2048))
    cw = w["L2_attn_conv_w"].reshape(4096, 4)
    out["convw_tiled"] = np.ascontiguousarray(
        cw.reshape(32, 128, 4).transpose(1, 0, 2).reshape(128, 128))
    xp = w["L2_attn_x_proj_w"]
    xpT = np.zeros((4096, 256), dtype=np.float32)
    xpT[:, 0:160] = xp.T
    out["xpT_tiled"] = np.ascontiguousarray(
        xpT.reshape(32, 128, 256).transpose(1, 0, 2).reshape(128, 32 * 256))
    out["dtwT_h"] = np.ascontiguousarray(w["L2_attn_dt_proj_w"].T)
    out["dtb_h"] = np.ascontiguousarray(
        w["L2_attn_dt_proj_b"].reshape(32, 128).T)
    out["alog_h"] = np.ascontiguousarray(
        w["L2_attn_A_log"].reshape(32, 128, 16).transpose(1, 0, 2).reshape(128, 512))
    out["dp_h"] = np.ascontiguousarray(w["L2_attn_D_param"].reshape(32, 128).T)
    out["convb_h"] = np.ascontiguousarray(w["L2_attn_conv_b"].reshape(32, 128).T)
    op = w["L2_attn_out_proj_w"]
    opT = op.T
    out["opT_tiled"] = np.ascontiguousarray(
        opT.reshape(32, 128, 16, 128).transpose(2, 1, 0, 3).reshape(16, 128, 4096))
    ohB = np.zeros((32, 16 * 128), dtype=bf)
    ohC = np.zeros((32, 16 * 128), dtype=bf)
    for n in range(16):
        ohB[n, n * 128:(n + 1) * 128] = 1
        ohC[16 + n, n * 128:(n + 1) * 128] = 1
    out["ohB"] = ohB
    out["ohC"] = ohC
    out["zeros4"] = np.zeros((128, 4), dtype=bf)
    out["ident_bf"] = np.eye(128, dtype=bf)
    out["ident_f32"] = np.eye(128, dtype=np.float32)
    return out


def phase_router(k, io, cst, tag="rt"):
    nc, block = k.nc, k.block

    with scoped(k) as kp:
        sc_sb = cst["scores_sb"]
        stg = [kp.sb(tag + f"_stg{j}", [128, 16], F32) for j in range(2)]
        st_hist = {}

        def evac(k, gi, ncg, ps, mm_cnt, group_pss):
            ib = gi % 2

            @block.scalar
            def _(s):
                s.wait_ge(k.sem("mt"), mm_cnt)
                if ("st", ib) in st_hist:
                    s.wait_ge(k.sem("qs"), st_hist[("st", ib)])
                s.copy(stg[ib][:], ps[:, 0:16])
                k.inc("ms", s.copy(sc_sb[:, gi:gi + 1], ps[:, 0:1]))
            c1 = k.cnt["ms"]

            @block.sync
            def _(sy):
                sy.wait_ge(k.sem("ms"), c1)
                k.dma(sy, io["scores_hbm"][gi * 128:(gi + 1) * 128, :], stg[ib][:], "qs")
            st_hist[("st", ib)] = k.cnt["qs"]

        phase_proj(kp, io["xn_bf"], io["wrT"], 512, evac, tag + "p", evac_sem="ms")
    k.barrier()

    with scoped(k) as kp:
        lo = kp.sb(tag + "_lo", [128, 2], F32)
        hi = kp.sb(tag + "_hi", [128, 2], F32)
        lo2 = kp.sb(tag + "_lo2", [128, 2], F32)
        hi2 = kp.sb(tag + "_hi2", [128, 2], F32)
        thrsum = kp.sb(tag + "_thrs", [128, 2], F32)
        thr2 = kp.sb(tag + "_thr2", [128, 2], F32)
        gt = kp.sb(tag + "_gt", [128, 32], BF16)
        cnt2 = kp.sb(tag + "_cnt2", [128, 2], F32)
        acc_t = kp.sb(tag + "_acct", [128, 16], F32)
        ge = kp.sb(tag + "_ge", [128, 2], F32)
        nge = kp.sb(tag + "_nge", [128, 2], F32)
        ta = kp.sb(tag + "_ta", [128, 4], F32)
        tb = kp.sb(tag + "_tb", [128, 4], F32)
        ta2 = kp.sb(tag + "_ta2", [128, 4], F32)
        tb2 = kp.sb(tag + "_tb2", [128, 4], F32)
        tc2 = kp.sb(tag + "_tc2", [128, 4], F32)
        ps_c = kp.ps(tag + "_psc", [128, 512])

        @block.sync
        def _(sy):
            k.dma(sy, lo[:], io["init_lo"], "qs")
            k.dma(sy, hi[:], io["init_hi"], "qs")
        k.barrier()

        for it in range(52):
            @block.scalar
            def _(s, it=it):
                s.copy(lo2[:], lo[:])
                k.inc("ms", s.copy(hi2[:], hi[:]))
            c0 = k.cnt["ms"]

            @block.vector
            def _(v, it=it):
                v.wait_ge(k.sem("ms"), c0)
                k.inc("mv", v.tensor_tensor(thrsum[:], lo2[:], hi2[:], ALU.add))
            c1 = k.cnt["mv"]

            @block.scalar
            def _(s, it=it):
                s.wait_ge(k.sem("mv"), c1)
                k.inc("ms", s.activation(thr2[:], thrsum[:], AF.Copy, scale=0.5))
            c2 = k.cnt["ms"]

            @block.vector
            def _(v, it=it):
                v.wait_ge(k.sem("ms"), c2)
                for b in range(2):
                    inst = v.tensor_scalar(gt[:, b * 16:(b + 1) * 16],
                                           sc_sb[:, b * 16:(b + 1) * 16],
                                           thr2[:, b:b + 1], None, ALU.is_gt)
                k.inc("mv", inst)
            c3 = k.cnt["mv"]

            @block.tensor
            def _(t, it=it):
                t.wait_ge(k.sem("mv"), c3)
                k.inc("mt", t.matmul(ps_c[:, 0:32], cst["ones128_bf"][:], gt[:],
                                     start=True, stop=True))
            c4 = k.cnt["mt"]

            @block.scalar
            def _(s, it=it):
                s.wait_ge(k.sem("mt"), c4)
                for b in range(2):
                    inst = s.activation(acc_t[:], ps_c[:, b * 16:(b + 1) * 16],
                                        AF.Copy, accum_out=cnt2[:, b:b + 1])
                k.inc("ms", inst)
            c5 = k.cnt["ms"]

            @block.vector
            def _(v, it=it):
                v.wait_ge(k.sem("ms"), c5)
                v.tensor_scalar(ge[:], cnt2[:], 256.0, None, ALU.is_ge)
                k.inc("mv", v.tensor_scalar(nge[:], cnt2[:], 256.0, None, ALU.is_lt))
            c6 = k.cnt["mv"]

            @block.scalar
            def _(s, it=it):
                s.wait_ge(k.sem("mv"), c6)
                s.copy(ta[:, 0:2], ge[:])
                k.inc("ms", s.copy(ta[:, 2:4], nge[:]))
            c7 = k.cnt["ms"]

            @block.vector
            def _(v, it=it):
                v.wait_ge(k.sem("ms"), c7)
                v.tensor_tensor(tb[:, 0:2], ta[:, 0:2], thr2[:], ALU.mult)
                v.tensor_tensor(tb[:, 2:4], ta[:, 2:4], lo2[:], ALU.mult)
                v.tensor_tensor(ta2[:, 0:2], ta[:, 2:4], thr2[:], ALU.mult)
                k.inc("mv", v.tensor_tensor(ta2[:, 2:4], ta[:, 0:2], hi2[:], ALU.mult))
            c8 = k.cnt["mv"]

            @block.scalar
            def _(s, it=it):
                s.wait_ge(k.sem("mv"), c8)
                s.copy(tb2[:], tb[:])
                k.inc("ms", s.copy(tc2[:], ta2[:]))
            c9 = k.cnt["ms"]

            @block.vector
            def _(v, it=it):
                v.wait_ge(k.sem("ms"), c9)
                v.tensor_tensor(lo[:], tb2[:, 0:2], tb2[:, 2:4], ALU.add)
                k.inc("mv", v.tensor_tensor(hi[:], tc2[:, 0:2], tc2[:, 2:4], ALU.add))
        k.barrier()

        selm = kp.sb(tag + "_selm", [128, 32], F32)
        selmb = kp.sb(tag + "_selmb", [128, 32], BF16)
        selm2 = kp.sb(tag + "_selm2", [128, 32], F32)
        pos = kp.sb(tag + "_pos", [128, 32], F32)
        pos2 = kp.sb(tag + "_pos2", [128, 32], F32)
        pos2b = kp.sb(tag + "_pos2b", [128, 32], F32)
        toff = kp.sb(tag + "_toff", [128, 32], F32)
        toff2 = kp.sb(tag + "_toff2", [128, 32], F32)
        cutm = kp.sb(tag + "_cutm", [128, 32], F32)
        cutm2 = kp.sb(tag + "_cutm2", [128, 32], F32)
        selmf = kp.sb(tag + "_selmf", [128, 32], F32)
        selm2f = kp.sb(tag + "_selm2f", [128, 32], F32)
        tot_row = kp.sb(tag + "_tot", [128, 32], F32)
        off_sb = kp.sb(tag + "_off", [128, 32], F32)
        offr = kp.sb(tag + "_offr", [1, 32], F32)
        Mt = kp.sb(tag + "_M", [128, 256], F32)
        idxf = cst["idxf"]
        ps_p = kp.ps(tag + "_psp", [128, 512])
        ps_i = [kp.ps(tag + f"_psi{j}", [128, 512]) for j in range(4)]

        @block.scalar
        def _(s):
            k.inc("ms", s.copy(lo2[:], lo[:]))
        c5 = k.cnt["ms"]

        @block.vector
        def _(v):
            v.wait_ge(k.sem("ms"), c5)
            for b in range(2):
                inst = v.tensor_scalar(selm[:, b * 16:(b + 1) * 16],
                                       sc_sb[:, b * 16:(b + 1) * 16],
                                       lo2[:, b:b + 1], None, ALU.is_gt)
            k.inc("mv", inst)
        c6 = k.cnt["mv"]

        @block.scalar
        def _(s):
            s.wait_ge(k.sem("mv"), c6)
            s.copy(selm2[:], selm[:])
            k.inc("ms", s.copy(selmb[:], selm[:]))
        c6b = k.cnt["ms"]

        @block.tensor
        def _(t):
            t.wait_ge(k.sem("ms"), c6b)
            k.inc("mt", t.matmul(ps_p[:, 0:32], cst["ltstrict_bf"][:], selmb[:],
                                 start=True, stop=True))
            k.inc("mt", t.matmul(ps_p[:, 32:64], cst["ones128_bf"][:], selmb[:],
                                 start=True, stop=True))
        c7 = k.cnt["mt"]

        @block.scalar
        def _(s):
            s.wait_ge(k.sem("mt"), c7)
            s.copy(pos[:], ps_p[:, 0:32])
            k.inc("ms", s.copy(tot_row[:], ps_p[:, 32:64]))
        c8 = k.cnt["ms"]

        @block.vector
        def _(v):
            v.wait_ge(k.sem("ms"), c8)
            for b in range(2):
                inst = v.tensor_tensor_scan(off_sb[0:1, b * 16:(b + 1) * 16],
                                            cst["onesrow32"][0:1, 0:16],
                                            tot_row[0:1, b * 16:(b + 1) * 16], 0.0,
                                            ALU.mult, ALU.add)
            k.inc("mv", inst)
        c9 = k.cnt["mv"]

        @block.scalar
        def _(s):
            s.wait_ge(k.sem("mv"), c9)
            k.inc("ms", s.copy(offr[:], off_sb[0:1, 0:32]))
        c9b = k.cnt["ms"]

        @block.tensor
        def _(t):
            t.wait_ge(k.sem("ms"), c9b)
            k.inc("mt", t.matmul(ps_p[:, 64:96], cst["ones_rowf"][0:1, :],
                                 offr[0:1, 0:32], start=True, stop=True))
        c10 = k.cnt["mt"]

        @block.vector
        def _(v):
            v.wait_ge(k.sem("mt"), c10)
            k.inc("mv", v.tensor_tensor(toff[:], ps_p[:, 64:96], tot_row[:],
                                        ALU.subtract))
        c10b = k.cnt["mv"]

        @block.scalar
        def _(s):
            s.wait_ge(k.sem("mv"), c10b)
            k.inc("ms", s.copy(toff2[:], toff[:]))
        c10c = k.cnt["ms"]

        @block.vector
        def _(v):
            v.wait_ge(k.sem("ms"), c10c)
            k.inc("mv", v.tensor_tensor(pos2[:], toff2[:], pos[:], ALU.add))
        c11 = k.cnt["mv"]

        @block.scalar
        def _(s):
            s.wait_ge(k.sem("mv"), c11)
            k.inc("ms", s.copy(pos2b[:], pos2[:]))
        c11b = k.cnt["ms"]

        @block.vector
        def _(v):
            v.wait_ge(k.sem("ms"), c11b)
            k.inc("mv", v.tensor_scalar(cutm[:], pos2b[:], 256.0, None, ALU.is_lt))
        c11c = k.cnt["mv"]

        @block.scalar
        def _(s):
            s.wait_ge(k.sem("mv"), c11c)
            k.inc("ms", s.copy(cutm2[:], cutm[:]))
        c11d = k.cnt["ms"]

        @block.vector
        def _(v):
            v.wait_ge(k.sem("ms"), c11d)
            k.inc("mv", v.tensor_tensor(selmf[:], selm2[:], cutm2[:], ALU.mult))
        c12 = k.cnt["mv"]

        @block.scalar
        def _(s):
            s.wait_ge(k.sem("mv"), c12)
            k.inc("ms", s.copy(selm2f[:], selmf[:]))
        c13 = k.cnt["ms"]

        idx_hist = {}
        for g_ in range(32):
            b = g_ // 16
            gi = g_ % 16

            @block.vector
            def _(v, g_=g_):
                v.wait_ge(k.sem("ms"), c13)
                if g_ >= 1:
                    v.wait_ge(k.sem("mt"), idx_hist[g_ - 1])
                k.inc("mv", v.tensor_scalar(Mt[:], cst["iota256f"][:],
                                            pos2b[:, g_:g_ + 1], selm2f[:, g_:g_ + 1],
                                            ALU.is_equal, ALU.mult))
            cM = k.cnt["mv"]

            @block.tensor
            def _(t, g_=g_, b=b, gi=gi, cM=cM):
                t.wait_ge(k.sem("mv"), cM)
                for jb in range(2):
                    mm = t.matmul(ps_i[b * 2 + jb][:, 0:1],
                                  Mt[:, jb * 128:(jb + 1) * 128],
                                  cst["tvals"][:, g_:g_ + 1],
                                  start=(gi == 0), stop=(gi == 15),
                                  skip_group_check=True)
                k.inc("mt", mm)
            idx_hist[g_] = k.cnt["mt"]
        idx_done = k.cnt["mt"]

        @block.scalar
        def _(s):
            s.wait_ge(k.sem("mt"), idx_done)
            for c in range(4):
                s.copy(idxf[:, c:c + 1], ps_i[c][:, 0:1])
            s.activation(cst["poswf"][:, 0:2], idxf[:, 0:2], AF.Copy)
            k.inc("ms", s.activation(cst["poswf"][:, 2:4], idxf[:, 2:4], AF.Copy,
                                     bias=-2048.0))
        c14 = k.cnt["ms"]

        @block.vector
        def _(v):
            v.wait_ge(k.sem("ms"), c14)
            v.tensor_copy(cst["idxu"][:], idxf[:])
            v.tensor_copy(cst["poswu"][:], cst["poswf"][:])
            k.inc("mv", v.tensor_scalar(Mt[:], Mt[:], 1.0, None, ALU.mult))
        c15 = k.cnt["mv"]

        @block.sync
        def _(sy):
            sy.wait_ge(k.sem("mv"), c15)
            k.dma(sy, io["idx_hbm"], cst["idxu"][:], "qs")
            k.dma(sy, io["posw_hbm"], cst["poswu"][:], "qs")
            k.dma(sy, io["selm_hbm"], selm2f[:], "qs")
            if "dbg2_hbm" in io:
                k.dma(sy, io["dbg2_hbm"][:, 0:32], pos2b[:], "qs")
                k.dma(sy, io["dbg2_hbm"][:, 32:64], cutm2[:], "qs")
                k.dma(sy, io["dbg2_hbm"][:, 64:66], lo2[:], "qs")
                k.dma(sy, io["dbg2_hbm"][:, 66:98], tot_row[:], "qs")
                k.dma(sy, io["dbg2_hbm"][:, 98:130], toff2[:], "qs")
    k.barrier()


def phase_sel_gather(k, io, cst, tag="sg"):
    nc, block = k.nc, k.block
    with scoped(k) as kp:
        gx = kp.sb(tag + "_gx", [128, D], BF16)
        gc = kp.sb(tag + "_gc", [128, 1024], F32)
        gs = kp.sb(tag + "_gs", [128, 1024], F32)
        gsc = kp.sb(tag + "_gsc", [128, 16], F32)
        gcb = kp.sb(tag + "_gcb", [128, 1024], BF16)
        gsb = kp.sb(tag + "_gsb", [128, 1024], BF16)
        hist = {}
        for c in range(4):
            @block.gpsimd
            def _(g, c=c):
                if c >= 1:
                    g.wait_ge(k.sem("qs"), hist[c - 1])
                k.inc("qg", g.indirect_dma_start(
                    out=gx[:], out_offset=None, in_=io["xn_bf"],
                    in_offset=bass.IndirectOffsetOnAxis(ap=cst["idxu"][:, c:c + 1], axis=0)), 16)
                k.inc("qg", g.indirect_dma_start(
                    out=gc[:], out_offset=None, in_=io["cos_tab"],
                    in_offset=bass.IndirectOffsetOnAxis(ap=cst["poswu"][:, c:c + 1], axis=0)), 16)
                k.inc("qg", g.indirect_dma_start(
                    out=gs[:], out_offset=None, in_=io["sin_tab"],
                    in_offset=bass.IndirectOffsetOnAxis(ap=cst["poswu"][:, c:c + 1], axis=0)), 16)
                k.inc("qg", g.indirect_dma_start(
                    out=gsc[:], out_offset=None, in_=io["scores_hbm"],
                    in_offset=bass.IndirectOffsetOnAxis(ap=cst["idxu"][:, c:c + 1], axis=0)), 16)
            gl = k.cnt["qg"]

            @block.vector
            def _(v, c=c):
                v.wait_ge(k.sem("qg"), gl)
                v.tensor_copy(gcb[:], gc[:])
                k.inc("mv", v.tensor_copy(gsb[:], gs[:]))
            cv1 = k.cnt["mv"]

            @block.scalar
            def _(s, c=c):
                s.wait_ge(k.sem("qg"), gl)
                k.inc("ms", s.copy(cst["selsc"][:, c:c + 1], gsc[:, 0:1]))
            cs1 = k.cnt["ms"]

            @block.sync
            def _(sy, c=c):
                sy.wait_ge(k.sem("mv"), cv1)
                sy.wait_ge(k.sem("ms"), cs1)
                sy.wait_ge(k.sem("qg"), gl)
                k.dma(sy, io["selx_hbm"][c * 128:(c + 1) * 128, :], gx[:], "qs")
                k.dma(sy, io["cossel_hbm"][c * 128:(c + 1) * 128, :], gcb[:], "qs")
                k.dma(sy, io["sinsel_hbm"][c * 128:(c + 1) * 128, :], gsb[:], "qs")
            hist[c] = k.cnt["qs"]
        cs_all = k.cnt["ms"]

        @block.sync
        def _(sy):
            sy.wait_ge(k.sem("ms"), cs_all)
            k.dma(sy, io["selsc_hbm"], cst["selsc"][:], "qs")
    k.barrier()


def make_selv_evac(k, out_hbm, brep, tag, nch_total=2):
    nc, block = k.nc, k.block
    vst = [k.sb(tag + f"_vst{j}", [128, 512 * nch_total], BF16) for j in range(2)]
    out_t = out_hbm.rearrange("(i p) d -> i p d", p=128)
    st_hist = {}

    def evac(k, m, ncg, ps, mm_cnt, group_pss):
        nc4 = ncg % 4
        buf = m % 2

        @block.vector
        def _(v):
            v.wait_ge(k.sem("mt"), mm_cnt)
            if nc4 == 0 and m >= 2:
                v.wait_ge(k.sem("qs"), st_hist[m - 2])
            k.inc("mv", v.tensor_tensor(vst[buf][:, nc4 * 512:(nc4 + 1) * 512], ps[:],
                                        brep[:, nc4 * 512:(nc4 + 1) * 512], ALU.add))
        cp_done = k.cnt["mv"]

        if nc4 == nch_total - 1:
            @block.sync
            def _(sy):
                sy.wait_ge(k.sem("mv"), cp_done)
                k.dma(sy, out_t[m][:, 0:512 * nch_total], vst[buf][:, 0:512 * nch_total], "qs")
            st_hist[m] = k.cnt["qs"]

    return evac


def phase_rms(k, in_hbm, out_hbm, wrep, ntiles, eps, tag):
    nc, block = k.nc, k.block
    with scoped(k) as kp:
        xb = [kp.sb(tag + f"_x{j}", [128, D], BF16) for j in range(ntiles)]
        sq = kp.sb(tag + "_sq", [128, D], F32)
        ssq = kp.sb(tag + "_ssq", [128, ntiles], F32)
        veps = kp.sb(tag + "_veps", [128, ntiles], F32)
        lnc = kp.sb(tag + "_lnc", [128, ntiles], F32)
        rstd = kp.sb(tag + "_rstd", [128, ntiles], F32)
        t1 = kp.sb(tag + "_t1", [128, D], F32)
        ob = [kp.sb(tag + f"_o{j}", [128, D], BF16) for j in range(2)]
        hist = {}
        for i in range(ntiles):
            @block.sync
            def _(sy, i=i):
                k.dma(sy, xb[i][:], in_hbm[i * 128:(i + 1) * 128, :], "qs")
            ld = k.cnt["qs"]

            @block.scalar
            def _(s, i=i, ld=ld):
                s.wait_ge(k.sem("qs"), ld)
                k.inc("ms", s.activation(sq[:], xb[i][:], AF.Square,
                                         accum_out=ssq[:, i:i + 1]))
        csq = k.cnt["ms"]

        @block.vector
        def _(v):
            v.wait_ge(k.sem("ms"), csq)
            k.inc("mv", v.tensor_scalar(veps[:], ssq[:], 1.0 / D, eps, ALU.mult, ALU.add))
        c1 = k.cnt["mv"]

        @block.scalar
        def _(s):
            s.wait_ge(k.sem("mv"), c1)
            k.inc("ms", s.activation(lnc[:], veps[:], AF.Ln))
        c2 = k.cnt["ms"]

        @block.vector
        def _(v):
            v.wait_ge(k.sem("ms"), c2)
            k.inc("mv", v.tensor_scalar(lnc[:], lnc[:], -0.5, None, ALU.mult))
        c3 = k.cnt["mv"]

        @block.scalar
        def _(s):
            s.wait_ge(k.sem("mv"), c3)
            k.inc("ms", s.activation(rstd[:], lnc[:], AF.Exp))
        c4 = k.cnt["ms"]

        for i in range(ntiles):
            @block.vector
            def _(v, i=i):
                v.wait_ge(k.sem("ms"), c4)
                if i >= 2:
                    v.wait_ge(k.sem("qs"), hist[("st", i - 2)])
                v.tensor_scalar(t1[:], xb[i][:], rstd[:, i:i + 1], None, ALU.mult)
                k.inc("mv", v.tensor_tensor(ob[i % 2][:], t1[:], wrep[:], ALU.mult))
            cv = k.cnt["mv"]

            @block.sync
            def _(sy, i=i, cv=cv):
                sy.wait_ge(k.sem("mv"), cv)
                k.dma(sy, out_hbm[i * 128:(i + 1) * 128, :], ob[i % 2][:], "qs")
            hist[("st", i)] = k.cnt["qs"]
    k.barrier()


def build_l2b1(nc, dbg=False, k=None, io=None):
    ext = {}

    def ein(name, shape, dt=F32):
        if io is not None:
            ext[name] = io[name]
            return ext[name]
        ext[name] = nc.dram_tensor(name, shape, dt, kind="ExternalInput").ap()
        return ext[name]

    def edecl(name, shape, dt=F32, kind="Internal"):
        if io is not None:
            return io[name]
        return nc.dram_tensor(name, shape, dt, kind=kind).ap()

    xn_bf = edecl("xn_bf", [TOK, D], BF16, "ExternalInput")
    wrT = ein("wrT", [D, 512])
    cos_tab = ein("cos_tab", [T, 1024])
    sin_tab = ein("sin_tab", [T, 1024])
    init_lo = ein("init_lo", [128, 2])
    init_hi = ein("init_hi", [128, 2])
    c_ones128 = edecl("c_ones128", [128, 128], BF16, "ExternalInput")
    c_lt = edecl("c_lt", [128, 128], BF16, "ExternalInput")
    c_onesrow = ein("c_onesrow", [128, 32])
    c_onesrowf = ein("c_onesrowf", [1, 128])
    c_iota = ein("c_iota", [128, 256])
    c_tvals = ein("c_tvals", [128, 32])
    ln1rep_h = ein("ln1rep", [128, D])
    bqrep_h = ein("bqrep", [128, 2048])
    bkrep_h = ein("bkrep", [128, 1024])
    bvrep_h = ein("bvrep", [128, 1024])
    wq_h = ein("wq", [D, 2048])
    wk_h = ein("wk", [D, 1024])
    wv_h = ein("wv", [D, 1024])

    scores_hbm = edecl("scores", [TOK, 16], F32, "ExternalOutput")
    idx_hbm = edecl("idx", [128, 4], U32, "ExternalOutput")
    posw_hbm = edecl("posw", [128, 4], U32, "ExternalOutput")
    selm_hbm = edecl("selm", [128, 32], F32, "ExternalOutput")
    dbg2_hbm = (nc.dram_tensor("dbg2", [128, 130], F32, kind="ExternalOutput").ap()
                if dbg else None)
    selx_hbm = edecl("selx", [512, D], BF16, "ExternalOutput")
    selsc_hbm = edecl("selsc", [128, 4], F32, "ExternalOutput")
    qsel_hbm = edecl("qsel", [512, 2048], BF16, "ExternalOutput")
    ksel_hbm = edecl("ksel", [512, 1024], BF16, "ExternalOutput")
    vsel_hbm = edecl("vsel", [512, 1024], BF16, "ExternalOutput")
    ikind = "ExternalOutput" if dbg else "Internal"
    cossel_hbm = edecl("cossel", [512, 1024], BF16, ikind)
    sinsel_hbm = edecl("sinsel", [512, 1024], BF16, ikind)
    hsel_hbm = edecl("hsel", [512, D], BF16, ikind)

    es = contextlib.ExitStack()
    with es:
        if k is None:
            block = es.enter_context(nc.Block())
            k = K(nc, block, es)
            for s_ in ["qs", "qg", "mt", "mv", "ms", "mg"]:
                k.sem(s_)
        else:
            block = k.block
            k = K.sub(k, es)
        cst = {}
        cst["scores_sb"] = k.sb("c_scsb", [128, 32], F32)
        cst["idxf"] = k.sb("c_idxf", [128, 4], F32)
        cst["idxu"] = k.sb("c_idxu", [128, 4], U32)
        cst["poswf"] = k.sb("c_poswf", [128, 4], F32)
        cst["poswu"] = k.sb("c_poswu", [128, 4], U32)
        cst["selsc"] = k.sb("c_selsc", [128, 4], F32)
        cst["ones128_bf"] = k.sb("c_o128", [128, 128], BF16)
        cst["ltstrict_bf"] = k.sb("c_ltst", [128, 128], BF16)
        cst["onesrow32"] = k.sb("c_or32", [128, 32], F32)
        cst["ones_rowf"] = k.sb("c_orf", [1, 128], F32)
        cst["iota256f"] = k.sb("s_iota", [128, 256], F32)
        cst["tvals"] = k.sb("s_tvals", [128, 32], F32)
        ln1rep = k.sb("c_ln1r", [128, D], F32)
        bqrep = k.sb("c_bqr", [128, 2048], F32)
        bkrep = k.sb("c_bkr", [128, 1024], F32)
        bvrep = k.sb("c_bvr", [128, 1024], F32)

        @block.sync
        def _(sy):
            k.dma(sy, cst["ones128_bf"][:], c_ones128, "qs")
            k.dma(sy, cst["ltstrict_bf"][:], c_lt, "qs")
            k.dma(sy, cst["onesrow32"][:], c_onesrow, "qs")
            k.dma(sy, cst["ones_rowf"][:], c_onesrowf, "qs")
            k.dma(sy, cst["iota256f"][:], c_iota, "qs")
            k.dma(sy, cst["tvals"][:], c_tvals, "qs")
            k.dma(sy, ln1rep[:], ln1rep_h, "qs")
            k.dma(sy, bqrep[:], bqrep_h, "qs")
            k.dma(sy, bkrep[:], bkrep_h, "qs")
            k.dma(sy, bvrep[:], bvrep_h, "qs")
        k.barrier()

        pio = dict(xn_bf=xn_bf, wrT=wrT, scores_hbm=scores_hbm, init_lo=init_lo,
                   init_hi=init_hi, idx_hbm=idx_hbm, posw_hbm=posw_hbm,
                   selm_hbm=selm_hbm, cos_tab=cos_tab, sin_tab=sin_tab,
                   selx_hbm=selx_hbm, cossel_hbm=cossel_hbm, sinsel_hbm=sinsel_hbm,
                   selsc_hbm=selsc_hbm)
        if dbg2_hbm is not None:
            pio["dbg2_hbm"] = dbg2_hbm
        phase_router(k, pio, cst)
        phase_sel_gather(k, pio, cst)
        phase_rms(k, selx_hbm, hsel_hbm, ln1rep, 4, 1e-6, "rms1")

        with scoped(k) as kp:
            evac = make_qk_evac(kp, None, cossel_hbm, sinsel_hbm, qsel_hbm, 0.0,
                                "sq", rope=True, do_rms=False, nh_total=16, brep=bqrep)
            phase_proj(kp, hsel_hbm, wq_h, 2048, evac, "pq", evac_sem="mv",
                       HT=512, NTOK=512)
        k.barrier()
        with scoped(k) as kp:
            evac = make_qk_evac(kp, None, cossel_hbm, sinsel_hbm, ksel_hbm, 0.0,
                                "sk", rope=True, do_rms=False, nh_total=8, brep=bkrep)
            phase_proj(kp, hsel_hbm, wk_h, 1024, evac, "pk", evac_sem="mv",
                       HT=512, NTOK=512)
        k.barrier()
        with scoped(k) as kp:
            evac = make_selv_evac(kp, vsel_hbm, bvrep, "sv", nch_total=2)
            phase_proj(kp, hsel_hbm, wv_h, 1024, evac, "pv", evac_sem="mv",
                       HT=512, NTOK=512)
        k.barrier()
    return nc


def prep_l2b1_inputs(xn_bf16, w):
    out = {"xn_bf": np.ascontiguousarray(xn_bf16, dtype=bf)}
    wrT = np.zeros((D, 512), np.float32)
    wrT[:, 0] = w["L2_ffn_w_router"]
    out["wrT"] = wrT
    inv = 1.0 / 10000.0 ** (np.arange(0, 128, 2, dtype=np.float32) / 128)
    ang = np.arange(T, dtype=np.float32)[:, None] * inv[None, :]
    out["cos_tab"] = np.ascontiguousarray(np.tile(np.cos(ang), (1, 16)), np.float32)
    out["sin_tab"] = np.ascontiguousarray(np.tile(np.sin(ang), (1, 16)), np.float32)
    out["init_lo"] = np.full((128, 2), -1e5, np.float32)
    out["init_hi"] = np.full((128, 2), 1e5, np.float32)
    out["c_ones128"] = np.ones((128, 128), bf)
    out["c_lt"] = (np.arange(128)[:, None] < np.arange(128)[None, :]).astype(bf)
    out["c_onesrow"] = np.ones((128, 32), np.float32)
    out["c_onesrowf"] = np.ones((1, 128), np.float32)
    out["c_iota"] = np.tile(np.arange(256, dtype=np.float32)[None, :], (128, 1))
    out["c_tvals"] = np.ascontiguousarray(
        (np.arange(32)[None, :] * 128 + np.arange(128)[:, None]).astype(np.float32))
    out["ln1rep"] = np.tile(w["L2_ffn_ln1_w"][None, :], (128, 1)).astype(np.float32)
    out["bqrep"] = np.tile(w["L2_ffn_bq"][None, :], (128, 1)).astype(np.float32)
    out["bkrep"] = np.tile(w["L2_ffn_bk"][None, :], (128, 1)).astype(np.float32)
    out["bvrep"] = np.tile(w["L2_ffn_bv"][None, :], (128, 1)).astype(np.float32)
    out["wq"] = np.ascontiguousarray(w["L2_ffn_wq"], np.float32)
    out["wk"] = np.ascontiguousarray(w["L2_ffn_wk"], np.float32)
    out["wv"] = np.ascontiguousarray(w["L2_ffn_wv"], np.float32)
    return out


def phase_sel_attn(k, io, cst, tag="sa"):
    nc, block = k.nc, k.block
    nh, nkv = 16, 8
    scale = 1.0 / np.sqrt(128.0)
    SK = 4096
    NKT = SK // 128
    NKC = SK // 512
    with scoped(k) as kp:
        amq = [kp.sb(f"{tag}_amq{j}", [128, SK], BF16) for j in range(4)]
        amk = kp.sb(f"{tag}_amk", [128, NKT * 512], BF16)
        kt_sb = kp.sb(f"{tag}_kt", [128, SK], BF16)
        v_sb = kp.sb(f"{tag}_v", [128, NKT * 128], BF16)
        qt_sb = kp.sb(f"{tag}_qt", [128, 512], BF16)
        p_sb = [kp.sb(f"{tag}_p{j}", [128, 512], BF16) for j in range(2)]
        negm = kp.sb(f"{tag}_negm", [1, 512], BF16)
        drow = kp.sb(f"{tag}_drow", [1, 512], F32)
        mcol = kp.sb(f"{tag}_mcol", [128, 4], F32)
        mbig = kp.sb(f"{tag}_mbig", [128, 8], F32)
        mbig2 = kp.sb(f"{tag}_mbig2", [128, 8], F32)
        lnd = kp.sb(f"{tag}_lnd", [128, 4], F32)
        dhat = kp.sb(f"{tag}_dhat", [128, 4], F32)
        ao_sb = [kp.sb(f"{tag}_ao{j}", [128, 2048], BF16) for j in range(4)]
        ps_s = [kp.ps(f"{tag}_pss{j}", [128, 512]) for j in range(2)]
        ps_o = [kp.ps(f"{tag}_pso{j}", [128, 512]) for j in range(4)]
        ps_d = kp.ps(f"{tag}_psd", [128, 512])
        ps_sp = ps_o
        ident_bf, ident_f32 = cst["ident_bf"], cst["ident_f32"]
        ones1, ones_row1 = cst["ones1"], cst["ones_row1"]

        @block.sync
        def _(sy):
            for j in range(4):
                k.dma(sy, amq[j][:], io["amq_hbm"][j * 128:(j + 1) * 128, :], "qs")
            k.dma(sy, amk.ap().rearrange("p (j q) -> p j q", q=512),
                  io["amk_hbm"].rearrange("(j p) q -> p j q", p=128), "qs")
        k.barrier()

        hist = {}
        for h in range(nh):
            hk = h // 2

            @block.sync
            def _(sy, h=h, hk=hk):
                if h >= 1:
                    sy.wait_ge(k.sem("mt"), hist[("hmm", h - 1)])
                if h % 2 == 0:
                    k.inc("qs", sy.dma_start_transpose(
                        kt_sb[:], io["kall_hbm"][:, hk * 128:(hk + 1) * 128]), 16)
                    k.dma(sy, v_sb.ap().rearrange("p (j x) -> p j x", x=128),
                          io["vall_hbm"][:, hk * 128:(hk + 1) * 128]
                          .rearrange("(j p) x -> p j x", p=128), "qs")
                k.inc("qs", sy.dma_start_transpose(
                    qt_sb[:], io["qsel_hbm"][:, h * 128:(h + 1) * 128]), 16)
            qk_ld = k.cnt["qs"]

            for qsub in range(4):
                for kc in range(NKC):
                    pb = (qsub * NKC + kc) % 4

                    @block.tensor
                    def _(t, h=h, qsub=qsub, kc=kc, pb=pb):
                        t.wait_ge(k.sem("qs"), qk_ld)
                        if ("ofree",) in hist and qsub == 0 and kc == 0:
                            t.wait_ge(k.sem("mv"), hist[("ofree",)])
                        if ("spfree", pb) in hist:
                            t.wait_ge(k.sem("mv"), hist[("spfree", pb)])
                        mm = t.matmul(ps_sp[pb][:, 0:512],
                                      qt_sb[:, qsub * 128:(qsub + 1) * 128],
                                      kt_sb[:, kc * 512:(kc + 1) * 512],
                                      start=True, stop=False, skip_group_check=True)
                        mm = t.matmul(ps_sp[pb][:, 0:512], ident_bf[:],
                                      amq[qsub][:, kc * 512:(kc + 1) * 512],
                                      start=False, stop=True, skip_group_check=True)
                        k.inc("mt", mm)
                    sp_done = k.cnt["mt"]

                    @block.vector
                    def _(v, qsub=qsub, kc=kc, pb=pb, sp_done=sp_done):
                        v.wait_ge(k.sem("mt"), sp_done)
                        k.inc("mv", v.tensor_reduce(
                            mbig[:, kc:kc + 1], ps_sp[pb][:, 0:512], AXL.X, ALU.max))
                    hist[("spfree", pb)] = k.cnt["mv"]
                mb_done = k.cnt["mv"]

                @block.scalar
                def _(s, qsub=qsub, mb_done=mb_done):
                    s.wait_ge(k.sem("mv"), mb_done)
                    k.inc("ms", s.copy(mbig2[:], mbig[:]))
                mb2_done = k.cnt["ms"]

                @block.vector
                def _(v, qsub=qsub, mb2_done=mb2_done):
                    v.wait_ge(k.sem("ms"), mb2_done)
                    k.inc("mv", v.tensor_reduce(
                        mcol[:, qsub:qsub + 1], mbig2[:], AXL.X, ALU.max))
            mcol_done = k.cnt["mv"]

            @block.tensor
            def _(t, mcol_done=mcol_done):
                t.wait_ge(k.sem("mv"), mcol_done)
                if ("dtfree",) in hist:
                    t.wait_ge(k.sem("ms"), hist[("dtfree",)])
                if ("dtfree2",) in hist:
                    t.wait_ge(k.sem("ms"), hist[("dtfree2",)])
                for j in range(4):
                    mm = t.transpose(ps_d[0:1, j * 128:(j + 1) * 128],
                                     mcol[:, j:j + 1], ident_f32[:])
                k.inc("mt", mm)
            mt_done = k.cnt["mt"]

            @block.scalar
            def _(s, mt_done=mt_done):
                s.wait_ge(k.sem("mt"), mt_done)
                k.inc("ms", s.activation(negm[0:1, 0:512], ps_d[0:1, 0:512],
                                         AF.Copy, scale=-1.0))
            hist[("dtfree",)] = k.cnt["ms"]
            negm_done = k.cnt["ms"]

            for kt in range(NKT):
                pb = kt % 2

                @block.tensor
                def _(t, h=h, kt=kt, pb=pb, negm_done=negm_done):
                    t.wait_ge(k.sem("ms"), negm_done)
                    if ("sfree", pb) in hist:
                        t.wait_ge(k.sem("ms"), hist[("sfree", pb)])
                    mm = t.matmul(ps_s[pb][:], kt_sb[:, kt * 128:(kt + 1) * 128],
                                  qt_sb[:, 0:512],
                                  start=True, stop=False, skip_group_check=True)
                    mm = t.matmul(ps_s[pb][:], ident_bf[:],
                                  amk[:, kt * 512:(kt + 1) * 512],
                                  start=False, stop=False, skip_group_check=True)
                    mm = t.matmul(ps_s[pb][:], ones_row1[0:1, :], negm[0:1, 0:512],
                                  start=False, stop=True, skip_group_check=True)
                    k.inc("mt", mm)
                sbar_done = k.cnt["mt"]

                @block.scalar
                def _(s, kt=kt, pb=pb, sbar_done=sbar_done):
                    s.wait_ge(k.sem("mt"), sbar_done)
                    if ("pfree", pb) in hist:
                        s.wait_ge(k.sem("mt"), hist[("pfree", pb)])
                    k.inc("ms", s.activation(p_sb[pb][:], ps_s[pb][:], AF.Exp,
                                             scale=scale))
                hist[("sfree", pb)] = k.cnt["ms"]
                p_done = k.cnt["ms"]

                @block.tensor
                def _(t, kt=kt, pb=pb, p_done=p_done):
                    t.wait_ge(k.sem("ms"), p_done)
                    if kt == 0 and ("ofree",) in hist:
                        t.wait_ge(k.sem("mv"), hist[("ofree",)])
                    if kt == 0 and ("dfree",) in hist:
                        t.wait_ge(k.sem("ms"), hist[("dfree",)])
                    if kt == 0 and ("dtfree2",) in hist:
                        t.wait_ge(k.sem("ms"), hist[("dtfree2",)])
                    mm = t.matmul(ps_d[0:1, :], ones1[:, 0:1], p_sb[pb][:],
                                  start=(kt == 0), stop=(kt == NKT - 1),
                                  skip_group_check=True)
                    for sub in range(4):
                        mm = t.matmul(ps_o[sub][:, 0:128],
                                      p_sb[pb][:, sub * 128:(sub + 1) * 128],
                                      v_sb[:, kt * 128:(kt + 1) * 128],
                                      start=(kt == 0), stop=(kt == NKT - 1),
                                      skip_group_check=True)
                    k.inc("mt", mm)
                hist[("pfree", pb)] = k.cnt["mt"]
            pv_done = k.cnt["mt"]

            @block.scalar
            def _(s, pv_done=pv_done):
                s.wait_ge(k.sem("mt"), pv_done)
                k.inc("ms", s.copy(drow[0:1, :], ps_d[0:1, 0:512]))
            hist[("dfree",)] = k.cnt["ms"]
            drow_done = k.cnt["ms"]

            @block.tensor
            def _(t, drow_done=drow_done):
                t.wait_ge(k.sem("ms"), drow_done)
                for j in range(4):
                    mm = t.transpose(ps_d[:, 448 + j: 449 + j],
                                     drow[0:1, j * 128:(j + 1) * 128],
                                     ident_f32[0:1, 0:1])
                k.inc("mt", mm)
            dt_done = k.cnt["mt"]

            @block.scalar
            def _(s, dt_done=dt_done):
                s.wait_ge(k.sem("mt"), dt_done)
                s.activation(lnd[:, 0:4], ps_d[:, 448:452], AF.Ln)
                k.inc("ms", s.activation(dhat[:, 0:4], lnd[:, 0:4], AF.Exp,
                                         scale=-1.0))
            hist[("dtfree2",)] = k.cnt["ms"]
            dhat_done = k.cnt["ms"]

            @block.vector
            def _(v, h=h, dhat_done=dhat_done):
                v.wait_ge(k.sem("ms"), dhat_done)
                for sub in range(4):
                    inst = v.tensor_scalar(ao_sb[sub][:, h * 128:(h + 1) * 128],
                                           ps_o[sub][:, 0:128],
                                           dhat[:, sub:sub + 1], None, ALU.mult)
                k.inc("mv", inst)
            hist[("ofree",)] = k.cnt["mv"]
            hist[("hmm", h)] = k.cnt["mt"]
        ao_done = k.cnt["mv"]

        @block.sync
        def _(sy):
            sy.wait_ge(k.sem("mv"), ao_done)
            for qt in range(4):
                k.dma(sy, io["attnout_hbm"][qt * 128:(qt + 1) * 128, :],
                      ao_sb[qt][:], "qs")
    k.barrier()


def phase_silu_tiles(k, in_hbm, out_hbm, ntiles, tag):
    nc, block = k.nc, k.block
    with scoped(k) as kp:
        xb = [kp.sb(tag + f"_x{j}", [128, D], BF16) for j in range(2)]
        ob = [kp.sb(tag + f"_o{j}", [128, D], BF16) for j in range(2)]
        hist = {}
        for i in range(ntiles):
            @block.sync
            def _(sy, i=i):
                if i >= 2:
                    sy.wait_ge(k.sem("ms"), hist[("sl", i - 2)])
                k.dma(sy, xb[i % 2][:], in_hbm[i * 128:(i + 1) * 128, :], "qs")
            ld = k.cnt["qs"]

            @block.scalar
            def _(s, i=i, ld=ld):
                s.wait_ge(k.sem("qs"), ld)
                if i >= 2:
                    s.wait_ge(k.sem("qs"), hist[("st", i - 2)])
                k.inc("ms", s.activation(ob[i % 2][:], xb[i % 2][:], AF.Silu))
            hist[("sl", i)] = k.cnt["ms"]
            sl = k.cnt["ms"]

            @block.sync
            def _(sy, i=i, sl=sl):
                sy.wait_ge(k.sem("ms"), sl)
                k.dma(sy, out_hbm[i * 128:(i + 1) * 128, :], ob[i % 2][:], "qs")
            hist[("st", i)] = k.cnt["qs"]
    k.barrier()


def make_x1_evac(k, selx_hbm, x1_hbm, x1b_hbm, tag):
    nc, block = k.nc, k.block
    sxt = k.sb(tag + "_sx", [128, D], F32)
    x1t = [k.sb(tag + f"_x1{j}", [128, D], F32) for j in range(2)]
    x1bt = [k.sb(tag + f"_x1b{j}", [128, D], BF16) for j in range(2)]
    ld_hist = {}
    st_hist = {}

    def evac(k, m, ncg, ps, mm_cnt, group_pss):
        nc4 = ncg % 4
        buf = m % 2
        if nc4 == 0:
            @block.gpsimd
            def _(g, m=m):
                if m >= 1:
                    g.wait_ge(k.sem("mv"), ld_hist[m - 1])
                k.dma(g, sxt[:], selx_hbm[m * 128:(m + 1) * 128, :], "qg")
        sx_ld = k.cnt["qg"]

        @block.vector
        def _(v, m=m, nc4=nc4, buf=buf, sx_ld=sx_ld, mm_cnt=mm_cnt):
            v.wait_ge(k.sem("mt"), mm_cnt)
            v.wait_ge(k.sem("qg"), sx_ld)
            if nc4 == 0 and m >= 2:
                v.wait_ge(k.sem("qs"), st_hist[m - 2])
            v.tensor_tensor(x1t[buf][:, nc4 * 512:(nc4 + 1) * 512], ps[:],
                            sxt[:, nc4 * 512:(nc4 + 1) * 512], ALU.add)
            k.inc("mv", v.tensor_scalar(x1bt[buf][:, nc4 * 512:(nc4 + 1) * 512],
                                        x1t[buf][:, nc4 * 512:(nc4 + 1) * 512],
                                        1.0, None, ALU.mult))
        ld_hist[m] = k.cnt["mv"]
        cp_done = k.cnt["mv"]

        if nc4 == 3:
            @block.sync
            def _(sy, m=m, buf=buf, cp_done=cp_done):
                sy.wait_ge(k.sem("mv"), cp_done)
                k.dma(sy, x1_hbm[m * 128:(m + 1) * 128, :], x1t[buf][:], "qs")
                k.dma(sy, x1b_hbm[m * 128:(m + 1) * 128, :], x1bt[buf][:], "qs")
            st_hist[m] = k.cnt["qs"]

    return evac


def make_gate_evac_sel(k, out_hbm, tag):
    nc, block = k.nc, k.block
    st = [k.sb(tag + f"_st{j}", [128, 512], BF16) for j in range(2)]
    st_hist = {}

    def evac(k, m, ncg, ps, mm_cnt, group_pss):
        ib = (m * 16 + ncg) % 2

        @block.scalar
        def _(s, m=m, ncg=ncg, ib=ib, mm_cnt=mm_cnt):
            s.wait_ge(k.sem("mt"), mm_cnt)
            if ("st", ib) in st_hist:
                s.wait_ge(k.sem("qs"), st_hist[("st", ib)])
            k.inc("ms", s.activation(st[ib][:], ps[:], AF.Silu))
        c1 = k.cnt["ms"]

        @block.sync
        def _(sy, m=m, ncg=ncg, ib=ib, c1=c1):
            sy.wait_ge(k.sem("ms"), c1)
            k.dma(sy, out_hbm[m * 128:(m + 1) * 128, ncg * 512:(ncg + 1) * 512],
                  st[ib][:], "qs")
        st_hist[("st", ib)] = k.cnt["qs"]

    return evac


def make_up_evac_sel(k, gate_hbm, out_hbm, tag):
    nc, block = k.nc, k.block
    gld = [k.sb(tag + f"_g{j}", [128, 512], BF16) for j in range(2)]
    st = [k.sb(tag + f"_st{j}", [128, 512], BF16) for j in range(2)]
    hist = {}

    def evac(k, m, ncg, ps, mm_cnt, group_pss):
        ib = (m * 16 + ncg) % 2

        @block.gpsimd
        def _(g, m=m, ncg=ncg, ib=ib):
            if ("mul", ib) in hist:
                g.wait_ge(k.sem("mv"), hist[("mul", ib)])
            k.dma(g, gld[ib][:],
                  gate_hbm[m * 128:(m + 1) * 128, ncg * 512:(ncg + 1) * 512], "qg")
        g_ld = k.cnt["qg"]

        @block.vector
        def _(v, m=m, ncg=ncg, ib=ib, mm_cnt=mm_cnt, g_ld=g_ld):
            v.wait_ge(k.sem("mt"), mm_cnt)
            v.wait_ge(k.sem("qg"), g_ld)
            if ("st", ib) in hist:
                v.wait_ge(k.sem("qs"), hist[("st", ib)])
            k.inc("mv", v.tensor_tensor(st[ib][:], ps[:], gld[ib][:], ALU.mult))
        hist[("mul", ib)] = k.cnt["mv"]
        c1 = k.cnt["mv"]

        @block.sync
        def _(sy, m=m, ncg=ncg, ib=ib, c1=c1):
            sy.wait_ge(k.sem("mv"), c1)
            k.dma(sy, out_hbm[m * 128:(m + 1) * 128, ncg * 512:(ncg + 1) * 512],
                  st[ib][:], "qs")
        hist[("st", ib)] = k.cnt["qs"]

    return evac


def make_proc_evac(k, x1_hbm, proc_hbm, tag):
    nc, block = k.nc, k.block
    x1t = [k.sb(tag + f"_x1{j}", [128, D], F32) for j in range(4)]
    pt = [k.sb(tag + f"_p{j}", [128, 512], F32) for j in range(2)]
    hist = {}

    @block.gpsimd
    def _(g):
        for m in range(4):
            k.dma(g, x1t[m][:], x1_hbm[m * 128:(m + 1) * 128, :], "qg")
    x1_ld = k.cnt["qg"]

    def evac(k, m, ncg, ps, mm_cnt, group_pss):
        ib = (m + ncg) % 2

        @block.vector
        def _(v, m=m, ncg=ncg, ib=ib, mm_cnt=mm_cnt):
            v.wait_ge(k.sem("mt"), mm_cnt)
            v.wait_ge(k.sem("qg"), x1_ld)
            if ("st", ib) in hist:
                v.wait_ge(k.sem("qs"), hist[("st", ib)])
            k.inc("mv", v.tensor_tensor(pt[ib][:], ps[:],
                                        x1t[m][:, ncg * 512:(ncg + 1) * 512], ALU.add))
        cp = k.cnt["mv"]

        @block.sync
        def _(sy, m=m, ncg=ncg, ib=ib, cp=cp):
            sy.wait_ge(k.sem("mv"), cp)
            k.dma(sy, proc_hbm[m * 128:(m + 1) * 128, ncg * 512:(ncg + 1) * 512],
                  pt[ib][:], "qs")
        hist[("st", ib)] = k.cnt["qs"]

    return evac


def phase_blend_scatter(k, io, cst, tag="bl"):
    nc, block = k.nc, k.block
    with scoped(k) as kp:
        proct = kp.sb(tag + "_pr", [128, D], F32)
        sxt = kp.sb(tag + "_sx", [128, D], BF16)
        sxf = kp.sb(tag + "_sxf", [128, D], F32)
        dt_ = kp.sb(tag + "_d", [128, D], F32)
        dg = kp.sb(tag + "_dg", [128, D], F32)
        gated = kp.sb(tag + "_gt", [128, D], F32)
        gs_t = kp.sb(tag + "_gs", [128, D], BF16)
        th = kp.sb(tag + "_th", [128, 4], F32)
        gate_c = kp.sb(tag + "_gc", [128, 4], F32)
        gate2 = kp.sb(tag + "_g2", [128, 4], F32)

        @block.scalar
        def _(s):
            k.inc("ms", s.activation(th[:], cst["selsc"][:], AF.Tanh, scale=0.5))
        c1 = k.cnt["ms"]

        @block.vector
        def _(v):
            v.wait_ge(k.sem("ms"), c1)
            k.inc("mv", v.tensor_scalar(gate_c[:], th[:], 0.5, 0.5, ALU.mult, ALU.add))
        c2 = k.cnt["mv"]

        @block.scalar
        def _(s):
            s.wait_ge(k.sem("mv"), c2)
            k.inc("ms", s.copy(gate2[:], gate_c[:]))
        c3 = k.cnt["ms"]

        hist = {}
        for c in range(4):
            @block.sync
            def _(sy, c=c):
                if c >= 1:
                    sy.wait_ge(k.sem("qg"), hist[c - 1])
                k.dma(sy, proct[:], io["proc_hbm"][c * 128:(c + 1) * 128, :], "qs")
                k.dma(sy, sxt[:], io["selx_hbm"][c * 128:(c + 1) * 128, :], "qs")
            ld = k.cnt["qs"]

            @block.vector
            def _(v, c=c, ld=ld):
                v.wait_ge(k.sem("qs"), ld)
                v.wait_ge(k.sem("ms"), c3)
                v.tensor_scalar(sxf[:], sxt[:], 1.0, None, ALU.mult)
                v.tensor_tensor(dt_[:], proct[:], sxf[:], ALU.subtract)
                v.tensor_scalar(dg[:], dt_[:], gate2[:, c:c + 1], None, ALU.mult)
                k.inc("mv", v.tensor_tensor(gated[:], sxf[:], dg[:], ALU.add))
            c4 = k.cnt["mv"]

            @block.scalar
            def _(s, c=c, c4=c4):
                s.wait_ge(k.sem("mv"), c4)
                k.inc("ms", s.activation(gs_t[:], gated[:], AF.Silu))
            c5 = k.cnt["ms"]

            @block.gpsimd
            def _(g, c=c, c5=c5):
                g.wait_ge(k.sem("ms"), c5)
                k.inc("qg", g.indirect_dma_start(
                    out=io["g_hbm"], out_offset=bass.IndirectOffsetOnAxis(
                        ap=cst["idxu"][:, c:c + 1], axis=0),
                    in_=gs_t[:], in_offset=None), 16)
            hist[c] = k.cnt["qg"]
    k.barrier()


def phase_residual(k, io, tag="rs"):
    nc, block = k.nc, k.block
    with scoped(k) as kp:
        xt = [kp.sb(tag + f"_x{j}", [128, D], F32) for j in range(2)]
        gt_ = [kp.sb(tag + f"_g{j}", [128, D], BF16) for j in range(2)]
        sT = [kp.sb(tag + f"_s{j}", [128, D], BF16) for j in range(2)]
        sg = kp.sb(tag + "_sg", [128, D], F32)
        xo = [kp.sb(tag + f"_o{j}", [128, D], F32) for j in range(2)]
        hist = {}
        for m in range(NTT):
            b, lt = m // 16, m % 16
            buf = m % 2

            @block.sync
            def _(sy, m=m, b=b, lt=lt, buf=buf):
                if m >= 2:
                    sy.wait_ge(k.sem("mv"), hist[("sg", m - 2)])
                k.dma(sy, xt[buf][:], io["x_hbm"][m * 128:(m + 1) * 128, :], "qs")
                k.dma(sy, gt_[buf][:], io["g_hbm"][m * 128:(m + 1) * 128, :], "qs")
                for j in range(16):
                    k.inc("qs", sy.dma_start_transpose(
                        sT[buf][:, j * 128:(j + 1) * 128],
                        io["s_hbm"][b * D + j * 128:b * D + (j + 1) * 128,
                                    lt * 128:(lt + 1) * 128]), 16)
            ld = k.cnt["qs"]

            @block.vector
            def _(v, m=m, buf=buf, ld=ld):
                v.wait_ge(k.sem("qs"), ld)
                if m >= 2:
                    v.wait_ge(k.sem("qs"), hist[("st", m - 2)])
                v.tensor_tensor(sg[:], sT[buf][:], gt_[buf][:], ALU.mult)
                k.inc("mv", v.tensor_tensor(xo[buf][:], sg[:], xt[buf][:], ALU.add))
            hist[("sg", m)] = k.cnt["mv"]
            cv = k.cnt["mv"]

            @block.sync
            def _(sy, m=m, buf=buf, cv=cv):
                sy.wait_ge(k.sem("mv"), cv)
                k.dma(sy, io["xout_hbm"][m * 128:(m + 1) * 128, :], xo[buf][:], "qs")
            hist[("st", m)] = k.cnt["qs"]
    k.barrier()


def build_l2b2(nc, core_id=0, dbg=False, k=None, io=None):
    ext = {}

    def ein(name, shape, dt=F32):
        if io is not None:
            ext[name] = io[name]
            return ext[name]
        ext[name] = nc.dram_tensor(name, shape, dt, kind="ExternalInput").ap()
        return ext[name]

    def edecl(name, shape, dt=F32, kind="Internal"):
        if io is not None:
            return io[name]
        return nc.dram_tensor(name, shape, dt, kind=kind).ap()

    xn_bf = edecl("xn_bf", [TOK, D], BF16, "ExternalInput")
    x_h = ein("x", [TOK, D])
    s_h = edecl("s_in", [2 * D, T], BF16, "ExternalInput")
    qsel = edecl("qsel", [512, 2048], BF16, "ExternalInput")
    kall = edecl("kall", [4096, 1024], BF16, "ExternalInput")
    vall = edecl("vall", [4096, 1024], BF16, "ExternalInput")
    selx = edecl("selx", [512, D], BF16, "ExternalInput")
    idx_in = edecl("idx", [128, 4], U32, "ExternalInput")
    selsc_in = ein("selsc", [128, 4])
    amq_h = edecl("amq", [512, 4096], BF16, "ExternalInput")
    amk_h = edecl("amk", [4096, 512], BF16, "ExternalInput")
    wo_h = ein("wo", [2048, 2048])
    wg_h = ein("wg", [2048, 8192])
    wu_h = ein("wu", [2048, 8192])
    wd_h = ein("wd", [8192, 2048])
    ln2rep_h = ein("ln2rep", [128, D])
    ident_bf_h = edecl("ident_bf", [128, 128], BF16, "ExternalInput")
    ident_f32_h = ein("ident_f32", [128, 128])
    ones1_h = edecl("ones1", [128, 1], BF16, "ExternalInput")
    ones_row1_h = edecl("ones_row1", [1, 128], BF16, "ExternalInput")

    xout = edecl("xout", [TOK, D], F32, "ExternalOutput")
    ikind = "ExternalOutput" if dbg else "Internal"
    g_hbm = edecl("g_sc", [TOK, D], BF16, ikind)
    attnout = edecl("attnout", [512, 2048], BF16, ikind)
    x1_hbm = edecl("x1_sc", [512, D], F32, ikind)
    x1b_hbm = edecl("x1b_sc", [512, D], BF16, ikind)
    h2_hbm = edecl("h2_sc", [512, D], BF16, ikind)
    gsel_hbm = edecl("gsel_sc", [512, 8192], BF16, ikind)
    gu_hbm = edecl("gu_sc", [512, 8192], BF16, ikind)
    proc_hbm = edecl("proc_sc", [512, D], F32, ikind)

    es = contextlib.ExitStack()
    with es:
        if k is None:
            block = es.enter_context(nc.Block())
            k = K(nc, block, es)
            for s_ in ["qs", "qg", "mt", "mv", "ms", "mg"]:
                k.sem(s_)
        else:
            block = k.block
            k = K.sub(k, es)
        cst = {}
        cst["ident_bf"] = k.sb("c_identbf", [128, 128], BF16)
        cst["ident_f32"] = k.sb("c_identf32", [128, 128], F32)
        cst["ones1"] = k.sb("c_ones1", [128, 1], BF16)
        cst["ones_row1"] = k.sb("c_onesrow1", [1, 128], BF16)
        cst["idxu"] = k.sb("c_idxu", [128, 4], U32)
        cst["selsc"] = k.sb("c_selsc", [128, 4], F32)
        ln2rep = k.sb("c_ln2r", [128, D], F32)

        @block.sync
        def _(sy):
            k.dma(sy, cst["ident_bf"][:], ident_bf_h, "qs")
            k.dma(sy, cst["ident_f32"][:], ident_f32_h, "qs")
            k.dma(sy, cst["ones1"][:], ones1_h, "qs")
            k.dma(sy, cst["ones_row1"][:], ones_row1_h, "qs")
            k.dma(sy, cst["idxu"][:], idx_in, "qs")
            k.dma(sy, cst["selsc"][:], selsc_in, "qs")
            k.dma(sy, ln2rep[:], ln2rep_h, "qs")
        k.barrier()

        pio = dict(qsel_hbm=qsel, kall_hbm=kall, vall_hbm=vall, amq_hbm=amq_h,
                   amk_hbm=amk_h, attnout_hbm=attnout, selx_hbm=selx,
                   proc_hbm=proc_hbm, g_hbm=g_hbm, x_hbm=x_h, s_hbm=s_h,
                   xout_hbm=xout)

        phase_silu_tiles(k, xn_bf, g_hbm, NTT, "gb")
        phase_sel_attn(k, pio, cst)

        with scoped(k) as kp:
            evac = make_x1_evac(kp, selx, x1_hbm, x1b_hbm, "x1")
            phase_proj(kp, attnout, wo_h, 2048, evac, "po", evac_sem="mv",
                       HT=512, NTOK=512)
        k.barrier()

        phase_rms(k, x1b_hbm, h2_hbm, ln2rep, 4, 1e-6, "rms2")

        with scoped(k) as kp:
            evac = make_gate_evac_sel(kp, gsel_hbm, "mg")
            phase_proj(kp, h2_hbm, wg_h, 8192, evac, "pg", evac_sem="ms",
                       HT=512, NTOK=512)
        k.barrier()

        with scoped(k) as kp:
            evac = make_up_evac_sel(kp, gsel_hbm, gu_hbm, "mu")
            phase_proj(kp, h2_hbm, wu_h, 8192, evac, "pu", evac_sem="mv",
                       HT=512, NTOK=512)
        k.barrier()

        with scoped(k) as kp:
            evac = make_proc_evac(kp, x1_hbm, proc_hbm, "pd")
            phase_proj(kp, gu_hbm, wd_h, 2048, evac, "pdn", evac_sem="mv",
                       KSTR=64, HT=512, NTOK=512, PASSW=512, nbuf=1)
        k.barrier()

        phase_blend_scatter(k, pio, cst)
        phase_residual(k, pio)
    return nc


def prep_l2b2_inputs(xn_bf16, x_f32, s_bf16, qsel, kall, vall, selx, idx_u32,
                     selsc, w, core_id):
    out = {
        "xn_bf": np.ascontiguousarray(xn_bf16, bf),
        "x": np.ascontiguousarray(x_f32, np.float32),
        "s_in": np.ascontiguousarray(s_bf16, bf),
        "qsel": np.ascontiguousarray(qsel, bf),
        "kall": np.ascontiguousarray(kall, bf),
        "vall": np.ascontiguousarray(vall, bf),
        "selx": np.ascontiguousarray(selx, bf),
        "idx": np.ascontiguousarray(idx_u32, np.uint32),
        "selsc": np.ascontiguousarray(selsc, np.float32),
        "wo": np.ascontiguousarray(w["L2_ffn_wo"], np.float32),
        "wg": np.ascontiguousarray(w["L2_ffn_w_gate"], np.float32),
        "wu": np.ascontiguousarray(w["L2_ffn_w_up"], np.float32),
        "wd": np.ascontiguousarray(w["L2_ffn_w_down"], np.float32),
        "ln2rep": np.tile(w["L2_ffn_ln2_w"][None, :], (128, 1)).astype(np.float32),
        "ident_bf": np.eye(128, dtype=bf),
        "ident_f32": np.eye(128, dtype=np.float32),
        "ones1": np.ones((128, 1), bf),
        "ones_row1": np.ones((1, 128), bf),
    }
    qg = core_id * 512 + np.arange(512)
    kg = np.arange(4096)
    allow = kg[None, :] <= qg[:, None]
    amq = np.where(allow, 0.0, -1e30).astype(bf)
    out["amq"] = amq
    out["amk"] = np.ascontiguousarray(amq.T)
    return out


def kernel_multilaunch(**inputs):
    from concourse.bass_utils import run_bass_kernel_spmd

    w = {n: np.asarray(v) for n, v in inputs.items()}
    x = np.ascontiguousarray(np.asarray(w["L0_attn_x"], dtype=np.float32))
    xs = [np.ascontiguousarray(x[2 * c:2 * c + 2].reshape(TOK, D)) for c in range(8)]
    cores = list(range(8))

    def VB(a):
        a = np.asarray(a)
        return a.view(bf) if a.dtype.kind == 'V' else a.astype(bf)

    def launch(build_fn, prep_fn, **bkw):
        nc = build_fn(bass.Bass("TRN2", target_bir_lowering=False), **bkw)
        ims = [prep_fn(c) for c in cores]
        return run_bass_kernel_spmd(nc, ims, core_ids=cores).results

    r = launch(build_l0, lambda c: prep_l0_inputs(xs[c], w))
    xs = [np.asarray(r[c]["xout"], dtype=np.float32) for c in cores]
    r = launch(build_l1, lambda c: prep_l1_inputs(xs[c], w))
    xs = [np.asarray(r[c]["xout"], dtype=np.float32) for c in cores]
    r = launch(build_l2_mamba_only, lambda c: prep_l2_mamba(xs[c], w), xn_out=True)
    s_bf = [VB(r[c]["s_out"]) for c in cores]
    xnbf = [VB(r[c]["xn_bf"]) for c in cores]
    r = launch(build_l2b1, lambda c: prep_l2b1_inputs(xnbf[c], w))
    kall = np.concatenate([VB(r[c]["ksel"]) for c in cores], axis=0)
    vall = np.concatenate([VB(r[c]["vsel"]) for c in cores], axis=0)
    qsel = [VB(r[c]["qsel"]) for c in cores]
    selx = [VB(r[c]["selx"]) for c in cores]
    idxs = [np.asarray(r[c]["idx"]) for c in cores]
    selscs = [np.asarray(r[c]["selsc"]) for c in cores]
    r = launch(build_l2b2,
               lambda c: prep_l2b2_inputs(xnbf[c], xs[c], s_bf[c], qsel[c], kall,
                                          vall, selx[c], idxs[c], selscs[c], w,
                                          core_id=c))
    xs = [np.asarray(r[c]["xout"], dtype=np.float32) for c in cores]
    r = launch(build_l3, lambda c: prep_l3_inputs(xs[c], w))
    xs = [np.asarray(r[c]["xout"], dtype=np.float32) for c in cores]

    return np.concatenate([xs[c].reshape(2, T, D) for c in cores], axis=0)


def build_full(nc, layers=(0, 1, 2, 3, 4, 5)):
    io = {}
    innames = set()

    def dr(pfx, name, shape, dt=F32, kind="Internal"):
        full = pfx + name
        io[full] = nc.dram_tensor(full, shape, dt, kind=kind).ap()
        if kind == "ExternalInput":
            innames.add(full)
        return io[full]

    EI = "ExternalInput"
    for nm, sh, dt in [
            ("x", [TOK, D], F32), ("wqT", [D, 2048], F32), ("wkT", [D, 2048], F32),
            ("wvT", [D, 2048], F32), ("woT", [2048, D], F32), ("fwT", [D, 2048], F32),
            ("qnw_rep", [128, 2048], F32), ("knw_rep", [128, 2048], F32),
            ("cos_rep", [T, 1024], F32), ("sin_rep", [T, 1024], F32),
            ("lnw_rep", [128, D], F32), ("lnb_rep", [128, D], F32),
            ("gamma_rep", [128, D], F32), ("beta_rep", [128, D], F32),
            ("ident_bf", [128, 128], BF16), ("ident_f32", [128, 128], F32),
            ("ones1", [128, 1], BF16), ("ones_row1", [1, 128], BF16),
            ("ones_rowf", [1, 128], F32), ("masks_f32", [4 * 128, 512], F32)]:
        dr("l0_", nm, sh, dt, EI)
    for nm, sh, dt in [("xout", [TOK, D], F32), ("xn_bf", [TOK, D], BF16),
                       ("xn2_bf", [TOK, D], BF16), ("q_sc", [TOK, 2048], BF16),
                       ("k_sc", [TOK, 2048], BF16), ("v_sc", [TOK, 2048], BF16),
                       ("g_sc", [TOK, D], BF16)]:
        dr("l0_", nm, sh, dt)
    for nm, sh, dt in [
            ("wqT", [D, 2048], F32), ("wkT", [D, 2048], F32), ("wvT", [D, 2048], F32),
            ("woT", [2048, D], F32), ("wgT", [D, 5632], I32), ("wuT", [D, 5632], I32),
            ("wdT", [5632, D], I32), ("sgate_rep", [128, 5632], F32),
            ("sup_rep", [128, 5632], F32), ("sdown_rep", [128, 2048], F32),
            ("qnw_rep", [128, 2048], F32), ("knw_rep", [128, 2048], F32),
            ("ident_bf", [128, 128], BF16), ("ident_f32", [128, 128], F32),
            ("ones1", [128, 1], BF16), ("ones_row1", [1, 128], BF16),
            ("masks_bf", [4 * 128, 512], BF16), ("masks2_bf", [4 * 128, 512], BF16)]:
        dr("l1_", nm, sh, dt, EI)
    for nm, sh, dt in [("xout", [TOK, D], F32), ("xn_bf", [TOK, D], BF16),
                       ("q_sc", [TOK, 2048], BF16), ("k_sc", [TOK, 2048], BF16),
                       ("v_sc", [TOK, 2048], BF16), ("g_sc", [TOK, D], BF16),
                       ("gate_sc", [TOK, 5632], BF16), ("guT_sc", [TOK, 5632], BF16)]:
        dr("l1_", nm, sh, dt)
    io["l1_x"] = io["l0_xout"]
    for nm, sh, dt in [
            ("ipT_tiled", [64, 128, 2048], F32), ("convw_tiled", [128, 128], F32),
            ("xpT_tiled", [128, 32 * 256], F32), ("dtwT_h", [128, 4096], F32),
            ("dtb_h", [128, 32], F32), ("alog_h", [128, 512], F32),
            ("dp_h", [128, 32], F32), ("convb_h", [128, 32], F32),
            ("opT_tiled", [16, 128, 4096], F32), ("ohB", [32, 16 * 128], BF16),
            ("ohC", [32, 16 * 128], BF16), ("zeros4", [128, 4], BF16),
            ("ident_bf", [128, 128], BF16), ("ident_f32", [128, 128], F32)]:
        dr("l2a_", nm, sh, dt, EI)
    for nm, sh, dt in [("s_out", [2 * D, T], BF16), ("xn_bf", [TOK, D], BF16),
                       ("u_sc", [4096, T], BF16), ("gate_sc", [4096, T], BF16),
                       ("u2_sc", [4096, T], BF16)]:
        dr("l2a_", nm, sh, dt)
    io["l2a_x"] = io["l1_xout"]
    for nm, sh, dt in [
            ("wrT", [D, 512], F32), ("cos_tab", [T, 1024], F32),
            ("sin_tab", [T, 1024], F32), ("init_lo", [128, 2], F32),
            ("init_hi", [128, 2], F32), ("c_ones128", [128, 128], BF16),
            ("c_lt", [128, 128], BF16), ("c_onesrow", [128, 32], F32),
            ("c_onesrowf", [1, 128], F32), ("c_iota", [128, 256], F32),
            ("c_tvals", [128, 32], F32), ("ln1rep", [128, D], F32),
            ("bqrep", [128, 2048], F32), ("bkrep", [128, 1024], F32),
            ("bvrep", [128, 1024], F32), ("wq", [D, 2048], F32),
            ("wk", [D, 1024], F32), ("wv", [D, 1024], F32)]:
        dr("l2b1_", nm, sh, dt, EI)
    for nm, sh, dt in [("scores", [TOK, 16], F32), ("idx", [128, 4], U32),
                       ("posw", [128, 4], U32), ("selm", [128, 32], F32),
                       ("selx", [512, D], BF16), ("selsc", [128, 4], F32),
                       ("qsel", [512, 2048], BF16), ("cossel", [512, 1024], BF16),
                       ("sinsel", [512, 1024], BF16), ("hsel", [512, D], BF16)]:
        dr("l2b1_", nm, sh, dt)
    io["l2b1_xn_bf"] = io["l2a_xn_bf"]
    cc_in = nc.dram_tensor("cc_in", [512, 2048], BF16, kind="Internal").ap()
    cc_out = nc.dram_tensor("cc_out", [4096, 2048], BF16, kind="Internal").ap()
    io["l2b1_ksel"] = cc_in[:, 0:1024]
    io["l2b1_vsel"] = cc_in[:, 1024:2048]
    for nm, sh, dt in [
            ("selsc_dummy", [1, 1], F32)]:
        pass
    for nm, sh, dt in [
            ("amq", [512, 4096], BF16), ("amk", [4096, 512], BF16),
            ("wo", [2048, 2048], F32), ("wg", [2048, 8192], F32),
            ("wu", [2048, 8192], F32), ("wd", [8192, 2048], F32),
            ("ln2rep", [128, D], F32), ("ident_bf", [128, 128], BF16),
            ("ident_f32", [128, 128], F32), ("ones1", [128, 1], BF16),
            ("ones_row1", [1, 128], BF16)]:
        dr("l2b2_", nm, sh, dt, EI)
    for nm, sh, dt in [("xout", [TOK, D], F32), ("g_sc", [TOK, D], BF16),
                       ("attnout", [512, 2048], BF16), ("x1_sc", [512, D], F32),
                       ("x1b_sc", [512, D], BF16), ("h2_sc", [512, D], BF16),
                       ("gsel_sc", [512, 8192], BF16), ("gu_sc", [512, 8192], BF16),
                       ("proc_sc", [512, D], F32)]:
        dr("l2b2_", nm, sh, dt)
    io["l2b2_xn_bf"] = io["l2a_xn_bf"]
    io["l2b2_x"] = io["l1_xout"]
    io["l2b2_s_in"] = io["l2a_s_out"]
    io["l2b2_qsel"] = io["l2b1_qsel"]
    io["l2b2_kall"] = cc_out[:, 0:1024]
    io["l2b2_vall"] = cc_out[:, 1024:2048]
    io["l2b2_selx"] = io["l2b1_selx"]
    io["l2b2_idx"] = io["l2b1_idx"]
    io["l2b2_selsc"] = io["l2b1_selsc"]
    for nm, sh, dt in [
            ("wqT", [D, 2048], F32), ("wkT", [D, 512], F32), ("wvT", [D, 512], F32),
            ("woT", [2048, D], F32), ("awT", [D, 1024], F32), ("bwT", [512, D], F32),
            ("cos_repq", [T, 1024], F32), ("sin_repq", [T, 1024], F32),
            ("cos_repk", [T, 256], F32), ("sin_repk", [T, 256], F32),
            ("ident_bf", [128, 128], BF16), ("ident_f32", [128, 128], F32),
            ("ones1", [128, 1], BF16), ("ones_row1", [1, 128], BF16),
            ("masks_bf", [4 * 128, 512], BF16), ("masks2_bf", [4 * 128, 512], BF16)]:
        dr("l3_", nm, sh, dt, EI)
    for nm, sh, dt in [("xn_bf", [TOK, D], BF16), ("q_sc", [TOK, 2048], BF16),
                       ("k_sc", [TOK, 512], BF16), ("v_sc", [TOK, 512], BF16),
                       ("g_sc", [TOK, D], BF16), ("h_sc", [TOK, 1024], BF16),
                       ("hlg_sc", [TOK, 8], F32)]:
        dr("l3_", nm, sh, dt)
    io["l3_x"] = io["l2b2_xout"]
    io["l3_xout"] = nc.dram_tensor("xout", [TOK, D], F32, kind="ExternalOutput").ap()

    def sub_io(pfx):
        n = len(pfx)
        return {kk[n:]: v for kk, v in io.items() if kk.startswith(pfx)}

    es = contextlib.ExitStack()
    with es:
        block = es.enter_context(nc.Block())
        kroot = K(nc, block, es)
        for s_ in ["qs", "qg", "mt", "mv", "ms", "mg", "cc"]:
            kroot.sem(s_)

        if 0 in layers:
            build_l0(nc, k=K.sub(kroot, None, "l0_"), io=sub_io("l0_"))
            kroot.barrier()
        if 1 in layers:
            build_l1(nc, k=K.sub(kroot, None, "l1_"), io=sub_io("l1_"))
            kroot.barrier()
        if 2 in layers:
            build_l2_mamba_only(nc, k=K.sub(kroot, None, "l2a_"), io=sub_io("l2a_"))
            kroot.barrier()
        if 3 in layers:
            build_l2b1(nc, k=K.sub(kroot, None, "l2b1_"), io=sub_io("l2b1_"))
            kroot.barrier()

        if 4 in layers:
            @block.gpsimd
            def _(g):
                kroot.inc("cc", g.collective_compute(
                    "AllGather", ALU.bypass,
                    replica_groups=[list(range(NCORES))],
                    ins=[cc_in.opt()], outs=[cc_out.opt()]), 1)
            kroot.barrier()

            build_l2b2(nc, k=K.sub(kroot, None, "l2b2_"), io=sub_io("l2b2_"))
            kroot.barrier()
        if 5 in layers:
            build_l3(nc, k=K.sub(kroot, None, "l3_"), io=sub_io("l3_"))
        if 5 not in layers:
            src = io["l0_x"]

            @block.sync
            def _(sy):
                for m_ in range(NTT):
                    kroot.dma(sy, io["l3_xout"][m_ * 128:(m_ + 1) * 128, :],
                              src[m_ * 128:(m_ + 1) * 128, :], "qs")
            kroot.barrier()
    return nc, innames


def prep_full_inputs(xc, w, core_id):
    zf = np.zeros((TOK, D), np.float32)
    zb = zf.astype(bf)
    out = {}
    for pfx, d in [
            ("l0_", prep_l0_inputs(xc, w)),
            ("l1_", prep_l1_inputs(zf, w)),
            ("l2a_", prep_l2_mamba(zf, w)),
            ("l2b1_", prep_l2b1_inputs(zb, w)),
            ("l2b2_", prep_l2b2_inputs(zb, zf, np.zeros((2 * D, T), bf),
                                       np.zeros((512, 2048), bf),
                                       np.zeros((4096, 1024), bf),
                                       np.zeros((4096, 1024), bf),
                                       np.zeros((512, D), bf),
                                       np.zeros((128, 4), np.uint32),
                                       np.zeros((128, 4), np.float32), w, core_id)),
            ("l3_", prep_l3_inputs(zf, w))]:
        for kk, v in d.items():
            out[pfx + kk] = v
    return out


def kernel(**inputs):
    from concourse.bass_utils import run_bass_kernel_spmd

    w = {n: np.asarray(v) for n, v in inputs.items()}
    x = np.ascontiguousarray(np.asarray(w["L0_attn_x"], dtype=np.float32))
    cores = list(range(8))
    nc, innames = build_full(bass.Bass("TRN2", target_bir_lowering=False))
    ims = []
    for c in cores:
        xc = np.ascontiguousarray(x[2 * c:2 * c + 2].reshape(TOK, D))
        im = prep_full_inputs(xc, w, c)
        ims.append({kk: v for kk, v in im.items() if kk in innames})
    res = run_bass_kernel_spmd(nc, ims, core_ids=cores).results
    return np.concatenate(
        [np.asarray(res[c]["xout"], dtype=np.float32).reshape(2, T, D)
         for c in cores], axis=0)


# version 07
# speedup vs baseline: 1.3606x; 1.0461x over previous
import contextlib
import numpy as np
import ml_dtypes
import concourse.bass as bass
import concourse.mybir as mybir
import bass_rust
from concourse.bass_utils import run_bass_kernel_spmd

F32 = mybir.dt.float32
BF16 = mybir.dt.bfloat16
I32 = mybir.dt.int32
U32 = mybir.dt.uint32
AF = mybir.ActivationFunctionType
ALU = mybir.AluOpType
AXL = mybir.AxisListType

NCORES = 8
BLOC = 2
T = 2048
D = 2048
TOK = BLOC * T
NTT = TOK // 128
NKD = D // 128

bf = ml_dtypes.bfloat16


def perm_deint(n=128):
    return np.concatenate([np.arange(0, n, 2), np.arange(1, n, 2)])


class K:

    def __init__(self, nc, block, es, prefix=""):
        self.nc = nc
        self.block = block
        self.es = es
        self.sems = {}
        self.cnt = {}
        self.prefix = prefix

    @staticmethod
    def sub(parent, es, prefix=None):
        kk = K.__new__(K)
        kk.nc, kk.block, kk.es = parent.nc, parent.block, es
        kk.sems, kk.cnt = parent.sems, parent.cnt
        kk.prefix = parent.prefix if prefix is None else prefix
        return kk

    def sem(self, name):
        if name not in self.sems:
            self.sems[name] = self.es.enter_context(self.nc.semaphore(name))
            self.cnt[name] = 0
        return self.sems[name]

    def inc(self, name, inst, n=1):
        inst.then_inc(self.sem(name), n)
        self.cnt[name] += n
        return self.cnt[name]

    def dma(self, eng, out, in_, name):
        return self.inc(name, eng.dma_start(out=out, in_=in_), 16)

    def sb(self, name, shape, dtype):
        return self.es.enter_context(
            self.nc.sbuf_tensor(self.prefix + name, shape, dtype))

    def ps(self, name, shape, dtype=F32):
        return self.es.enter_context(
            self.nc.psum_tensor(self.prefix + name, shape, dtype))

    def barrier(self):
        block = self.block
        snap = [(n, self.cnt[n]) for n in self.sems if self.cnt[n] > 0]

        @block.tensor
        def _(t):
            for n, c in snap:
                t.wait_ge(self.sems[n], c)

        @block.vector
        def _(v):
            for n, c in snap:
                v.wait_ge(self.sems[n], c)

        @block.scalar
        def _(s):
            for n, c in snap:
                s.wait_ge(self.sems[n], c)

        @block.gpsimd
        def _(g):
            for n, c in snap:
                g.wait_ge(self.sems[n], c)

        @block.sync
        def _(sy):
            for n, c in snap:
                sy.wait_ge(self.sems[n], c)


def phase_ln(k, x_hbm, ident=None, xnT_sb=None, xn_hbm=None, rowsum_sb=None,
             eps=1e-5, tag="ln", in_dtype=F32, lnw_rep=None, lnb_rep=None,
             out_bf_hbm=None):
    nc, block = k.nc, k.block
    x_t = x_hbm.rearrange("(i p) d -> i p d", p=128)
    xs = [k.sb(f"{tag}_xs{j}", [128, D], in_dtype) for j in range(2)]
    xnb = [k.sb(f"{tag}_xnb{j}", [128, D], BF16) for j in range(2)]
    xnm = k.sb(f"{tag}_xnm", [128, D], F32) if lnw_rep is not None else None
    xnf = [k.sb(f"{tag}_xnf{j}", [128, D], F32) for j in range(2)] if xn_hbm is not None else None
    sq = k.sb(f"{tag}_sq", [128, D], F32)
    mneg = k.sb(f"{tag}_mneg", [128, 1], F32)
    mneg2 = k.sb(f"{tag}_mneg2", [128, 1], F32)
    red_s = k.sb(f"{tag}_reds", [128, 1], F32)
    ssq = k.sb(f"{tag}_ssq", [128, 1], F32)
    veps = k.sb(f"{tag}_veps", [128, 1], F32)
    lncol = k.sb(f"{tag}_lncol", [128, 1], F32)
    lnh = k.sb(f"{tag}_lnh", [128, 1], F32)
    rstd = k.sb(f"{tag}_rstd", [128, 1], F32)
    if xnT_sb is not None:
        tpA = k.ps(f"{tag}_tpA", [128, 1024], BF16)
        tpB = k.ps(f"{tag}_tpB", [128, 1024], BF16)

    hist = {}
    for i in range(NTT):
        buf = i % 2

        @block.sync
        def _(sy, i=i, buf=buf):
            if i >= 2:
                sy.wait_ge(k.sem("mv"), hist[("apply", i - 2)])
            k.dma(sy, xs[buf][:], x_t[i], "qs")
        ld_done = k.cnt["qs"]

        @block.scalar
        def _(s, i=i, buf=buf):
            s.wait_ge(k.sem("qs"), ld_done)
            k.inc("ms", s.activation(sq[:], xs[buf][:], AF.Copy, accum_out=red_s[:, 0:1]))
        red_done = k.cnt["ms"]

        @block.vector
        def _(v, i=i, buf=buf):
            v.wait_ge(k.sem("ms"), red_done)
            k.inc("mv", v.tensor_scalar(mneg[:, 0:1], red_s[:, 0:1], -1.0 / D, None, ALU.mult))
        mneg_done = k.cnt["mv"]

        @block.scalar
        def _(s, i=i, buf=buf):
            s.wait_ge(k.sem("mv"), mneg_done)
            k.inc("ms", s.activation(sq[:], xs[buf][:], AF.Square, bias=mneg[:, 0:1],
                                     accum_out=ssq[:, 0:1]))
        ssq_done = k.cnt["ms"]

        @block.vector
        def _(v, i=i, buf=buf):
            v.wait_ge(k.sem("ms"), ssq_done)
            k.inc("mv", v.tensor_scalar(veps[:, 0:1], ssq[:, 0:1], 1.0 / D, eps, ALU.mult, ALU.add))
        veps_done = k.cnt["mv"]

        @block.scalar
        def _(s, i=i, buf=buf):
            s.wait_ge(k.sem("mv"), veps_done)
            k.inc("ms", s.activation(lncol[:, 0:1], veps[:, 0:1], AF.Ln))
        ln_done = k.cnt["ms"]

        @block.vector
        def _(v, i=i, buf=buf):
            v.wait_ge(k.sem("ms"), ln_done)
            k.inc("mv", v.tensor_scalar(lnh[:, 0:1], lncol[:, 0:1], -0.5, None, ALU.mult))
        lnh_done = k.cnt["mv"]

        @block.scalar
        def _(s, i=i, buf=buf):
            s.wait_ge(k.sem("mv"), lnh_done)
            s.activation(rstd[:, 0:1], lnh[:, 0:1], AF.Exp)
            k.inc("ms", s.copy(mneg2[:, 0:1], mneg[:, 0:1]))
        rstd_done = k.cnt["ms"]

        @block.vector
        def _(v, i=i, buf=buf):
            v.wait_ge(k.sem("ms"), rstd_done)
            if xnT_sb is not None and i >= 2:
                v.wait_ge(k.sem("mt"), hist[("tp", i - 2)])
            if xnf is not None and i >= 2:
                v.wait_ge(k.sem("qg"), hist[("xnf", i - 2)])
            if out_bf_hbm is not None and i >= 2:
                v.wait_ge(k.sem("qs"), hist[("xnbst", i - 2)])
            if lnw_rep is not None:
                v.tensor_scalar(xnm[:], xs[buf][:], mneg2[:, 0:1], rstd[:, 0:1],
                                ALU.add, ALU.mult)
                v.tensor_tensor(xnm[:], xnm[:], lnw_rep[:], ALU.mult)
                k.inc("mv", v.tensor_tensor(xnb[buf][:], xnm[:], lnb_rep[:], ALU.add))
            else:
                k.inc("mv", v.tensor_scalar(xnb[buf][:], xs[buf][:], mneg2[:, 0:1], rstd[:, 0:1],
                                            ALU.add, ALU.mult))
            if xnf is not None:
                k.inc("mv", v.tensor_scalar(xnf[buf][:], xs[buf][:], mneg2[:, 0:1], rstd[:, 0:1],
                                            ALU.add, ALU.mult))
            if rowsum_sb is not None:
                k.inc("mv", v.tensor_scalar(sq[:], xnb[buf][:], 1.0, None, ALU.mult,
                                            ALU.add, accum_out=rowsum_sb[:, i:i + 1]))
        hist[("apply", i)] = k.cnt["mv"]
        xn_done = k.cnt["mv"]

        if xnf is not None:
            @block.gpsimd
            def _(g, i=i, buf=buf):
                g.wait_ge(k.sem("mv"), xn_done)
                k.dma(g, xn_hbm.rearrange("(i p) d -> i p d", p=128)[i], xnf[buf][:], "qg")
            hist[("xnf", i)] = k.cnt["qg"]

        if out_bf_hbm is not None:
            @block.sync
            def _(sy, i=i, buf=buf):
                sy.wait_ge(k.sem("mv"), xn_done)
                k.dma(sy, out_bf_hbm.rearrange("(i p) d -> i p d", p=128)[i], xnb[buf][:], "qs")
            hist[("xnbst", i)] = k.cnt["qs"]

        if xnT_sb is not None:
            @block.tensor
            def _(t, i=i, buf=buf):
                t.wait_ge(k.sem("mv"), xn_done)
                if i >= 1:
                    t.wait_ge(k.sem("ms"), hist[("tpcopy", i - 1)])
                for j in range(8):
                    mm = t.transpose(tpA[:, j * 128:(j + 1) * 128],
                                     xnb[buf][:, j * 128:(j + 1) * 128], ident[:])
                k.inc("mt", mm)
                for j in range(8, NKD):
                    mm = t.transpose(tpB[:, (j - 8) * 128:(j - 8) * 128 + 128],
                                     xnb[buf][:, j * 128:(j + 1) * 128], ident[:])
                k.inc("mt", mm)
            hist[("tp", i)] = k.cnt["mt"]

            @block.scalar
            def _(s, i=i, buf=buf):
                s.wait_ge(k.sem("mt"), hist[("tp", i)] - 1)
                for j in range(8):
                    s.copy(xnT_sb[:, j * TOK + i * 128: j * TOK + (i + 1) * 128],
                           tpA[:, j * 128:(j + 1) * 128])
                s.wait_ge(k.sem("mt"), hist[("tp", i)])
                for j in range(8, NKD):
                    inst = s.copy(xnT_sb[:, j * TOK + i * 128: j * TOK + (i + 1) * 128],
                                  tpB[:, (j - 8) * 128:(j - 8) * 128 + 128])
                k.inc("ms", inst)
            hist[("tpcopy", i)] = k.cnt["ms"]

    return hist


def phase_proj(k, xn_hbm, w_hbm, W, evac_cb, tag, evac_sem="ms", wprep_cb=None,
               KSTR=NKD, HT=T, PASSW=2048, nbuf=None, NTOK=TOK):
    nc, block = k.nc, k.block
    kdim = KSTR * 128
    npanels = NTOK // HT
    MT = HT // 128
    npass = (W + PASSW - 1) // PASSW
    if nbuf is None:
        nbuf = 2 if npass > 1 else 1
    NCH_MAX = PASSW // 512
    wsb = [k.sb(f"{tag}_w{j}", [128, KSTR * PASSW], BF16) for j in range(nbuf)]
    xnT = k.sb(f"{tag}_xnT", [128, KSTR * HT], BF16)
    pss = [k.ps(f"{tag}_ps{j}", [128, 512]) for j in range(8)]
    hist = {}

    for panel in range(npanels):
        @block.sync
        def _(sy, panel=panel):
            if panel >= 1:
                sy.wait_ge(k.sem("mt"), hist[("panelmm", panel - 1)])
            for j in range(KSTR):
                k.inc("qs", sy.dma_start_transpose(
                    xnT[:, j * HT:(j + 1) * HT],
                    xn_hbm[panel * HT:(panel + 1) * HT, j * 128:(j + 1) * 128]), 16)
        xn_ld = k.cnt["qs"]

        for p in range(npass):
            pw = min(PASSW, W - p * PASSW)
            nch = pw // 512
            pbuf = p % nbuf

            if panel == 0 or npass > 1:
                @block.gpsimd
                def _(g, p=p, pbuf=pbuf, pw=pw, panel=panel):
                    if ("wmm", pbuf) in hist:
                        g.wait_ge(k.sem("mt"), hist[("wmm", pbuf)])
                    for kk in range(KSTR):
                        k.dma(g, wsb[pbuf][:, kk * PASSW: kk * PASSW + pw],
                              w_hbm[kk * 128:(kk + 1) * 128, p * PASSW: p * PASSW + pw], "qg")
                wld = k.cnt["qg"]
                wready = ("qg", wld)
                if wprep_cb is not None:
                    wready = wprep_cb(k, p, pbuf, wsb[pbuf], pw, wld)
                hist[("wready", p)] = wready
            wready = hist[("wready", p)]

            for m in range(MT):
                gi = panel * MT + m
                grp = m % 2
                ngrp = min(4, nch)

                @block.tensor
                def _(t, p=p, m=m, gi=gi, grp=grp, pbuf=pbuf, nch=nch, wready=wready,
                      xn_ld=xn_ld, panel=panel, ngrp=ngrp):
                    if m == 0:
                        t.wait_ge(k.sem(wready[0]), wready[1])
                        t.wait_ge(k.sem("qs"), xn_ld)
                    gidx = (panel * npass + p) * MT + m
                    if gidx >= 2:
                        t.wait_ge(k.sem(evac_sem), hist[("evac", gidx - 2)])
                    for kk in range(KSTR):
                        for n in range(nch):
                            mm = t.matmul(
                                pss[grp * 4 + n][:],
                                xnT[:, kk * HT + m * 128: kk * HT + (m + 1) * 128],
                                wsb[pbuf][:, kk * PASSW + n * 512: kk * PASSW + (n + 1) * 512],
                                start=(kk == 0), stop=(kk == KSTR - 1))
                    k.inc("mt", mm)
                mm_cnt = k.cnt["mt"]
                hist[("wmm", pbuf)] = k.cnt["mt"]

                group_pss = [pss[grp * 4 + n] for n in range(nch)]
                for n in range(nch):
                    evac_cb(k, gi, p * NCH_MAX + n, pss[grp * 4 + n], mm_cnt, group_pss)
                hist[("evac", (panel * npass + p) * MT + m)] = k.cnt[evac_sem]
        hist[("panelmm", panel)] = k.cnt["mt"]
    return hist


def make_qk_evac(k, wrep, cos_hbm, sin_hbm, out_hbm, eps, tag, rope=True, do_rms=True,
                 nh_total=16, brep=None):
    nc, block = k.nc, k.block
    csb = [k.sb(tag + f"_cs{j}", [128, 1024], BF16) for j in range(2)] if rope else None
    ssb = [k.sb(tag + f"_sn{j}", [128, 1024], BF16) for j in range(2)] if rope else None
    tbl_hist = {}
    ssqb = k.sb(tag + "_ssqb", [128, 16], F32)
    sqd = k.sb(tag + "_sqd", [128, 128], F32)
    veps = k.sb(tag + "_veps", [128, 16], F32)
    lnc = k.sb(tag + "_lnc", [128, 16], F32)
    lnh = k.sb(tag + "_lnh", [128, 16], F32)
    rstd = k.sb(tag + "_rstd", [128, 16], F32)
    qn = k.sb(tag + "_qn", [128, D], F32)
    qw = k.sb(tag + "_qw", [128, D], BF16)
    t1 = k.sb(tag + "_t1", [128, 1024], F32)
    t2 = k.sb(tag + "_t2", [128, 1024], F32)
    qrot = [k.sb(tag + f"_qrot{j}", [128, D], BF16) for j in range(2)]
    out_t = out_hbm.rearrange("(i p) d -> i p d", p=128)
    st_hist = {}

    def evac(k, m, ncg, ps, mm_cnt, group_pss):
        nc4 = ncg % 4
        if rope and nc4 == 0:
            trow = m % 16

            @block.gpsimd
            def _(g):
                if m >= 2 and (m - 2) in tbl_hist:
                    g.wait_ge(k.sem("mv"), tbl_hist[m - 2])
                k.dma(g, csb[m % 2][:, 0:nh_total * 64],
                      cos_hbm[trow * 128:(trow + 1) * 128, 0:nh_total * 64], "qg")
                k.dma(g, ssb[m % 2][:, 0:nh_total * 64],
                      sin_hbm[trow * 128:(trow + 1) * 128, 0:nh_total * 64], "qg")
            tbl_hist[("ld", m)] = k.cnt["qg"]

        if do_rms:
            @block.scalar
            def _(s):
                s.wait_ge(k.sem("mt"), mm_cnt)
                for hh in range(4):
                    inst = s.activation(sqd[:], ps[:, hh * 128:(hh + 1) * 128], AF.Square,
                                        accum_out=ssqb[:, nc4 * 4 + hh: nc4 * 4 + hh + 1])
                k.inc("ms", inst)
        sq_done = k.cnt["ms"]

        if nc4 != (3 if len(group_pss) == 4 else len(group_pss) - 1):
            return

        if do_rms:
            @block.vector
            def _(v):
                v.wait_ge(k.sem("ms"), sq_done)
                k.inc("mv", v.tensor_scalar(veps[:], ssqb[:], 1.0 / 128, eps, ALU.mult, ALU.add))
            c1 = k.cnt["mv"]

            @block.scalar
            def _(s):
                s.wait_ge(k.sem("mv"), c1)
                k.inc("ms", s.activation(lnc[:], veps[:], AF.Ln))
            c2 = k.cnt["ms"]

            @block.vector
            def _(v):
                v.wait_ge(k.sem("ms"), c2)
                k.inc("mv", v.tensor_scalar(lnh[:], lnc[:], -0.5, None, ALU.mult))
            c3 = k.cnt["mv"]

            @block.scalar
            def _(s):
                s.wait_ge(k.sem("mv"), c3)
                k.inc("ms", s.activation(rstd[:], lnh[:], AF.Exp))
        c4 = k.cnt["ms"]

        buf = m % 2
        trow = m % 16

        nheads = len(group_pss) * 4

        @block.vector
        def _(v):
            v.wait_ge(k.sem("ms"), c4)
            v.wait_ge(k.sem("mt"), mm_cnt)
            if m >= 2:
                v.wait_ge(k.sem("qs"), st_hist[m - 2])
            for h in range(nheads):
                if do_rms:
                    v.tensor_scalar(qn[:, h * 128:(h + 1) * 128],
                                    group_pss[h // 4][:, (h % 4) * 128:(h % 4) * 128 + 128],
                                    rstd[:, h:h + 1], None, ALU.mult)
                else:
                    v.tensor_scalar(qn[:, h * 128:(h + 1) * 128],
                                    group_pss[h // 4][:, (h % 4) * 128:(h % 4) * 128 + 128],
                                    1.0, None, ALU.mult)
            if brep is not None:
                last = v.tensor_tensor(qw[:, 0:nheads * 128], qn[:, 0:nheads * 128],
                                       brep[:, 0:nheads * 128], ALU.add)
            elif wrep is not None:
                last = v.tensor_tensor(qw[:, 0:nheads * 128], qn[:, 0:nheads * 128],
                                       wrep[:, 0:nheads * 128], ALU.mult)
            else:
                last = v.tensor_copy(qw[:, 0:nheads * 128], qn[:, 0:nheads * 128])
            if rope:
                v.wait_ge(k.sem("qg"), tbl_hist[("ld", m)])
                nh = nheads
                qwv = qw.ap()[:, 0:nh * 128].rearrange("p (h x) -> p h x", h=nh)
                qrv = qrot[buf].ap()[:, 0:nh * 128].rearrange("p (h x) -> p h x", h=nh)
                x1, x2 = qwv[:, :, 0:64], qwv[:, :, 64:128]
                cs = csb[m % 2].ap()[:, 0:nh * 64].rearrange("p (h x) -> p h x", h=nh)
                sn = ssb[m % 2].ap()[:, 0:nh * 64].rearrange("p (h x) -> p h x", h=nh)
                t1v = t1.ap()[:, 0:nh * 64].rearrange("p (h x) -> p h x", h=nh)
                t2v = t2.ap()[:, 0:nh * 64].rearrange("p (h x) -> p h x", h=nh)
                v.tensor_tensor(t1v, x1, cs, ALU.mult)
                v.tensor_tensor(t2v, x2, sn, ALU.mult)
                v.tensor_tensor(qrv[:, :, 0:64], t1v, t2v, ALU.subtract)
                v.tensor_tensor(t1v, x1, sn, ALU.mult)
                v.tensor_tensor(t2v, x2, cs, ALU.mult)
                last = v.tensor_tensor(qrv[:, :, 64:128], t1v, t2v, ALU.add)
            else:
                last = v.tensor_copy(qrot[buf][:, 0:nheads * 128], qw[:, 0:nheads * 128])
            k.inc("mv", last)
        c5 = k.cnt["mv"]
        tbl_hist[m] = c5

        @block.sync
        def _(sy):
            sy.wait_ge(k.sem("mv"), c5)
            k.dma(sy, out_t[m][:, 0:nheads * 128], qrot[buf][:, 0:nheads * 128], "qs")
        st_hist[m] = k.cnt["qs"]

    return evac


def make_v_evac(k, out_hbm, tag, nch_total=4):
    nc, block = k.nc, k.block
    vst = [k.sb(tag + f"_vst{j}", [128, D], BF16) for j in range(2)]
    out_t = out_hbm.rearrange("(i p) d -> i p d", p=128)
    st_hist = {}

    def evac(k, m, ncg, ps, mm_cnt, group_pss):
        nc4 = ncg % 4
        buf = m % 2

        @block.scalar
        def _(s):
            s.wait_ge(k.sem("mt"), mm_cnt)
            if nc4 == 0 and m >= 2:
                s.wait_ge(k.sem("qs"), st_hist[m - 2])
            k.inc("ms", s.copy(vst[buf][:, nc4 * 512:(nc4 + 1) * 512], ps[:]))
        cp_done = k.cnt["ms"]

        if nc4 == nch_total - 1:
            @block.sync
            def _(sy):
                sy.wait_ge(k.sem("ms"), cp_done)
                k.dma(sy, out_t[m][:, 0:512 * nch_total], vst[buf][:, 0:512 * nch_total], "qs")
            st_hist[m] = k.cnt["qs"]

    return evac


def phase_attn_oproj(k, cfg):
    nc, block = k.nc, k.block
    nh = 16
    kv_rep = cfg.get("kv_rep", 1)
    KW = (nh // kv_rep) * 128
    softcap = cfg.get("softcap", None)
    scale = cfg["scale"]
    ident_bf = cfg["ident_bf"]
    ident_f32 = cfg["ident_f32"]
    ones1 = cfg["ones1"]
    ones_row1 = cfg["ones_row1"]
    q_hbm, k_hbm, v_hbm = cfg["q_hbm"], cfg["k_hbm"], cfg["v_hbm"]
    v_t = v_hbm.rearrange("(i p) d -> i p d", p=128)

    oes = contextlib.ExitStack()
    ko_ = K.__new__(K)
    ko_.nc, ko_.block, ko_.es = nc, block, oes
    ko_.sems, ko_.cnt = k.sems, k.cnt
    ko_.prefix = k.prefix
    ob_sb = ko_.sb("atob", [128, 16 * T], BF16)
    nmask = cfg["masks_hbm"].shape[0] // 128
    masks = [ko_.sb(f"atmk{j}", [128, 512], cfg["mask_dtype"]) for j in range(nmask)]
    masks2 = None
    if cfg.get("masks2_hbm") is not None:
        masks2 = [ko_.sb(f"atmk2{j}", [128, 512], BF16) for j in range(4)]

    @block.sync
    def _(sy):
        for j in range(nmask):
            ko_.dma(sy, masks[j][:], cfg["masks_hbm"][j * 128:(j + 1) * 128, :], "qs")
        if masks2 is not None:
            for j in range(4):
                ko_.dma(sy, masks2[j][:], cfg["masks2_hbm"][j * 128:(j + 1) * 128, :], "qs")

    for b in range(BLOC):
        k.barrier()
        with contextlib.ExitStack() as pes:
            ka = K.__new__(K)
            ka.nc, ka.block, ka.es = nc, block, pes
            ka.sems, ka.cnt = k.sems, k.cnt
            ka.prefix = k.prefix

            v_sb = ka.sb(f"at{b}_v", [128, 16 * KW], BF16)
            qt_sb = [ka.sb(f"at{b}_qt{j}", [128, T], BF16) for j in range(2)]
            kt_sb = [ka.sb(f"at{b}_kt{j}", [128, T], BF16) for j in range(2)]
            p_sb = [ka.sb(f"at{b}_p{j}", [128, 512], BF16) for j in range(2)]
            t_sb = ka.sb(f"at{b}_t", [128, 512], F32)
            t2_sb = ka.sb(f"at{b}_t2", [128, 512], F32)
            drow = ka.sb(f"at{b}_drow", [128, 512], F32)
            negm = ka.sb(f"at{b}_negm", [128, 512], BF16)
            mcol = ka.sb(f"at{b}_mcol", [128, 4], F32)
            lnd = ka.sb(f"at{b}_lnd", [128, 4], F32)
            mbig = ka.sb(f"at{b}_mbig", [128, 4], F32)
            mbig2 = ka.sb(f"at{b}_mbig2", [128, 4], F32)
            dhat = ka.sb(f"at{b}_dhat", [128, 4], F32)
            on_sb = ka.sb(f"at{b}_on", [128, 512], BF16)
            sdump = ka.sb(f"at{b}_sdump", [128, 512], F32)
            ps_s = [ka.ps(f"at{b}_pss{j}", [128, 512]) for j in range(2)]
            ps_o = [ka.ps(f"at{b}_pso{j}", [128, 512]) for j in range(4)]
            ps_d = ka.ps(f"at{b}_psd", [128, 512])
            ps_tp = ka.ps(f"at{b}_pstp", [128, 1024], BF16)
            ps_sp = ps_o

            @block.sync
            def _(sy):
                for m in range(16):
                    ka.dma(sy, v_sb[:, m * KW:(m + 1) * KW],
                           v_t[b * 16 + m][:, 0:KW], "qs")
            vld = k.cnt["qs"]

            hist = {}
            for h in range(nh):
                hb = h % 2
                kvh = h // kv_rep

                @block.sync
                def _(sy, h=h, hb=hb, kvh=kvh):
                    if h >= 2:
                        sy.wait_ge(k.sem("mt"), hist[("hmm", h - 2)])
                    k.inc("qs", sy.dma_start_transpose(
                        qt_sb[hb][:], q_hbm[b * T:(b + 1) * T, h * 128:(h + 1) * 128]), 16)
                    k.inc("qs", sy.dma_start_transpose(
                        kt_sb[hb][:], k_hbm[b * T:(b + 1) * T, kvh * 128:(kvh + 1) * 128]), 16)
                qk_ld = k.cnt["qs"]

                for qc in range(4):
                    ktop = 4 * (qc + 1)

                    if softcap is None:
                        for qsub in range(4):
                            for kc in range(qc + 1):
                                pb = (qsub * (qc + 1) + kc) % 4

                                @block.tensor
                                def _(t, h=h, hb=hb, qc=qc, qsub=qsub, kc=kc, pb=pb):
                                    if h == 0 and qc == 0 and qsub == 0 and kc == 0:
                                        t.wait_ge(k.sem("qs"), vld)
                                    t.wait_ge(k.sem("qs"), qk_ld)
                                    if qsub == 0 and kc == 0 and ("ofree",) in hist:
                                        t.wait_ge(k.sem("mv"), hist[("ofree",)])
                                    if ("spfree", pb) in hist:
                                        t.wait_ge(k.sem("mv"), hist[("spfree", pb)])
                                    diag = (kc == qc)
                                    mm = t.matmul(ps_sp[pb][:, 0:512],
                                                  qt_sb[hb][:, (qc * 4 + qsub) * 128:(qc * 4 + qsub + 1) * 128],
                                                  kt_sb[hb][:, kc * 512:(kc + 1) * 512],
                                                  start=True, stop=not diag, skip_group_check=True)
                                    if diag:
                                        mm = t.matmul(ps_sp[pb][:, 0:512], ident_bf[:], masks2[qsub][:],
                                                      start=False, stop=True, skip_group_check=True)
                                    k.inc("mt", mm)
                                sp_done = k.cnt["mt"]

                                @block.vector
                                def _(v, h=h, qsub=qsub, kc=kc, pb=pb):
                                    v.wait_ge(k.sem("mt"), sp_done)
                                    k.inc("mv", v.tensor_reduce(
                                        mbig[:, kc:kc + 1], ps_sp[pb][:, 0:512], AXL.X, ALU.max))
                                hist[("spfree", pb)] = k.cnt["mv"]
                            mb_done = k.cnt["mv"]

                            @block.scalar
                            def _(s, qc=qc, qsub=qsub):
                                s.wait_ge(k.sem("mv"), mb_done)
                                k.inc("ms", s.copy(mbig2[:, 0:qc + 1], mbig[:, 0:qc + 1]))
                            mb2_done = k.cnt["ms"]

                            @block.vector
                            def _(v, qc=qc, qsub=qsub):
                                v.wait_ge(k.sem("ms"), mb2_done)
                                k.inc("mv", v.tensor_reduce(
                                    mcol[:, qsub:qsub + 1], mbig2[:, 0:qc + 1],
                                    AXL.X, ALU.max))
                        mcol_done = k.cnt["mv"]

                        @block.tensor
                        def _(t, qc=qc):
                            t.wait_ge(k.sem("mv"), mcol_done)
                            if ("dtfree",) in hist:
                                t.wait_ge(k.sem("ms"), hist[("dtfree",)])
                            if ("dtfree2",) in hist:
                                t.wait_ge(k.sem("ms"), hist[("dtfree2",)])
                            for j in range(4):
                                mm = t.transpose(ps_d[0:1, j * 128:(j + 1) * 128],
                                                 mcol[:, j:j + 1], ident_f32[:])
                            k.inc("mt", mm)
                        mt_done = k.cnt["mt"]

                        @block.scalar
                        def _(s, qc=qc):
                            s.wait_ge(k.sem("mt"), mt_done)
                            k.inc("ms", s.activation(negm[0:1, 0:512], ps_d[0:1, 0:512],
                                                     AF.Copy, scale=-1.0))
                        hist[("dtfree",)] = k.cnt["ms"]
                        negm_done = k.cnt["ms"]

                    for kt2 in range(0, ktop, 2):
                        dpair = qc * 4 <= kt2

                        @block.tensor
                        def _(t, h=h, hb=hb, qc=qc, kt2=kt2):
                            if softcap is not None and h == 0 and qc == 0 and kt2 == 0:
                                t.wait_ge(k.sem("qs"), vld)
                            t.wait_ge(k.sem("qs"), qk_ld)
                            if ("sfree",) in hist:
                                t.wait_ge(k.sem("ms"), hist[("sfree",)])
                            if softcap is None:
                                t.wait_ge(k.sem("ms"), negm_done)
                            for j in range(2):
                                kt = kt2 + j
                                diag = qc * 4 <= kt
                                if softcap is not None:
                                    mm = t.matmul(ps_s[j][:],
                                                  kt_sb[hb][:, kt * 128:(kt + 1) * 128],
                                                  qt_sb[hb][:, qc * 512:(qc + 1) * 512],
                                                  start=True, stop=True,
                                                  skip_group_check=True)
                                else:
                                    mm = t.matmul(ps_s[j][:],
                                                  kt_sb[hb][:, kt * 128:(kt + 1) * 128],
                                                  qt_sb[hb][:, qc * 512:(qc + 1) * 512],
                                                  start=True, stop=False,
                                                  skip_group_check=True)
                                    if diag:
                                        mm = t.matmul(ps_s[j][:], ident_bf[:],
                                                      masks[kt - qc * 4][:],
                                                      start=False, stop=False,
                                                      skip_group_check=True)
                                    mm = t.matmul(ps_s[j][:], ones_row1[0:1, :],
                                                  negm[0:1, 0:512],
                                                  start=False, stop=True,
                                                  skip_group_check=True)
                            k.inc("mt", mm)
                        sbar_done = k.cnt["mt"]

                        if softcap is not None and dpair:
                            for j in range(2):
                                kt = kt2 + j

                                @block.scalar
                                def _(s, j=j, sbar_done=sbar_done):
                                    s.wait_ge(k.sem("mt"), sbar_done)
                                    if j == 0 and ("pfree",) in hist:
                                        s.wait_ge(k.sem("mt"), hist[("pfree",)])
                                    k.inc("ms", s.activation(t_sb[:], ps_s[j][:], AF.Tanh,
                                                             scale=1.0 / (softcap / scale)))
                                tanh_done = k.cnt["ms"]

                                @block.vector
                                def _(v, qc=qc, kt=kt, tanh_done=tanh_done):
                                    v.wait_ge(k.sem("ms"), tanh_done)
                                    k.inc("mv", v.tensor_tensor(t2_sb[:], t_sb[:],
                                                                masks[kt - qc * 4][:],
                                                                ALU.add))
                                madd_done = k.cnt["mv"]

                                @block.scalar
                                def _(s, j=j, madd_done=madd_done):
                                    s.wait_ge(k.sem("mv"), madd_done)
                                    k.inc("ms", s.activation(p_sb[j][:], t2_sb[:], AF.Exp,
                                                             scale=softcap))
                        elif softcap is not None:
                            @block.scalar
                            def _(s, sbar_done=sbar_done):
                                s.wait_ge(k.sem("mt"), sbar_done)
                                if ("pfree",) in hist:
                                    s.wait_ge(k.sem("mt"), hist[("pfree",)])
                                s.activation(t_sb[:], ps_s[0][:], AF.Tanh,
                                             scale=1.0 / (softcap / scale))
                                s.activation(p_sb[0][:], t_sb[:], AF.Exp, scale=softcap)
                                s.activation(t2_sb[:], ps_s[1][:], AF.Tanh,
                                             scale=1.0 / (softcap / scale))
                                k.inc("ms", s.activation(p_sb[1][:], t2_sb[:], AF.Exp,
                                                         scale=softcap))
                        else:
                            @block.scalar
                            def _(s, sbar_done=sbar_done):
                                s.wait_ge(k.sem("mt"), sbar_done)
                                if ("pfree",) in hist:
                                    s.wait_ge(k.sem("mt"), hist[("pfree",)])
                                s.activation(p_sb[0][:], ps_s[0][:], AF.Exp, scale=scale)
                                k.inc("ms", s.activation(p_sb[1][:], ps_s[1][:], AF.Exp,
                                                         scale=scale))
                        hist[("sfree",)] = k.cnt["ms"]
                        p_done = k.cnt["ms"]

                        @block.tensor
                        def _(t, qc=qc, kt2=kt2, ktop=ktop, kvh=kvh, p_done=p_done):
                            t.wait_ge(k.sem("ms"), p_done)
                            if kt2 == 0 and ("ofree",) in hist:
                                t.wait_ge(k.sem("mv"), hist[("ofree",)])
                            if kt2 == 0 and ("dfree",) in hist:
                                t.wait_ge(k.sem("ms"), hist[("dfree",)])
                            if kt2 == 0 and ("dtfree2",) in hist:
                                t.wait_ge(k.sem("ms"), hist[("dtfree2",)])
                            for j in range(2):
                                kt = kt2 + j
                                mm = t.matmul(ps_d[0:1, :], ones1[:, 0:1], p_sb[j][:],
                                              start=(kt == 0), stop=(kt == ktop - 1),
                                              skip_group_check=True)
                                for sub in range(4):
                                    if kt <= qc * 4 + sub:
                                        mm = t.matmul(
                                            ps_o[sub][:, 0:128],
                                            p_sb[j][:, sub * 128:(sub + 1) * 128],
                                            v_sb[:, kt * KW + kvh * 128: kt * KW + (kvh + 1) * 128],
                                            start=(kt == 0), stop=(kt == qc * 4 + sub),
                                            skip_group_check=True)
                            k.inc("mt", mm)
                        hist[("pfree",)] = k.cnt["mt"]
                    pv_done = k.cnt["mt"]

                    @block.scalar
                    def _(s):
                        s.wait_ge(k.sem("mt"), pv_done)
                        k.inc("ms", s.copy(drow[0:1, :], ps_d[0:1, :]))
                    hist[("dfree",)] = k.cnt["ms"]
                    drow_done = k.cnt["ms"]

                    @block.tensor
                    def _(t):
                        t.wait_ge(k.sem("ms"), drow_done)
                        for j in range(4):
                            mm = t.transpose(ps_d[:, 448 + j: 449 + j],
                                             drow[0:1, j * 128:(j + 1) * 128],
                                             ident_f32[0:1, 0:1])
                        k.inc("mt", mm)
                    dt_done = k.cnt["mt"]

                    @block.scalar
                    def _(s):
                        s.wait_ge(k.sem("mt"), dt_done)
                        s.activation(lnd[:, 0:4], ps_d[:, 448:452], AF.Ln)
                        k.inc("ms", s.activation(dhat[:, 0:4], lnd[:, 0:4], AF.Exp,
                                                 scale=-1.0))
                    hist[("dtfree2",)] = k.cnt["ms"]
                    dhat_done = k.cnt["ms"]

                    @block.vector
                    def _(v):
                        v.wait_ge(k.sem("ms"), dhat_done)
                        if ("tpfree",) in hist:
                            v.wait_ge(k.sem("mt"), hist[("tpfree",)])
                        for sub in range(4):
                            inst = v.tensor_scalar(on_sb[:, sub * 128:(sub + 1) * 128],
                                                   ps_o[sub][:, 0:128],
                                                   dhat[:, sub:sub + 1], None, ALU.mult)
                        k.inc("mv", inst)
                    hist[("ofree",)] = k.cnt["mv"]
                    on_done = k.cnt["mv"]

                    @block.tensor
                    def _(t):
                        t.wait_ge(k.sem("mv"), on_done)
                        if ("obcp",) in hist:
                            t.wait_ge(k.sem("ms"), hist[("obcp",)])
                        for sub in range(4):
                            mm = t.transpose(ps_tp[:, sub * 128:(sub + 1) * 128],
                                             on_sb[:, sub * 128:(sub + 1) * 128], ident_bf[:])
                        k.inc("mt", mm)
                    hist[("tpfree",)] = k.cnt["mt"]
                    tp_done = k.cnt["mt"]

                    @block.scalar
                    def _(s, h=h, qc=qc):
                        s.wait_ge(k.sem("mt"), tp_done)
                        k.inc("ms", s.copy(ob_sb[:, h * T + qc * 512: h * T + (qc + 1) * 512],
                                           ps_tp[:, 0:512]))
                    hist[("obcp",)] = k.cnt["ms"]

                hist[("hmm", h)] = k.cnt["mt"]

        k.barrier()
        with contextlib.ExitStack() as pes:
            ko = K.__new__(K)
            ko.nc, ko.block, ko.es = nc, block, pes
            ko.sems, ko.cnt = k.sems, k.cnt
            ko.prefix = k.prefix
            phase_oproj_resid(ko, cfg, b, ob_sb)
        k.barrier()
    oes.close()


def phase_oproj_resid(k, cfg, b, ob_sb):
    nc, block = k.nc, k.block
    woT = cfg["woT_hbm"]
    g_t = cfg["g_hbm"].rearrange("(i p) d -> i p d", p=128)
    x_t = cfg["x_hbm"].rearrange("(i p) d -> i p d", p=128)
    xo_t = cfg["xout_hbm"].rearrange("(i p) d -> i p d", p=128)

    wo_sb = k.sb(f"op{b}_wo", [128, NKD * D], BF16)
    g_sb = [k.sb(f"op{b}_g{j}", [128, D], BF16) for j in range(2)]
    x_sb = [k.sb(f"op{b}_x{j}", [128, D], F32) for j in range(2)]
    xo_sb = [k.sb(f"op{b}_xo{j}", [128, D], F32) for j in range(2)]
    sg_sb = k.sb(f"op{b}_sg", [128, 512], F32)
    pso = [k.ps(f"op{b}_ps{j}", [128, 512]) for j in range(8)]

    @block.gpsimd
    def _(g):
        for kk in range(NKD):
            k.dma(g, wo_sb[:, kk * D:(kk + 1) * D], woT[kk * 128:(kk + 1) * 128, :], "qg")
    wo_ld = k.cnt["qg"]

    hist = {}
    for m in range(16):
        gm = b * 16 + m
        buf = m % 2
        grp = m % 2

        @block.sync
        def _(sy, m=m, gm=gm, buf=buf):
            if m >= 2:
                sy.wait_ge(k.sem("mv"), hist[("resid", m - 2)])
            k.dma(sy, g_sb[buf][:], g_t[gm], "qs")
            k.dma(sy, x_sb[buf][:], x_t[gm], "qs")
        gx_ld = k.cnt["qs"]

        @block.tensor
        def _(t, m=m, grp=grp):
            if m == 0:
                t.wait_ge(k.sem("qg"), wo_ld)
            if m >= 2:
                t.wait_ge(k.sem("mv"), hist[("evac", m - 2)])
            for kk in range(NKD):
                for n in range(4):
                    mm = t.matmul(pso[grp * 4 + n][:],
                                  ob_sb[:, kk * T + m * 128: kk * T + (m + 1) * 128],
                                  wo_sb[:, kk * D + n * 512: kk * D + (n + 1) * 512],
                                  start=(kk == 0), stop=(kk == NKD - 1))
            k.inc("mt", mm)
        mm_done = k.cnt["mt"]

        @block.vector
        def _(v, m=m, gm=gm, buf=buf, grp=grp):
            v.wait_ge(k.sem("mt"), mm_done)
            v.wait_ge(k.sem("qs"), gx_ld)
            if m >= 2:
                v.wait_ge(k.sem("qs"), hist[("xout", m - 2)])
            for n in range(4):
                v.tensor_tensor(sg_sb[:], pso[grp * 4 + n][:],
                                g_sb[buf][:, n * 512:(n + 1) * 512], ALU.mult)
                inst = v.tensor_tensor(xo_sb[buf][:, n * 512:(n + 1) * 512], sg_sb[:],
                                       x_sb[buf][:, n * 512:(n + 1) * 512], ALU.add)
            k.inc("mv", inst)
        hist[("evac", m)] = k.cnt["mv"]
        hist[("resid", m)] = k.cnt["mv"]
        ev_done = k.cnt["mv"]

        @block.sync
        def _(sy, m=m, gm=gm, buf=buf):
            sy.wait_ge(k.sem("mv"), ev_done)
            k.dma(sy, xo_t[gm], xo_sb[buf][:], "qs")
        hist[("xout", m)] = k.cnt["qs"]


def scoped(k):
    class _S:
        def __enter__(self):
            self.pes = contextlib.ExitStack()
            self.pes.__enter__()
            kk = K.__new__(K)
            kk.nc, kk.block, kk.es = k.nc, k.block, self.pes
            kk.sems, kk.cnt = k.sems, k.cnt
            kk.prefix = k.prefix
            return kk

        def __exit__(self, *a):
            return self.pes.__exit__(*a)
    return _S()


def make_sign_wprep(k, tag):
    tA = k.sb(tag + "_sgA", [128, 2048], F32)
    tB = k.sb(tag + "_sgB", [128, 2048], F32)

    def wprep(k, p, pbuf, wsb_buf, pw, wld):
        nc, block = k.nc, k.block

        @block.vector
        def _(v):
            v.wait_ge(k.sem("qg"), wld)
            for kk in range(NKD):
                sl = wsb_buf[:, kk * 2048: kk * 2048 + pw]
                v.tensor_scalar(tA[:, 0:pw], sl, 0.0, None, ALU.is_gt)
                v.tensor_scalar(tB[:, 0:pw], sl, 0.0, None, ALU.is_lt)
                inst = v.tensor_tensor(sl, tA[:, 0:pw], tB[:, 0:pw], ALU.subtract)
            k.inc("mv", inst)
        return ("mv", k.cnt["mv"])

    return wprep


def phase_absmax(k, w_hbm, W, wmax_col, tag, ident_f32=None, ones_rowf=None):
    nc, block = k.nc, k.block
    wt = [k.sb(tag + f"_aw{j}", [128, 2048], F32) for j in range(2)]
    nstrip = w_hbm.shape[0] // 128
    ncol = (W + 2047) // 2048
    ntile = nstrip * ncol
    cols = k.sb(tag + "_cols", [128, ntile], F32)
    cols2 = k.sb(tag + "_cols2", [128, ntile], F32)
    hist = {}
    for kk in range(nstrip):
        for c in range(ncol):
            i = kk * ncol + c
            cw = min(2048, W - c * 2048)
            buf = i % 2

            @block.sync
            def _(sy, kk=kk, c=c, buf=buf, cw=cw, i=i):
                if i >= 2:
                    sy.wait_ge(k.sem("mv"), hist[i - 2])
                k.dma(sy, wt[buf][:, 0:cw],
                      w_hbm[kk * 128:(kk + 1) * 128, c * 2048:c * 2048 + cw], "qs")
            ld = k.cnt["qs"]

            @block.vector
            def _(v, buf=buf, cw=cw, i=i):
                v.wait_ge(k.sem("qs"), ld)
                k.inc("mv", v.tensor_reduce(cols[:, i:i + 1], wt[buf][:, 0:cw], AXL.X,
                                            ALU.max, apply_absolute_value=True))
            hist[i] = k.cnt["mv"]
    red_done = k.cnt["mv"]

    @block.scalar
    def _(s):
        s.wait_ge(k.sem("mv"), red_done)
        k.inc("ms", s.copy(cols2[:], cols[:]))
    cp_done = k.cnt["ms"]

    @block.vector
    def _(v):
        v.wait_ge(k.sem("ms"), cp_done)
        k.inc("mv", v.tensor_reduce(wmax_col[:, 0:1], cols2[:], AXL.X, ALU.max))
    red2 = k.cnt["mv"]

    ps_x = k.ps(tag + "_psx", [128, 512])
    rowb = k.sb(tag + "_rowb", [128, 128], F32)
    rowb2 = k.sb(tag + "_rowb2", [128, 128], F32)
    val = k.sb(tag + "_val", [128, 1], F32)
    val2 = k.sb(tag + "_val2", [128, 1], F32)

    @block.tensor
    def _(t):
        t.wait_ge(k.sem("mv"), red2)
        k.inc("mt", t.transpose(ps_x[0:1, 0:128], wmax_col[:, 0:1], ident_f32[:]))
    c1 = k.cnt["mt"]

    @block.scalar
    def _(s):
        s.wait_ge(k.sem("mt"), c1)
        k.inc("ms", s.copy(rowb[0:1, 0:128], ps_x[0:1, 0:128]))
    c2 = k.cnt["ms"]

    @block.vector
    def _(v):
        v.wait_ge(k.sem("ms"), c2)
        k.inc("mv", v.tensor_reduce(val[0:1, 0:1], rowb[0:1, 0:128], AXL.X, ALU.max))
    c3 = k.cnt["mv"]

    @block.scalar
    def _(s):
        s.wait_ge(k.sem("mv"), c3)
        k.inc("ms", s.copy(val2[0:1, 0:1], val[0:1, 0:1]))
    c3b = k.cnt["ms"]

    @block.tensor
    def _(t):
        t.wait_ge(k.sem("ms"), c3b)
        k.inc("mt", t.matmul(ps_x[:, 256:257], ones_rowf[0:1, :], val2[0:1, 0:1],
                             start=True, stop=True))
    c4 = k.cnt["mt"]

    @block.scalar
    def _(s):
        s.wait_ge(k.sem("mt"), c4)
        k.inc("ms", s.copy(wmax_col[:, 0:1], ps_x[:, 256:257]))
    return k.cnt["ms"]


def make_ffn_evac(k, gamma_rep, beta_rep, wmax_col, g_hbm, tag):
    nc, block = k.nc, k.block
    tf = k.sb(tag + "_tf", [128, 512], F32)
    t2 = [k.sb(tag + f"_t2{j}", [128, 2048], F32) for j in range(2)]
    gst = [k.sb(tag + f"_g{j}", [128, 2048], BF16) for j in range(2)]
    g_t = g_hbm.rearrange("(i p) d -> i p d", p=128)
    st_hist = {}

    def evac(k, gi, ncg, ps, mm_cnt, group_pss):
        nc4 = ncg % 4
        buf = gi % 2

        @block.vector
        def _(v):
            v.wait_ge(k.sem("mt"), mm_cnt)
            if ("silu", buf, nc4) in st_hist:
                v.wait_ge(k.sem("ms"), st_hist[("silu", buf, nc4)])
            v.tensor_tensor(tf[:], ps[:], gamma_rep[:, nc4 * 512:(nc4 + 1) * 512], ALU.mult)
            k.inc("mv", v.scalar_tensor_tensor(t2[buf][:, nc4 * 512:(nc4 + 1) * 512], tf[:],
                                               wmax_col[:, 0:1],
                                               beta_rep[:, nc4 * 512:(nc4 + 1) * 512],
                                               ALU.mult, ALU.add))
        c1 = k.cnt["mv"]

        @block.scalar
        def _(s):
            s.wait_ge(k.sem("mv"), c1)
            if nc4 == 0 and gi >= 2:
                s.wait_ge(k.sem("qs"), st_hist[gi - 2])
            k.inc("ms", s.activation(gst[buf][:, nc4 * 512:(nc4 + 1) * 512],
                                     t2[buf][:, nc4 * 512:(nc4 + 1) * 512], AF.Silu))
        c2 = k.cnt["ms"]
        st_hist[("silu", buf, nc4)] = c2

        if nc4 == 3:
            @block.sync
            def _(sy):
                sy.wait_ge(k.sem("ms"), c2)
                k.dma(sy, g_t[gi], gst[buf][:], "qs")
            st_hist[gi] = k.cnt["qs"]

    return evac


def build_l0(nc, phases=(1,2,3,4,5), dbg=False, k=None, io=None):
    ext = {}

    def ein(name, shape, dt=F32):
        if io is not None:
            ext[name] = io[name]
            return ext[name]
        ext[name] = nc.dram_tensor(name, shape, dt, kind="ExternalInput").ap()
        return ext[name]

    def edecl(name, shape, dt=F32, kind="Internal"):
        if io is not None:
            return io[name]
        return nc.dram_tensor(name, shape, dt, kind=kind).ap()

    x = ein("x", [TOK, D])
    wqT = ein("wqT", [D, 2048])
    wkT = ein("wkT", [D, 2048])
    wvT = ein("wvT", [D, 2048])
    woT = ein("woT", [2048, D])
    fwT = ein("fwT", [D, 2048])
    qnw_rep = ein("qnw_rep", [128, 2048])
    knw_rep = ein("knw_rep", [128, 2048])
    cos_rep = ein("cos_rep", [T, 1024])
    sin_rep = ein("sin_rep", [T, 1024])
    lnw_rep = ein("lnw_rep", [128, D])
    lnb_rep = ein("lnb_rep", [128, D])
    gamma_rep = ein("gamma_rep", [128, D])
    beta_rep = ein("beta_rep", [128, D])
    ident_bf_h = ein("ident_bf", [128, 128], BF16)
    ident_f32_h = ein("ident_f32", [128, 128])
    ones1_h = ein("ones1", [128, 1], BF16)
    ones_row1_h = ein("ones_row1", [1, 128], BF16)
    ones_rowf_h = ein("ones_rowf", [1, 128])
    masks_h = ein("masks_f32", [4 * 128, 512])
    xout = edecl("xout", [TOK, D], F32, "ExternalOutput")

    ikind = "ExternalOutput" if dbg else "Internal"
    xn_bf = edecl("xn_bf", [TOK, D], BF16, ikind)
    xn2_bf = edecl("xn2_bf", [TOK, D], BF16, ikind)
    q_hbm = edecl("q_sc", [TOK, 2048], BF16, ikind)
    k_hbm = edecl("k_sc", [TOK, 2048], BF16, ikind)
    v_hbm = edecl("v_sc", [TOK, 2048], BF16, ikind)
    g_hbm = edecl("g_sc", [TOK, D], BF16, ikind)

    es = contextlib.ExitStack()
    with es:
        if k is None:
            block = es.enter_context(nc.Block())
            k = K(nc, block, es)
            for s_ in ["qs", "qg", "mt", "mv", "ms", "mg"]:
                k.sem(s_)
        else:
            block = k.block
            k = K.sub(k, es)

        ident_bf = k.sb("c_identbf", [128, 128], BF16)
        ident_f32 = k.sb("c_identf32", [128, 128], F32)
        ones1 = k.sb("c_ones1", [128, 1], BF16)
        ones_row1 = k.sb("c_onesrow", [1, 128], BF16)
        ones_rowf = k.sb("c_onesrowf", [1, 128], F32)
        wmax_col = k.sb("c_wmax", [128, 1], F32)

        @block.sync
        def _(sy):
            k.dma(sy, ident_bf[:], ident_bf_h, "qs")
            k.dma(sy, ident_f32[:], ident_f32_h, "qs")
            k.dma(sy, ones1[:], ones1_h, "qs")
            k.dma(sy, ones_row1[:], ones_row1_h, "qs")
            k.dma(sy, ones_rowf[:], ones_rowf_h, "qs")
        k.barrier()

        if 1 in phases:
            with scoped(k) as kp:
                phase_ln(kp, x, out_bf_hbm=xn_bf, eps=1e-5, tag="ln1")
            k.barrier()

        for name, wT, out_h, wrep_h, rope in [] if 2 not in phases else [
                ("q", wqT, q_hbm, qnw_rep, True),
                ("k", wkT, k_hbm, knw_rep, True),
                ("v", wvT, v_hbm, None, False)]:
            with scoped(k) as kp:
                if name == "v":
                    evac = make_v_evac(kp, out_h, "pv")
                    phase_proj(kp, xn_bf, wT, 2048, evac, "pv", evac_sem="ms")
                else:
                    wrep_sb = kp.sb("c_wrep" + name, [128, 2048], F32)

                    @block.sync
                    def _(sy):
                        kp.dma(sy, wrep_sb[:], wrep_h, "qs")
                    evac = make_qk_evac(kp, wrep_sb, cos_rep, sin_rep, out_h,
                                        1e-6, "p" + name, rope=rope)
                    phase_proj(kp, xn_bf, wT, 2048, evac, "p" + name, evac_sem="mv")
            k.barrier()

        if 3 in phases:
          with scoped(k) as kp:
            lnwr = kp.sb("c_lnw", [128, D], F32)
            lnbr = kp.sb("c_lnb", [128, D], F32)

            @block.sync
            def _(sy):
                kp.dma(sy, lnwr[:], lnw_rep, "qs")
                kp.dma(sy, lnbr[:], lnb_rep, "qs")
            import os as _os
            if _os.environ.get("LN2_NOLNW"):
                phase_ln(kp, xn_bf, out_bf_hbm=xn2_bf, eps=1e-5, tag="ln2",
                         in_dtype=BF16)
            else:
                phase_ln(kp, xn_bf, out_bf_hbm=xn2_bf, eps=1e-5, tag="ln2",
                         in_dtype=BF16, lnw_rep=lnwr, lnb_rep=lnbr)
          k.barrier()

        if 4 in phases or 6 in phases:
          with scoped(k) as kp:
            phase_absmax(kp, fwT, 2048, wmax_col, "amx", ident_f32=ident_f32,
                         ones_rowf=ones_rowf)
          k.barrier()

        if 4 in phases or 7 in phases:
          with scoped(k) as kp:
            gammar = kp.sb("c_gamma", [128, D], F32)
            betar = kp.sb("c_beta", [128, D], F32)

            @block.sync
            def _(sy):
                kp.dma(sy, gammar[:], gamma_rep, "qs")
                kp.dma(sy, betar[:], beta_rep, "qs")
            wprep = make_sign_wprep(kp, "ffs")
            evac = make_ffn_evac(kp, gammar, betar, wmax_col, g_hbm, "ffe")
            phase_proj(kp, xn2_bf, fwT, 2048, evac, "ff", evac_sem="mv", wprep_cb=wprep)
          k.barrier()

        cfg = dict(q_hbm=q_hbm, k_hbm=k_hbm, v_hbm=v_hbm, woT_hbm=woT,
                   g_hbm=g_hbm, x_hbm=x, xout_hbm=xout,
                   ident_bf=ident_bf, ident_f32=ident_f32, ones1=ones1,
                   ones_row1=ones_row1, masks_hbm=masks_h, mask_dtype=F32,
                   softcap=50.0, scale=1.0 / float(np.sqrt(128.0)), kv_rep=1)
        if 5 in phases:
            phase_attn_oproj(k, cfg)
        k.barrier()
    return nc


def prep_l0_inputs(xc, w):
    P = perm_deint(128)
    out = {"x": np.ascontiguousarray(xc, dtype=np.float32)}

    def permrows(m):
        return np.ascontiguousarray(m.reshape(16, 128, -1)[:, P, :].reshape(2048, -1))

    out["wqT"] = np.ascontiguousarray(permrows(w["L0_attn_wq"]).T)
    out["wkT"] = np.ascontiguousarray(permrows(w["L0_attn_wk"]).T)
    out["wvT"] = np.ascontiguousarray(w["L0_attn_wv"].T)
    out["woT"] = np.ascontiguousarray(w["L0_attn_wo"].T)
    out["fwT"] = np.ascontiguousarray(w["L0_ffn_weight"].T)
    qn = w["L0_attn_q_norm_w"][P]
    kn = w["L0_attn_k_norm_w"][P]
    out["qnw_rep"] = np.ascontiguousarray(np.tile(qn, (128, 16)))
    out["knw_rep"] = np.ascontiguousarray(np.tile(kn, (128, 16)))
    out["cos_rep"] = np.ascontiguousarray(
        np.tile(np.asarray(w["L0_attn_rope_cos"])[:16, :].reshape(1, 1024), (T, 1)))
    out["sin_rep"] = np.ascontiguousarray(
        np.tile(np.asarray(w["L0_attn_rope_sin"])[:16, :].reshape(1, 1024), (T, 1)))
    out["lnw_rep"] = np.ascontiguousarray(np.tile(w["L0_ffn_ln_weight"], (128, 1)))
    out["lnb_rep"] = np.ascontiguousarray(np.tile(w["L0_ffn_ln_bias"], (128, 1)))
    out["gamma_rep"] = np.ascontiguousarray(np.tile(w["L0_ffn_gamma"], (128, 1)))
    out["beta_rep"] = np.ascontiguousarray(np.tile(w["L0_ffn_beta"], (128, 1)))
    out["ident_bf"] = np.eye(128, dtype=bf)
    out["ident_f32"] = np.eye(128, dtype=np.float32)
    out["ones1"] = np.ones((128, 1), dtype=bf)
    out["ones_row1"] = np.ones((1, 128), dtype=bf)
    out["ones_rowf"] = np.ones((1, 128), dtype=np.float32)
    mk = np.zeros((4 * 128, 512), dtype=np.float32)
    for dlt in range(4):
        p = np.arange(128)[:, None]
        j = np.arange(512)[None, :]
        mk[dlt * 128:(dlt + 1) * 128] = np.where(j >= p + dlt * 128, 0.0, -1e30)
    out["masks_f32"] = mk
    return out


def make_gate_evac(k, n128rs, sgate_rep, gate_hbm, tag):
    nc, block = k.nc, k.block
    tf = k.sb(tag + "_tf", [128, 512], F32)
    t2 = [k.sb(tag + f"_t2{j}", [128, 512], F32) for j in range(2)]
    gb = [k.sb(tag + f"_gb{j}", [128, 512], BF16) for j in range(2)]
    st_hist = {}

    def evac(k, gi, ncg, ps, mm_cnt, group_pss):
        ib = (gi * 11 + ncg) % 2

        @block.vector
        def _(v):
            v.wait_ge(k.sem("mt"), mm_cnt)
            if ("silu", ib) in st_hist:
                v.wait_ge(k.sem("ms"), st_hist[("silu", ib)])
            v.tensor_scalar(tf[:], ps[:], n128rs[:, gi:gi + 1], None, ALU.add)
            k.inc("mv", v.tensor_tensor(t2[ib][:], tf[:],
                                        sgate_rep[:, ncg * 512:(ncg + 1) * 512], ALU.mult))
        c1 = k.cnt["mv"]

        @block.scalar
        def _(s):
            s.wait_ge(k.sem("mv"), c1)
            if ("st", ib) in st_hist:
                s.wait_ge(k.sem("qs"), st_hist[("st", ib)])
            k.inc("ms", s.activation(gb[ib][:], t2[ib][:], AF.Silu))
        c2 = k.cnt["ms"]
        st_hist[("silu", ib)] = c2

        @block.sync
        def _(sy):
            sy.wait_ge(k.sem("ms"), c2)
            k.dma(sy, gate_hbm[gi * 128:(gi + 1) * 128, ncg * 512:(ncg + 1) * 512],
                  gb[ib][:], "qs")
        st_hist[("st", ib)] = k.cnt["qs"]

    return evac


def make_up_evac(k, n128rs, sud_rep, gate_hbm, guT_hbm, gusum_sb, tag):
    nc, block = k.nc, k.block
    tf = k.sb(tag + "_tf", [128, 512], F32)
    t2 = k.sb(tag + "_t2", [128, 512], F32)
    gsl = [k.sb(tag + f"_gsl{j}", [128, 512], BF16) for j in range(2)]
    gu = [k.sb(tag + f"_gu{j}", [128, 512], BF16) for j in range(2)]
    st_hist = {}

    def evac(k, gi, ncg, ps, mm_cnt, group_pss):
        ib = (gi * 11 + ncg) % 2

        @block.sync
        def _(sy):
            if ("st", ib) in st_hist:
                sy.wait_ge(k.sem("mv"), st_hist[("gsl", ib)])
            k.dma(sy, gsl[ib][:],
                  gate_hbm[gi * 128:(gi + 1) * 128, ncg * 512:(ncg + 1) * 512], "qs")
        g_ld = k.cnt["qs"]

        @block.vector
        def _(v):
            v.wait_ge(k.sem("mt"), mm_cnt)
            v.wait_ge(k.sem("qs"), g_ld)
            if ("st", ib) in st_hist:
                v.wait_ge(k.sem("qs"), st_hist[("st", ib)])
            v.tensor_scalar(tf[:], ps[:], n128rs[:, gi:gi + 1], None, ALU.add)
            v.tensor_tensor(t2[:], tf[:], sud_rep[:, ncg * 512:(ncg + 1) * 512], ALU.mult)
            v.tensor_tensor(gu[ib][:], t2[:], gsl[ib][:], ALU.mult)
            k.inc("mv", v.tensor_scalar(tf[:], gu[ib][:], 1.0, None, ALU.mult, ALU.add,
                                        accum_out=gusum_sb[:, gi * 11 + ncg: gi * 11 + ncg + 1]))
        c1 = k.cnt["mv"]
        st_hist[("gsl", ib)] = k.cnt["mv"]

        @block.sync
        def _(sy):
            sy.wait_ge(k.sem("mv"), c1)
            k.dma(sy, guT_hbm[gi * 128:(gi + 1) * 128, ncg * 512:(ncg + 1) * 512],
                  gu[ib][:], "qs")
        st_hist[("st", ib)] = k.cnt["qs"]

    return evac


def make_down_evac(k, n128gs, sdown_rep, g_hbm, tag):
    nc, block = k.nc, k.block
    tf = [k.sb(tag + f"_tf{j}", [128, 512], F32) for j in range(2)]
    gb = [k.sb(tag + f"_gb{j}", [128, 512], BF16) for j in range(2)]
    st_hist = {}

    def evac(k, gi, ncg, ps, mm_cnt, group_pss):
        ib = (gi + ncg) % 2

        @block.vector
        def _(v):
            v.wait_ge(k.sem("mt"), mm_cnt)
            if ("silu", ib) in st_hist:
                v.wait_ge(k.sem("ms"), st_hist[("silu", ib)])
            v.tensor_scalar(tf[ib][:], ps[:], n128gs[:, gi:gi + 1], None, ALU.add)
            k.inc("mv", v.tensor_tensor(tf[ib][:], tf[ib][:],
                                        sdown_rep[:, ncg * 512:(ncg + 1) * 512], ALU.mult))
        c1 = k.cnt["mv"]

        @block.scalar
        def _(s):
            s.wait_ge(k.sem("mv"), c1)
            if ("st", ib) in st_hist:
                s.wait_ge(k.sem("qs"), st_hist[("st", ib)])
            k.inc("ms", s.activation(gb[ib][:], tf[ib][:], AF.Silu))
        c2 = k.cnt["ms"]
        st_hist[("silu", ib)] = c2

        @block.sync
        def _(sy):
            sy.wait_ge(k.sem("ms"), c2)
            k.dma(sy, g_hbm[gi * 128:(gi + 1) * 128, ncg * 512:(ncg + 1) * 512],
                  gb[ib][:], "qs")
        st_hist[("st", ib)] = k.cnt["qs"]

    return evac


def build_l1(nc, dbg=False, k=None, io=None):
    ext = {}

    def ein(name, shape, dt=F32):
        if io is not None:
            ext[name] = io[name]
            return ext[name]
        ext[name] = nc.dram_tensor(name, shape, dt, kind="ExternalInput").ap()
        return ext[name]

    def edecl(name, shape, dt=F32, kind="Internal"):
        if io is not None:
            return io[name]
        return nc.dram_tensor(name, shape, dt, kind=kind).ap()

    x = ein("x", [TOK, D])
    wqT = ein("wqT", [D, 2048])
    wkT = ein("wkT", [D, 2048])
    wvT = ein("wvT", [D, 2048])
    woT = ein("woT", [2048, D])
    wgT_i = ein("wgT", [D, 5632], I32)
    wuT_i = ein("wuT", [D, 5632], I32)
    wdT_i = ein("wdT", [5632, D], I32)
    sgate_rep = ein("sgate_rep", [128, 5632])
    sup_rep = ein("sup_rep", [128, 5632])
    sdown_rep = ein("sdown_rep", [128, 2048])
    qnw_rep = ein("qnw_rep", [128, 2048])
    knw_rep = ein("knw_rep", [128, 2048])
    ident_bf_h = ein("ident_bf", [128, 128], BF16)
    ident_f32_h = ein("ident_f32", [128, 128])
    ones1_h = ein("ones1", [128, 1], BF16)
    ones_row1_h = ein("ones_row1", [1, 128], BF16)
    masks_h = ein("masks_bf", [4 * 128, 512], BF16)
    masks2_h = ein("masks2_bf", [4 * 128, 512], BF16)
    xout = edecl("xout", [TOK, D], F32, "ExternalOutput")

    ikind = "ExternalOutput" if dbg else "Internal"
    xn_bf = edecl("xn_bf", [TOK, D], BF16, ikind)
    q_hbm = edecl("q_sc", [TOK, 2048], BF16, ikind)
    k_hbm = edecl("k_sc", [TOK, 2048], BF16, ikind)
    v_hbm = edecl("v_sc", [TOK, 2048], BF16, ikind)
    g_hbm = edecl("g_sc", [TOK, D], BF16, ikind)
    gate_hbm = edecl("gate_sc", [TOK, 5632], BF16, ikind)
    guT_hbm = edecl("guT_sc", [TOK, 5632], BF16, ikind)

    es = contextlib.ExitStack()
    with es:
        if k is None:
            block = es.enter_context(nc.Block())
            k = K(nc, block, es)
            for s_ in ["qs", "qg", "mt", "mv", "ms", "mg"]:
                k.sem(s_)
        else:
            block = k.block
            k = K.sub(k, es)

        ident_bf = k.sb("c_identbf", [128, 128], BF16)
        ident_f32 = k.sb("c_identf32", [128, 128], F32)
        ones1 = k.sb("c_ones1", [128, 1], BF16)
        ones_row1 = k.sb("c_onesrow", [1, 128], BF16)
        rowsum_sb = k.sb("c_rowsum", [128, NTT], F32)
        rowsum2 = k.sb("c_rowsum2", [128, NTT], F32)
        n128rs = k.sb("c_n128rs", [128, NTT], F32)
        gusum_sb = k.sb("c_gusum", [128, NTT * 11], F32)
        gusum2 = k.sb("c_gusum2", [128, NTT * 11], F32)
        n128gs = k.sb("c_n128gs", [128, NTT], F32)

        @block.sync
        def _(sy):
            k.dma(sy, ident_bf[:], ident_bf_h, "qs")
            k.dma(sy, ident_f32[:], ident_f32_h, "qs")
            k.dma(sy, ones1[:], ones1_h, "qs")
            k.dma(sy, ones_row1[:], ones_row1_h, "qs")
        k.barrier()

        with scoped(k) as kp:
            phase_ln(kp, x, out_bf_hbm=xn_bf, eps=1e-5, tag="ln1",
                     rowsum_sb=rowsum_sb)
        k.barrier()

        @block.scalar
        def _(s):
            k.inc("ms", s.copy(rowsum2[:], rowsum_sb[:]))
        c_rs = k.cnt["ms"]

        @block.vector
        def _(v):
            v.wait_ge(k.sem("ms"), c_rs)
            k.inc("mv", v.tensor_scalar(n128rs[:], rowsum2[:], -128.0, None, ALU.mult))
        k.barrier()

        for name, wT, out_h, wrep_h in [("q", wqT, q_hbm, qnw_rep),
                                        ("k", wkT, k_hbm, knw_rep),
                                        ("v", wvT, v_hbm, None)]:
            with scoped(k) as kp:
                if name == "v":
                    evac = make_v_evac(kp, out_h, "pv")
                    phase_proj(kp, xn_bf, wT, 2048, evac, "pv", evac_sem="ms")
                else:
                    wrep_sb = kp.sb("c_wrep" + name, [128, 2048], F32)

                    @block.sync
                    def _(sy):
                        kp.dma(sy, wrep_sb[:], wrep_h, "qs")
                    evac = make_qk_evac(kp, wrep_sb, None, None, out_h,
                                        1e-5, "p" + name, rope=False)
                    phase_proj(kp, xn_bf, wT, 2048, evac, "p" + name, evac_sem="mv")
            k.barrier()

        with scoped(k) as kp:
            sg_sb = kp.sb("c_sg", [128, 5632], F32)

            @block.sync
            def _(sy):
                kp.dma(sy, sg_sb[:], sgate_rep, "qs")
            evac = make_gate_evac(kp, n128rs, sg_sb, gate_hbm, "pg")
            phase_proj(kp, xn_bf, wgT_i, 5632, evac, "pg", evac_sem="mv",
                       PASSW=1024, nbuf=2)
        k.barrier()

        with scoped(k) as kp:
            su_sb = kp.sb("c_su", [128, 5632], F32)

            @block.sync
            def _(sy):
                kp.dma(sy, su_sb[:], sup_rep, "qs")
            k.barrier()
            evac = make_up_evac(kp, n128rs, su_sb, gate_hbm, guT_hbm, gusum_sb, "pu")
            phase_proj(kp, xn_bf, wuT_i, 5632, evac, "pu", evac_sem="mv",
                       PASSW=1024, nbuf=2)
        k.barrier()

        @block.scalar
        def _(s):
            k.inc("ms", s.copy(gusum2[:], gusum_sb[:]))
        c_gs = k.cnt["ms"]

        @block.vector
        def _(v):
            v.wait_ge(k.sem("ms"), c_gs)
            for i in range(NTT):
                inst = v.tensor_reduce(gusum_sb[:, i:i + 1], gusum2[:, i * 11:(i + 1) * 11],
                                       AXL.X, ALU.add)
            k.inc("mv", inst)
        k.barrier()

        @block.scalar
        def _(s):
            k.inc("ms", s.copy(gusum2[:, 0:NTT], gusum_sb[:, 0:NTT]))
        c_gs2 = k.cnt["ms"]

        @block.vector
        def _(v):
            v.wait_ge(k.sem("ms"), c_gs2)
            k.inc("mv", v.tensor_scalar(n128gs[:], gusum2[:, 0:NTT], -128.0, None, ALU.mult))
        k.barrier()

        with scoped(k) as kp:
            sdn_sb = kp.sb("c_sdn", [128, 2048], F32)

            @block.sync
            def _(sy):
                kp.dma(sy, sdn_sb[:], sdown_rep, "qs")
            k.barrier()
            evac = make_down_evac(kp, n128gs, sdn_sb, g_hbm, "pd")
            phase_proj(kp, guT_hbm, wdT_i, 2048, evac, "pd", evac_sem="mv",
                       KSTR=44, HT=1024, PASSW=512, nbuf=2)
        k.barrier()

        cfg = dict(q_hbm=q_hbm, k_hbm=k_hbm, v_hbm=v_hbm, woT_hbm=woT,
                   g_hbm=g_hbm, x_hbm=x, xout_hbm=xout,
                   ident_bf=ident_bf, ident_f32=ident_f32, ones1=ones1,
                   ones_row1=ones_row1, masks_hbm=masks_h, mask_dtype=BF16,
                   masks2_hbm=masks2_h,
                   softcap=None, scale=1.0 / float(np.sqrt(128.0)), kv_rep=1)
        phase_attn_oproj(k, cfg)
        k.barrier()
    return nc


def prep_l1_inputs(xc, w):
    out = {"x": np.ascontiguousarray(xc, dtype=np.float32)}
    out["wqT"] = np.ascontiguousarray(w["L1_attn_Wq"].T)
    out["wkT"] = np.ascontiguousarray(w["L1_attn_Wk"].T)
    out["wvT"] = np.ascontiguousarray(w["L1_attn_Wv"].T)
    out["woT"] = np.ascontiguousarray(w["L1_attn_Wo"].T)
    out["wgT"] = np.ascontiguousarray(w["L1_ffn_wq_gate"].T)
    out["wuT"] = np.ascontiguousarray(w["L1_ffn_wq_up"].T)
    out["wdT"] = np.ascontiguousarray(w["L1_ffn_wq_down"].T)
    out["sgate_rep"] = np.ascontiguousarray(np.tile(w["L1_ffn_s_gate"], (128, 1)))
    out["sup_rep"] = np.ascontiguousarray(np.tile(w["L1_ffn_s_up"], (128, 1)))
    out["sdown_rep"] = np.ascontiguousarray(np.tile(w["L1_ffn_s_down"], (128, 1)))
    out["qnw_rep"] = np.ascontiguousarray(np.tile(w["L1_attn_gq"], (128, 16)))
    out["knw_rep"] = np.ascontiguousarray(np.tile(w["L1_attn_gk"], (128, 16)))
    out["ident_bf"] = np.eye(128, dtype=bf)
    out["ident_f32"] = np.eye(128, dtype=np.float32)
    out["ones1"] = np.ones((128, 1), dtype=bf)
    out["ones_row1"] = np.ones((1, 128), dtype=bf)
    mk = np.zeros((4 * 128, 512), dtype=np.float32)
    for dlt in range(4):
        p = np.arange(128)[:, None]
        j = np.arange(512)[None, :]
        mk[dlt * 128:(dlt + 1) * 128] = np.where(j >= p + dlt * 128, 0.0, -1e30)
    out["masks_bf"] = mk.astype(bf)
    mk2 = np.zeros((4 * 128, 512), dtype=np.float32)
    for qs_ in range(4):
        p = np.arange(128)[:, None]
        j = np.arange(512)[None, :]
        mk2[qs_ * 128:(qs_ + 1) * 128] = np.where(j <= p + qs_ * 128, 0.0, -1e30)
    out["masks2_bf"] = mk2.astype(bf)
    return out


def phase_l3ffn(k, h_hbm, bwT_h, g_hbm, ident_bf, hlg_hbm=None, tag="l3f", dbg_hbm=None):
    nc, block = k.nc, k.block
    bw_sb = k.sb(tag + "_bw", [128, 4 * D], BF16)
    hs = [k.sb(tag + f"_h{j}", [128, 1024], BF16) for j in range(2)]
    lg_sb = k.sb(tag + "_lg", [128, 256], F32)
    mtree = k.sb(tag + "_mtree", [128, 256], F32)
    lmaxall = k.sb(tag + "_lmax", [128, 32], F32)
    lmaxall2 = k.sb(tag + "_lmax2", [128, 32], F32)
    suball = k.sb(tag + "_sub", [128, 256], F32)
    expall = k.sb(tag + "_exp", [128, 256], F32)
    sumtree = k.sb(tag + "_sumtree", [128, 256], F32)
    sumeall = k.sb(tag + "_sume", [128, 32], F32)
    pselall = k.sb(tag + "_psel", [128, 32], F32)
    sc4all = k.sb(tag + "_sc4", [128, 32], F32)
    maskall = k.sb(tag + "_maska", [128, 256], F32)
    maskall2 = k.sb(tag + "_maskb", [128, 256], F32)
    hm = k.sb(tag + "_hm", [128, 512], BF16)
    hmT = k.sb(tag + "_hmT", [128, 512], BF16)
    gst = [k.sb(tag + f"_g{j}", [128, D], BF16) for j in range(2)]
    zb = [k.sb(tag + f"_zb{j}", [128, D], F32) for j in range(2)]
    ps_tp = k.ps(tag + "_pstp", [128, 1024], BF16)
    pso = [k.ps(tag + f"_pso{j}", [128, 512]) for j in range(4)]
    h_t = h_hbm.rearrange("(i p) d -> i p d", p=128)
    g_t = g_hbm.rearrange("(i p) d -> i p d", p=128)

    @block.gpsimd
    def _(g):
        for kk in range(4):
            k.dma(g, bw_sb[:, kk * D:(kk + 1) * D],
                  bwT_h[kk * 128:(kk + 1) * 128, :], "qg")
    bw_ld = k.cnt["qg"]

    @block.sync
    def _(sy):
        for i in range(NTT):
            k.dma(sy, lg_sb[:, i * 8:(i + 1) * 8], hlg_hbm[i * 128:(i + 1) * 128, :], "qs")
    lg_ld = k.cnt["qs"]

    lgv = lg_sb.ap().rearrange("p (i e) -> p i e", e=8)
    mtv = mtree.ap().rearrange("p (i e) -> p i e", e=8)
    stv = sumtree.ap().rearrange("p (i e) -> p i e", e=8)
    exv = expall.ap().rearrange("p (i e) -> p i e", e=8)

    @block.vector
    def _(v):
        v.wait_ge(k.sem("qs"), lg_ld)
        v.tensor_tensor(mtv[:, :, 0:4], lgv[:, :, 0:4], lgv[:, :, 4:8], ALU.max)
        v.tensor_tensor(mtv[:, :, 4:6], mtv[:, :, 0:2], mtv[:, :, 2:4], ALU.max)
        k.inc("mv", v.tensor_tensor(lmaxall[:], mtv[:, :, 4:5], mtv[:, :, 5:6], ALU.max))
    c1 = k.cnt["mv"]

    @block.scalar
    def _(s):
        s.wait_ge(k.sem("mv"), c1)
        k.inc("ms", s.copy(lmaxall2[:], lmaxall[:]))
    c2 = k.cnt["ms"]

    @block.vector
    def _(v):
        v.wait_ge(k.sem("ms"), c2)
        for i in range(NTT):
            v.tensor_scalar(suball[:, i * 8:(i + 1) * 8], lg_sb[:, i * 8:(i + 1) * 8],
                            lmaxall2[:, i:i + 1], None, ALU.subtract)
            inst = v.tensor_scalar(maskall[:, i * 8:(i + 1) * 8], lg_sb[:, i * 8:(i + 1) * 8],
                                   lmaxall2[:, i:i + 1], None, ALU.is_ge)
        k.inc("mv", inst)
    c3 = k.cnt["mv"]

    @block.scalar
    def _(s):
        s.wait_ge(k.sem("mv"), c3)
        s.activation(expall[:], suball[:], AF.Exp)
        k.inc("ms", s.copy(maskall2[:], maskall[:]))
    c4 = k.cnt["ms"]

    @block.vector
    def _(v):
        v.wait_ge(k.sem("ms"), c4)
        v.tensor_tensor(stv[:, :, 0:4], exv[:, :, 0:4], exv[:, :, 4:8], ALU.add)
        v.tensor_tensor(stv[:, :, 4:6], stv[:, :, 0:2], stv[:, :, 2:4], ALU.add)
        k.inc("mv", v.tensor_tensor(sumeall[:], stv[:, :, 4:5], stv[:, :, 5:6], ALU.add))
    crt0 = k.cnt["mv"]

    @block.scalar
    def _(s):
        s.wait_ge(k.sem("mv"), crt0)
        s.activation(pselall[:], sumeall[:], AF.Ln, scale=0.25)
        k.inc("ms", s.activation(sc4all[:], pselall[:], AF.Exp, scale=-1.0))
    crt = k.cnt["ms"]
    hist = {}

    for i in range(NTT):
        buf = i % 2

        @block.sync
        def _(sy, i=i, buf=buf):
            if i >= 2:
                sy.wait_ge(k.sem("mv"), hist[("hmdone", i - 2)])
            k.dma(sy, hs[buf][:], h_t[i], "qs")
        h_ld = k.cnt["qs"]

        @block.vector
        def _(v, i=i, buf=buf):
            v.wait_ge(k.sem("qs"), h_ld)
            v.wait_ge(k.sem("ms"), c4)
            if i >= 1:
                v.wait_ge(k.sem("mt"), hist[("hmtp", i - 1)])
            for e in range(8):
                inst = v.tensor_scalar(hm[:, e * 64:(e + 1) * 64],
                                       hs[buf][:, e * 64:(e + 1) * 64],
                                       maskall2[:, i * 8 + e:i * 8 + e + 1], None, ALU.mult)
            k.inc("mv", inst)
        hist[("hmdone", i)] = k.cnt["mv"]
        c5 = k.cnt["mv"]

        @block.tensor
        def _(t, i=i):
            t.wait_ge(k.sem("mv"), c5)
            if i >= 1:
                t.wait_ge(k.sem("ms"), hist[("hmTcp", i - 1)])
            for j in range(4):
                mm = t.transpose(ps_tp[:, j * 128:(j + 1) * 128],
                                 hm[:, j * 128:(j + 1) * 128], ident_bf[:])
            k.inc("mt", mm)
        hist[("hmtp", i)] = k.cnt["mt"]
        c6 = k.cnt["mt"]

        @block.scalar
        def _(s, i=i):
            s.wait_ge(k.sem("mt"), c6)
            k.inc("ms", s.copy(hmT[:], ps_tp[:, 0:512]))
        hist[("hmTcp", i)] = k.cnt["ms"]
        c7 = k.cnt["ms"]

        @block.tensor
        def _(t, i=i):
            t.wait_ge(k.sem("ms"), c7)
            if i == 0:
                t.wait_ge(k.sem("qg"), bw_ld)
            if i >= 1:
                t.wait_ge(k.sem("mv"), hist[("gdone", i - 1)])
            for n in range(4):
                for kk in range(4):
                    mm = t.matmul(pso[n][:], hmT[:, kk * 128:(kk + 1) * 128],
                                  bw_sb[:, kk * D + n * 512: kk * D + (n + 1) * 512],
                                  start=(kk == 0), stop=(kk == 3))
            k.inc("mt", mm)
        c8 = k.cnt["mt"]

        @block.vector
        def _(v, i=i, buf=buf):
            v.wait_ge(k.sem("mt"), c8)
            v.wait_ge(k.sem("ms"), crt)
            if i >= 2:
                v.wait_ge(k.sem("ms"), hist[("zdone", i - 2)])
            for n in range(4):
                inst = v.tensor_scalar(zb[i % 2][:, n * 512:(n + 1) * 512], pso[n][:],
                                       sc4all[:, i:i + 1], None, ALU.mult)
            k.inc("mv", inst)
        hist[("gdone", i)] = k.cnt["mv"]
        c8b = k.cnt["mv"]

        @block.scalar
        def _(s, i=i, buf=buf):
            s.wait_ge(k.sem("mv"), c8b)
            if i >= 2:
                s.wait_ge(k.sem("qs"), hist[("gst", i - 2)])
            inst = s.activation(gst[buf][:], zb[i % 2][:], AF.Silu)
            k.inc("ms", inst)
        hist[("zdone", i)] = k.cnt["ms"]
        c9 = k.cnt["ms"]

        @block.sync
        def _(sy, i=i, buf=buf):
            sy.wait_ge(k.sem("ms"), c9)
            k.dma(sy, g_t[i], gst[buf][:], "qs")
        hist[("gst", i)] = k.cnt["qs"]


def make_store_evac(k, out_hbm, W, tag, f32_chunk=None, f32_hbm=None, f32_w=512):
    nc, block = k.nc, k.block
    st = [k.sb(tag + f"_st{j}", [128, 512], BF16) for j in range(2)]
    stf = [k.sb(tag + f"_stf{j}", [128, f32_w], F32) for j in range(2)] if f32_hbm is not None else None
    st_hist = {}

    def evac(k, gi, ncg, ps, mm_cnt, group_pss):
        ib = (gi * 16 + ncg) % 2

        @block.scalar
        def _(s):
            s.wait_ge(k.sem("mt"), mm_cnt)
            if ("st", ib) in st_hist:
                s.wait_ge(k.sem("qs"), st_hist[("st", ib)])
            inst = s.copy(st[ib][:], ps[:])
            if f32_hbm is not None and ncg == f32_chunk:
                inst = s.copy(stf[ib][:], ps[:, 0:f32_w])
            k.inc("ms", inst)
        c1 = k.cnt["ms"]

        @block.sync
        def _(sy):
            sy.wait_ge(k.sem("ms"), c1)
            k.dma(sy, out_hbm[gi * 128:(gi + 1) * 128, ncg * 512:(ncg + 1) * 512],
                  st[ib][:], "qs")
            if f32_hbm is not None and ncg == f32_chunk:
                k.dma(sy, f32_hbm[gi * 128:(gi + 1) * 128, :], stf[ib][:], "qs")
        st_hist[("st", ib)] = k.cnt["qs"]

    return evac


def build_l3(nc, dbg=False, k=None, io=None):
    ext = {}

    def ein(name, shape, dt=F32):
        if io is not None:
            ext[name] = io[name]
            return ext[name]
        ext[name] = nc.dram_tensor(name, shape, dt, kind="ExternalInput").ap()
        return ext[name]

    def edecl(name, shape, dt=F32, kind="Internal"):
        if io is not None:
            return io[name]
        return nc.dram_tensor(name, shape, dt, kind=kind).ap()

    x = ein("x", [TOK, D])
    wqT = ein("wqT", [D, 2048])
    wkT = ein("wkT", [D, 512])
    wvT = ein("wvT", [D, 512])
    woT = ein("woT", [2048, D])
    awT = ein("awT", [D, 1024])
    bwT_h = ein("bwT", [512, D])
    cos_repq = ein("cos_repq", [T, 1024])
    sin_repq = ein("sin_repq", [T, 1024])
    cos_repk = ein("cos_repk", [T, 256])
    sin_repk = ein("sin_repk", [T, 256])
    ident_bf_h = ein("ident_bf", [128, 128], BF16)
    ident_f32_h = ein("ident_f32", [128, 128])
    ones1_h = ein("ones1", [128, 1], BF16)
    ones_row1_h = ein("ones_row1", [1, 128], BF16)
    masks_h = ein("masks_bf", [4 * 128, 512], BF16)
    masks2_h = ein("masks2_bf", [4 * 128, 512], BF16)
    xout = edecl("xout", [TOK, D], F32, "ExternalOutput")

    ikind = "ExternalOutput" if dbg else "Internal"
    xn_bf = edecl("xn_bf", [TOK, D], BF16, ikind)
    q_hbm = edecl("q_sc", [TOK, 2048], BF16, ikind)
    k_hbm = edecl("k_sc", [TOK, 512], BF16, ikind)
    v_hbm = edecl("v_sc", [TOK, 512], BF16, ikind)
    g_hbm = edecl("g_sc", [TOK, D], BF16, ikind)
    h_hbm = edecl("h_sc", [TOK, 1024], BF16, ikind)
    hlg_hbm = edecl("hlg_sc", [TOK, 8], F32, ikind)

    es = contextlib.ExitStack()
    with es:
        if k is None:
            block = es.enter_context(nc.Block())
            k = K(nc, block, es)
            for s_ in ["qs", "qg", "mt", "mv", "ms", "mg"]:
                k.sem(s_)
        else:
            block = k.block
            k = K.sub(k, es)

        ident_bf = k.sb("c_identbf", [128, 128], BF16)
        ident_f32 = k.sb("c_identf32", [128, 128], F32)
        ones1 = k.sb("c_ones1", [128, 1], BF16)
        ones_row1 = k.sb("c_onesrow", [1, 128], BF16)

        @block.sync
        def _(sy):
            k.dma(sy, ident_bf[:], ident_bf_h, "qs")
            k.dma(sy, ident_f32[:], ident_f32_h, "qs")
            k.dma(sy, ones1[:], ones1_h, "qs")
            k.dma(sy, ones_row1[:], ones_row1_h, "qs")
        k.barrier()

        with scoped(k) as kp:
            phase_ln(kp, x, out_bf_hbm=xn_bf, eps=1e-5, tag="ln1")
        k.barrier()

        with scoped(k) as kp:
            evac = make_qk_evac(kp, None, cos_repq, sin_repq, q_hbm, 0.0, "pq",
                                rope=True, do_rms=False, nh_total=16)
            phase_proj(kp, xn_bf, wqT, 2048, evac, "pq", evac_sem="mv")
        k.barrier()
        with scoped(k) as kp:
            evac = make_qk_evac(kp, None, cos_repk, sin_repk, k_hbm, 0.0, "pk",
                                rope=True, do_rms=False, nh_total=4)
            phase_proj(kp, xn_bf, wkT, 512, evac, "pk", evac_sem="mv")
        k.barrier()
        with scoped(k) as kp:
            evac = make_v_evac(kp, v_hbm, "pv", nch_total=1)
            phase_proj(kp, xn_bf, wvT, 512, evac, "pv", evac_sem="ms")
        k.barrier()

        with scoped(k) as kp:
            evac = make_store_evac(kp, h_hbm, 1024, "ph", f32_chunk=1,
                                   f32_hbm=hlg_hbm, f32_w=8)
            phase_proj(kp, xn_bf, awT, 1024, evac, "ph", evac_sem="ms")
        k.barrier()

        dbg3_hbm = nc.dram_tensor("dbg3", [TOK, 12], F32, kind=ikind).ap() if dbg else None
        with scoped(k) as kp:
            phase_l3ffn(kp, h_hbm, bwT_h, g_hbm, ident_bf, hlg_hbm=hlg_hbm,
                        dbg_hbm=dbg3_hbm)
        k.barrier()

        cfg = dict(q_hbm=q_hbm, k_hbm=k_hbm, v_hbm=v_hbm, woT_hbm=woT,
                   g_hbm=g_hbm, x_hbm=x, xout_hbm=xout,
                   ident_bf=ident_bf, ident_f32=ident_f32, ones1=ones1,
                   ones_row1=ones_row1, masks_hbm=masks_h, mask_dtype=BF16,
                   masks2_hbm=masks2_h,
                   softcap=None, scale=1.0 / float(np.sqrt(128.0)), kv_rep=4)
        phase_attn_oproj(k, cfg)
        k.barrier()
    return nc


def prep_l3_inputs(xc, w):
    P = perm_deint(128)
    out = {"x": np.ascontiguousarray(xc, dtype=np.float32)}

    def permrows(m, nh):
        return np.ascontiguousarray(m.reshape(nh, 128, -1)[:, P, :].reshape(nh * 128, -1))

    out["wqT"] = np.ascontiguousarray(permrows(w["L3_attn_Wq"], 16).T)
    out["wkT"] = np.ascontiguousarray(permrows(w["L3_attn_Wk"], 4).T)
    out["wvT"] = np.ascontiguousarray(w["L3_attn_Wv"].T)
    out["woT"] = np.ascontiguousarray(w["L3_attn_Wo"].T)
    A = w["L3_ffn_A"]
    Wg = w["L3_ffn_Wg"]
    awT = np.zeros((2048, 1024), dtype=np.float32)
    awT[:, 0:512] = A.reshape(512, 2048).T
    awT[:, 512:520] = Wg.T
    out["awT"] = awT
    out["bwT"] = np.ascontiguousarray(
        w["L3_ffn_Bw"].transpose(0, 2, 1).reshape(512, 2048))
    freqs = 1.0 / 10000.0 ** (np.arange(0, 128, 2, dtype=np.float32) / 128)
    ang = np.arange(T, dtype=np.float32)[:, None] * freqs[None, :]
    cos, sin = np.cos(ang), np.sin(ang)
    out["cos_repq"] = np.ascontiguousarray(np.tile(cos, (1, 16)))
    out["sin_repq"] = np.ascontiguousarray(np.tile(sin, (1, 16)))
    out["cos_repk"] = np.ascontiguousarray(np.tile(cos, (1, 4)))
    out["sin_repk"] = np.ascontiguousarray(np.tile(sin, (1, 4)))
    out["ident_bf"] = np.eye(128, dtype=bf)
    out["ident_f32"] = np.eye(128, dtype=np.float32)
    out["ones1"] = np.ones((128, 1), dtype=bf)
    out["ones_row1"] = np.ones((1, 128), dtype=bf)
    mk = np.zeros((4 * 128, 512), dtype=np.float32)
    for dlt in range(4):
        p = np.arange(128)[:, None]
        j = np.arange(512)[None, :]
        mk[dlt * 128:(dlt + 1) * 128] = np.where(j >= p + dlt * 128, 0.0, -1e30)
    out["masks_bf"] = mk.astype(bf)
    mk2 = np.zeros((4 * 128, 512), dtype=np.float32)
    for qs_ in range(4):
        p = np.arange(128)[:, None]
        j = np.arange(512)[None, :]
        mk2[qs_ * 128:(qs_ + 1) * 128] = np.where(j <= p + qs_ * 128, 0.0, -1e30)
    out["masks2_bf"] = mk2.astype(bf)
    return out


def phase_proj_rev(k, xnT_b, w_hbm_tiled, PTILES, evac_cb, tag, evac_sem="ms"):
    nc, block = k.nc, k.block
    wsb = [k.sb(f"{tag}_w{j}", [128, NKD * 128], BF16) for j in range(2)]
    pss = [k.ps(f"{tag}_ps{j}", [128, 512]) for j in range(8)]
    hist = {}
    for pt in range(PTILES):
        wbuf = pt % 2

        @block.gpsimd
        def _(g, pt=pt, wbuf=wbuf):
            if pt >= 2:
                g.wait_ge(k.sem("mt"), hist[("mm", pt - 2)])
            k.dma(g, wsb[wbuf][:], w_hbm_tiled[pt], "qg")
        wld = k.cnt["qg"]

        grp = pt % 2

        @block.tensor
        def _(t, pt=pt, wbuf=wbuf, grp=grp):
            t.wait_ge(k.sem("qg"), wld)
            if pt >= 2:
                t.wait_ge(k.sem(evac_sem), hist[("evac", pt - 2)])
            for kk in range(NKD):
                for n in range(4):
                    mm = t.matmul(pss[grp * 4 + n][:],
                                  wsb[wbuf][:, kk * 128:(kk + 1) * 128],
                                  xnT_b[:, kk * T + n * 512: kk * T + (n + 1) * 512],
                                  start=(kk == 0), stop=(kk == NKD - 1))
            k.inc("mt", mm)
        hist[("mm", pt)] = k.cnt["mt"]
        mm_cnt = k.cnt["mt"]

        for n in range(4):
            evac_cb(k, pt, n, pss[grp * 4 + n], mm_cnt)
        hist[("evac", pt)] = k.cnt[evac_sem]
    return hist


def build_l2a_mamba(k, b, io, cst):
    nc, block = k.nc, k.block
    NDT = 32

    with scoped(k) as kp:
        xnT_b = kp.sb(f"m{b}_xnT", [128, NKD * T], BF16)

        @block.sync
        def _(sy):
            for j in range(NKD):
                k.inc("qs", sy.dma_start_transpose(
                    xnT_b[:, j * T:(j + 1) * T],
                    io["xn_bf"][b * T:(b + 1) * T, j * 128:(j + 1) * 128]), 16)
        k.barrier()

        u_t = io["u_hbm"].rearrange("(i p) l -> i p l", p=128)
        g_t = io["gate_hbm"].rearrange("(i p) l -> i p l", p=128)
        st = [kp.sb(f"m{b}_ipst{j}", [128, T], BF16) for j in range(2)]
        st_hist = {}

        def ip_evac(k, pt, lc, ps, mm_cnt):
            ib = pt % 2

            @block.scalar
            def _(s):
                s.wait_ge(k.sem("mt"), mm_cnt)
                if lc == 0 and ("st", ib) in st_hist:
                    s.wait_ge(k.sem("qs"), st_hist[("st", ib)])
                k.inc("ms", s.copy(st[ib][:, lc * 512:(lc + 1) * 512], ps[:]))
            c1 = k.cnt["ms"]

            if lc == 3:
                @block.sync
                def _(sy):
                    sy.wait_ge(k.sem("ms"), c1)
                    tgt = u_t[pt] if pt < NDT else g_t[pt - NDT]
                    k.dma(sy, tgt, st[ib][:], "qs")
                st_hist[("st", ib)] = k.cnt["qs"]

        phase_proj_rev(kp, xnT_b, io["ipT_tiled"], 64, ip_evac, f"m{b}ip")
    k.barrier()

    with scoped(k) as kp:
        us = [kp.sb(f"m{b}_us{j}", [128, T + 4], BF16) for j in range(2)]
        u2s = [kp.sb(f"m{b}_u2s{j}", [128, T], BF16) for j in range(2)]
        dw = kp.sb(f"m{b}_dw", [128, 128], F32)
        diag = kp.sb(f"m{b}_diag", [128, 4 * 128], BF16)
        psc = [kp.ps(f"m{b}_psc{j}", [128, 512]) for j in range(4)]
        u_t = io["u_hbm"].rearrange("(i p) l -> i p l", p=128)
        u2_t = io["u2_hbm"].rearrange("(i p) l -> i p l", p=128)

        @block.sync
        def _(sy):
            k.dma(sy, dw[:, 0:NDT * 4], io["convw_tiled"], "qs")
        k.barrier()
        hist = {}
        for dt_ in range(NDT):
            buf = dt_ % 2

            @block.sync
            def _(sy, dt_=dt_, buf=buf):
                if dt_ >= 2:
                    sy.wait_ge(k.sem("mt"), hist[("cmm", dt_ - 2)])
                k.dma(sy, us[buf][:, 3:T + 3], u_t[dt_], "qs")
                k.dma(sy, us[buf][:, 0:3], io["zeros4"][0:128, 0:3], "qs")
            u_ld = k.cnt["qs"]

            @block.vector
            def _(v, dt_=dt_, buf=buf):
                if dt_ >= 1:
                    v.wait_ge(k.sem("mt"), hist[("cmm", dt_ - 1)])
                for tp in range(4):
                    inst = v.tensor_scalar(diag[:, tp * 128:(tp + 1) * 128],
                                           cst["ident_f32"][:],
                                           dw[:, dt_ * 4 + tp:dt_ * 4 + tp + 1], None, ALU.mult)
                k.inc("mv", inst)
            dg_done = k.cnt["mv"]

            @block.tensor
            def _(t, dt_=dt_, buf=buf):
                t.wait_ge(k.sem("mv"), dg_done)
                t.wait_ge(k.sem("qs"), u_ld)
                if dt_ >= 2:
                    t.wait_ge(k.sem("ms"), hist[("cevac", dt_ - 2)])
                for lc in range(4):
                    for tp in range(4):
                        mm = t.matmul(psc[lc][:], diag[:, tp * 128:(tp + 1) * 128],
                                      us[buf][:, lc * 512 + tp: lc * 512 + tp + 512],
                                      start=(tp == 0), stop=(tp == 3))
                k.inc("mt", mm)
            hist[("cmm", dt_)] = k.cnt["mt"]
            cm = k.cnt["mt"]

            @block.scalar
            def _(s, dt_=dt_, buf=buf):
                s.wait_ge(k.sem("mt"), cm)
                if dt_ >= 2:
                    s.wait_ge(k.sem("qs"), hist[("cst", dt_ - 2)])
                for lc in range(4):
                    inst = s.activation(u2s[buf][:, lc * 512:(lc + 1) * 512],
                                        psc[lc][:], AF.Silu, bias=cst["convb"][:, dt_:dt_ + 1])
                k.inc("ms", inst)
            hist[("cevac", dt_)] = k.cnt["ms"]
            ce = k.cnt["ms"]

            @block.sync
            def _(sy, dt_=dt_, buf=buf):
                sy.wait_ge(k.sem("ms"), ce)
                k.dma(sy, u2_t[dt_], u2s[buf][:], "qs")
            hist[("cst", dt_)] = k.cnt["qs"]
    k.barrier()

    with scoped(k) as kouter:
      ssm1 = kouter.sb(f"m{b}_ssm1", [128, T], BF16)
      ssm2 = kouter.sb(f"m{b}_ssm2", [128, T], BF16)
      with scoped(k) as kp:
        xpw = kp.sb(f"m{b}_xpw", [128, NDT * 256], BF16)
        u2sx = [kp.sb(f"m{b}_u2x{j}", [128, T], BF16) for j in range(2)]
        psx = [kp.ps(f"m{b}_psx{j}", [128, 512]) for j in range(8)]
        u2_t = io["u2_hbm"].rearrange("(i p) l -> i p l", p=128)

        @block.gpsimd
        def _(g):
            k.dma(g, xpw[:], io["xpT_tiled"], "qg")
        w_ld = k.cnt["qg"]
        hist = {}
        for kk in range(NDT):
            buf = kk % 2

            @block.sync
            def _(sy, kk=kk, buf=buf):
                if kk >= 2:
                    sy.wait_ge(k.sem("mt"), hist[("xmm", kk - 2)])
                k.dma(sy, u2sx[buf][:], u2_t[kk], "qs")
            u_ld = k.cnt["qs"]

            @block.tensor
            def _(t, kk=kk, buf=buf):
                t.wait_ge(k.sem("qs"), u_ld)
                if kk == 0:
                    t.wait_ge(k.sem("qg"), w_ld)
                for blk in range(2):
                    for lc in range(4):
                        mm = t.matmul(psx[blk * 4 + lc][:],
                                      xpw[:, kk * 256 + blk * 128: kk * 256 + (blk + 1) * 128],
                                      u2sx[buf][:, lc * 512:(lc + 1) * 512],
                                      start=(kk == 0), stop=(kk == NDT - 1))
                k.inc("mt", mm)
            hist[("xmm", kk)] = k.cnt["mt"]
        xp_done = k.cnt["mt"]

        @block.scalar
        def _(s):
            s.wait_ge(k.sem("mt"), xp_done)
            for lc in range(4):
                s.copy(ssm1[:, lc * 512:(lc + 1) * 512], psx[lc][:])
                inst = s.copy(ssm2[:, lc * 512:(lc + 1) * 512], psx[4 + lc][:])
            k.inc("ms", inst)
      k.barrier()

      with scoped(k) as kp2:
        phase_scan(k, kp2, b, io, cst, ssm1, ssm2)
    k.barrier()


def phase_scan(k, kp, b, io, cst, ssm1, ssm2):
    nc, block = k.nc, k.block
    NDT = 32
    Bb = [kp.sb(f"s{b}_Bb{j}", [128, 16 * 512], BF16) for j in range(2)]
    Cb = [kp.sb(f"s{b}_Cb{j}", [128, 16 * 512], BF16) for j in range(2)]
    dts = kp.sb(f"s{b}_dts", [128, 512], F32)
    esb = kp.sb(f"s{b}_esb", [128, 512], F32)
    dtu = kp.sb(f"s{b}_dtu", [128, 512], BF16)
    dAb = [kp.sb(f"s{b}_dAb{j}", [128, 512], BF16) for j in range(2)]
    beta = kp.sb(f"s{b}_beta", [128, 512], BF16)
    S_all = kp.sb(f"s{b}_S", [128, 16 * 512], BF16)
    P_all = kp.sb(f"s{b}_P", [128, 16 * 512], BF16)
    carry = kp.sb(f"s{b}_carry", [128, NDT * 16], F32)
    u2sl = [kp.sb(f"s{b}_u2sl{j}", [128, 512], BF16) for j in range(2)]
    gsl = [kp.sb(f"s{b}_gsl{j}", [128, 512], BF16) for j in range(2)]
    yt = kp.sb(f"s{b}_yt", [128, 512], F32)
    yg_all = kp.sb(f"s{b}_yg", [128, NDT * 512], BF16)
    opw = [kp.sb(f"s{b}_opw{j}", [128, NDT * 128], BF16) for j in range(2)]
    sgst = [kp.sb(f"s{b}_sgst{j}", [128, 512], BF16) for j in range(2)]
    sstg = [kp.sb(f"s{b}_sstg{j}", [128, 512], BF16) for j in range(2)]
    ps_bc = [kp.ps(f"s{b}_psbc{j}", [128, 512]) for j in range(2)]
    ps_dt = [kp.ps(f"s{b}_psdt{j}", [128, 512]) for j in range(2)]
    ps_y = [kp.ps(f"s{b}_psy{j}", [128, 512]) for j in range(2)]
    ps_o = [kp.ps(f"s{b}_pso{j}", [128, 512]) for j in range(2)]
    u2_tl = io["u2_hbm"].rearrange("(i p) (c f) -> i c p f", p=128, f=512)
    g_tl = io["gate_hbm"].rearrange("(i p) (c f) -> i c p f", p=128, f=512)
    s_hbm = io["s_hbm"]

    hist = {}

    def emit_bc(lcx):
        par = lcx % 2
        for n in range(16):
            pb = n % 2

            @block.tensor
            def _(t, lcx=lcx, n=n, pb=pb):
                if ("bcf", pb) in hist:
                    t.wait_ge(k.sem("ms"), hist[("bcf", pb)])
                mm = t.matmul(ps_bc[pb][:], cst["ohB"][:, n * 128:(n + 1) * 128],
                              ssm2[0:32, lcx * 512:(lcx + 1) * 512], start=True, stop=True,
                              skip_group_check=True)
                k.inc("mt", mm)
            c1 = k.cnt["mt"]

            @block.scalar
            def _(s, lcx=lcx, n=n, pb=pb, par=par):
                s.wait_ge(k.sem("mt"), c1)
                k.inc("ms", s.copy(Bb[par][:, n * 512:(n + 1) * 512], ps_bc[pb][:]))
            hist[("bcf", pb)] = k.cnt["ms"]

            @block.tensor
            def _(t, lcx=lcx, n=n, pb=pb):
                t.wait_ge(k.sem("ms"), hist[("bcf", pb)])
                mm = t.matmul(ps_bc[pb][:], cst["ohC"][:, n * 128:(n + 1) * 128],
                              ssm2[0:32, lcx * 512:(lcx + 1) * 512], start=True, stop=True,
                              skip_group_check=True)
                k.inc("mt", mm)
            c2 = k.cnt["mt"]

            @block.scalar
            def _(s, lcx=lcx, n=n, pb=pb, par=par):
                s.wait_ge(k.sem("mt"), c2)
                k.inc("ms", s.copy(Cb[par][:, n * 512:(n + 1) * 512], ps_bc[pb][:]))
            hist[("bcf", pb)] = k.cnt["ms"]

    for lc in range(4):
        if lc == 0:
            emit_bc(0)
        bc_done = k.cnt["ms"]

        for dt_ in range(NDT):
            pb = dt_ % 2
            ub = dt_ % 2

            @block.sync
            def _(sy, lc=lc, dt_=dt_, ub=ub):
                if ("uld", ub) in hist:
                    sy.wait_ge(k.sem("mv"), hist[("uld", ub)])
                k.dma(sy, u2sl[ub][:], u2_tl[dt_, lc], "qs")
                if ("gld", ub) in hist:
                    sy.wait_ge(k.sem("ms"), hist[("gld", ub)])
                k.dma(sy, gsl[ub][:], g_tl[dt_, lc], "qs")
            u_ld = k.cnt["qs"]

            @block.tensor
            def _(t, lc=lc, dt_=dt_, pb=pb):
                if ("dtf", pb) in hist:
                    t.wait_ge(k.sem("ms"), hist[("dtf", pb)])
                mm = t.matmul(ps_dt[pb][:], cst["dtwT"][:, dt_ * 128:(dt_ + 1) * 128],
                              ssm1[:, lc * 512:(lc + 1) * 512], start=True, stop=True)
                k.inc("mt", mm)
            c3 = k.cnt["mt"]

            @block.scalar
            def _(s, lc=lc, dt_=dt_, pb=pb):
                s.wait_ge(k.sem("mt"), c3)
                s.activation(esb[:], ps_dt[pb][:], AF.Exp, bias=cst["dtb"][:, dt_:dt_ + 1])
                k.inc("ms", s.activation(dts[:], esb[:], AF.Ln, bias=1.0))
            hist[("dtf", pb)] = k.cnt["ms"]
            c4 = k.cnt["ms"]

            @block.vector
            def _(v, lc=lc, dt_=dt_, ub=ub):
                v.wait_ge(k.sem("ms"), c4)
                v.wait_ge(k.sem("qs"), u_ld)
                k.inc("mv", v.tensor_tensor(dtu[:], dts[:], u2sl[ub][:], ALU.mult))
            c5 = k.cnt["mv"]

            for n in range(16):
                nb = n % 2

                @block.scalar
                def _(s, lc=lc, dt_=dt_, n=n, nb=nb):
                    if n == 0:
                        s.wait_ge(k.sem("mv"), c5)
                    if ("scanuse", nb) in hist:
                        s.wait_ge(k.sem("mv"), hist[("scanuse", nb)])
                    k.inc("ms", s.activation(dAb[nb][:], dts[:], AF.Exp,
                                             scale=cst["A_sb"][:, dt_ * 16 + n:dt_ * 16 + n + 1]))
                c6 = k.cnt["ms"]

                @block.vector
                def _(v, lc=lc, dt_=dt_, n=n, nb=nb):
                    v.wait_ge(k.sem("ms"), c6)
                    if n == 0 and ("ylast",) in hist:
                        v.wait_ge(k.sem("mt"), hist[("ylast",)])
                    v.tensor_tensor(beta[:], dtu[:], Bb[lc % 2][:, n * 512:(n + 1) * 512],
                                    ALU.mult)
                    if lc == 0:
                        init = 0.0
                    else:
                        init = carry[:, dt_ * 16 + n:dt_ * 16 + n + 1]
                    v.tensor_tensor_scan(S_all[:, n * 512:(n + 1) * 512], dAb[nb][:], beta[:],
                                         init, ALU.mult, ALU.add)
                    k.inc("mv", v.tensor_tensor(P_all[:, n * 512:(n + 1) * 512],
                                                S_all[:, n * 512:(n + 1) * 512],
                                                Cb[lc % 2][:, n * 512:(n + 1) * 512],
                                                ALU.mult))
                hist[("scanuse", nb)] = k.cnt["mv"]
                hist[("lastscan",)] = k.cnt["mv"]
                c7 = k.cnt["mv"]

                @block.tensor
                def _(t, lc=lc, dt_=dt_, n=n, pb=pb):
                    t.wait_ge(k.sem("mv"), c7)
                    if n == 0 and ("yfree", pb) in hist:
                        t.wait_ge(k.sem("mv"), hist[("yfree", pb)])
                    mm = t.matmul(ps_y[pb][:], cst["ident_bf"][:],
                                  P_all[:, n * 512:(n + 1) * 512],
                                  start=(n == 0), stop=(n == 15), skip_group_check=True)
                    k.inc("mt", mm)
            y_done = k.cnt["mt"]

            @block.scalar
            def _(s, lc=lc, dt_=dt_):
                s.wait_ge(k.sem("mt"), y_done)
                k.inc("ms", s.copy(carry[:, dt_ * 16:(dt_ + 1) * 16],
                                   S_all.ap()[:, 511:16 * 512:512]))
            carry_done = k.cnt["ms"]

            @block.scalar
            def _(s, lc=lc, dt_=dt_, ub=ub):
                s.wait_ge(k.sem("qs"), u_ld)
                k.inc("ms", s.activation(sgst[ub][:], gsl[ub][:], AF.Silu))
            hist[("gld", ub)] = k.cnt["ms"]
            sg_done = k.cnt["ms"]

            @block.vector
            def _(v, lc=lc, dt_=dt_, ub=ub, pb=pb):
                v.wait_ge(k.sem("mt"), y_done)
                v.wait_ge(k.sem("ms"), sg_done)
                if ("ygf",) in hist:
                    v.wait_ge(k.sem("mt"), hist[("ygf",)])
                v.tensor_scalar(yt[:], u2sl[ub][:], cst["Dp"][:, dt_:dt_ + 1], None, ALU.mult)
                v.tensor_tensor(yt[:], yt[:], ps_y[pb][:], ALU.add)
                k.inc("mv", v.tensor_tensor(yg_all[:, dt_ * 512:(dt_ + 1) * 512], yt[:],
                                            sgst[ub][:], ALU.mult))
            hist[("yfree", pb)] = k.cnt["mv"]
            hist[("uld", ub)] = k.cnt["mv"]
            hist[("ylast",)] = y_done
        yg_done = k.cnt["mv"]

        if lc < 3:
            emit_bc(lc + 1)

        for Dmt in range(16):
            pb = Dmt % 2

            @block.gpsimd
            def _(g, lc=lc, Dmt=Dmt, pb=pb):
                if ("omm", pb) in hist:
                    g.wait_ge(k.sem("mt"), hist[("omm", pb)])
                k.dma(g, opw[pb][:], io["opT_tiled"][Dmt], "qg")
            ow_ld = k.cnt["qg"]

            @block.tensor
            def _(t, lc=lc, Dmt=Dmt, pb=pb):
                t.wait_ge(k.sem("qg"), ow_ld)
                t.wait_ge(k.sem("mv"), yg_done)
                if ("sof", pb) in hist:
                    t.wait_ge(k.sem("ms"), hist[("sof", pb)])
                for kk in range(NDT):
                    mm = t.matmul(ps_o[pb][:],
                                  opw[pb][:, kk * 128:(kk + 1) * 128],
                                  yg_all[:, kk * 512:(kk + 1) * 512],
                                  start=(kk == 0), stop=(kk == NDT - 1))
                k.inc("mt", mm)
            hist[("omm", pb)] = k.cnt["mt"]
            if Dmt == 15:
                hist[("ygf",)] = k.cnt["mt"]
            c8 = k.cnt["mt"]

            @block.scalar
            def _(s, lc=lc, Dmt=Dmt, pb=pb):
                s.wait_ge(k.sem("mt"), c8)
                if ("sost", pb) in hist:
                    s.wait_ge(k.sem("qs"), hist[("sost", pb)])
                k.inc("ms", s.copy(sstg[pb][:], ps_o[pb][:]))
            hist[("sof", pb)] = k.cnt["ms"]
            c9 = k.cnt["ms"]

            @block.sync
            def _(sy, lc=lc, Dmt=Dmt, pb=pb):
                sy.wait_ge(k.sem("ms"), c9)
                k.dma(sy, s_hbm[Dmt * 128:(Dmt + 1) * 128, lc * 512:(lc + 1) * 512],
                      sstg[pb][:], "qs")
            hist[("sost", pb)] = k.cnt["qs"]


def build_l2_mamba_only(nc, dbg=False, xn_out=False, k=None, io=None):
    ext = {}

    def ein(name, shape, dt=F32):
        if io is not None:
            ext[name] = io[name]
            return ext[name]
        ext[name] = nc.dram_tensor(name, shape, dt, kind="ExternalInput").ap()
        return ext[name]

    def edecl(name, shape, dt=F32, kind="Internal"):
        if io is not None:
            return io[name]
        return nc.dram_tensor(name, shape, dt, kind=kind).ap()

    x = ein("x", [TOK, D])
    ipT_tiled = ein("ipT_tiled", [64, 128, 2048])
    convw_tiled = ein("convw_tiled", [128, 128])
    xpT_tiled = ein("xpT_tiled", [128, 32 * 256])
    dtwT = ein("dtwT_h", [128, 4096])
    dtb = ein("dtb_h", [128, 32])
    alog = ein("alog_h", [128, 512])
    dp = ein("dp_h", [128, 32])
    convb = ein("convb_h", [128, 32])
    opT_tiled = ein("opT_tiled", [16, 128, 4096])
    ohB_h = ein("ohB", [32, 16 * 128], BF16)
    ohC_h = ein("ohC", [32, 16 * 128], BF16)
    zeros4 = ein("zeros4", [128, 4], BF16)
    ident_bf_h = ein("ident_bf", [128, 128], BF16)
    ident_f32_h = ein("ident_f32", [128, 128])
    s_out = edecl("s_out", [2 * D, T], BF16, "ExternalOutput")

    ikind = "ExternalOutput" if dbg else "Internal"
    xn_bf = edecl("xn_bf", [TOK, D], BF16,
                  "ExternalOutput" if (dbg or xn_out) else "Internal")
    u_hbm = edecl("u_sc", [4096, T], BF16, ikind)
    gate_hbm = edecl("gate_sc", [4096, T], BF16, ikind)
    u2_hbm = edecl("u2_sc", [4096, T], BF16, ikind)

    es = contextlib.ExitStack()
    with es:
        if k is None:
            block = es.enter_context(nc.Block())
            k = K(nc, block, es)
            for s_ in ["qs", "qg", "mt", "mv", "ms", "mg"]:
                k.sem(s_)
        else:
            block = k.block
            k = K.sub(k, es)
        cst = {}
        cst["ident_bf"] = k.sb("c_identbf", [128, 128], BF16)
        cst["ident_f32"] = k.sb("c_identf32", [128, 128], F32)
        cst["ohB"] = k.sb("c_ohB", [32, 16 * 128], BF16)
        cst["ohC"] = k.sb("c_ohC", [32, 16 * 128], BF16)
        cst["dtwT"] = k.sb("c_dtwT", [128, 4096], BF16)
        cst["dtb"] = k.sb("c_dtb", [128, 32], F32)
        cst["Dp"] = k.sb("c_dp", [128, 32], F32)
        cst["convb"] = k.sb("c_convb", [128, 32], F32)
        cst["A_sb"] = k.sb("c_asb", [128, 512], F32)
        atmp = k.sb("c_atmp", [128, 512], F32)

        @block.sync
        def _(sy):
            k.dma(sy, cst["ident_bf"][:], ident_bf_h, "qs")
            k.dma(sy, cst["ident_f32"][:], ident_f32_h, "qs")
            k.dma(sy, cst["ohB"][:], ohB_h, "qs")
            k.dma(sy, cst["ohC"][:], ohC_h, "qs")
            k.dma(sy, cst["dtb"][:], dtb, "qs")
            k.dma(sy, cst["Dp"][:], dp, "qs")
            k.dma(sy, cst["convb"][:], convb, "qs")
            k.dma(sy, atmp[:], alog, "qs")

        @block.gpsimd
        def _(g):
            k.dma(g, cst["dtwT"][:], dtwT, "qg")
        k.barrier()

        @block.scalar
        def _(s):
            k.inc("ms", s.activation(cst["A_sb"][:], atmp[:], AF.Exp))
        ca = k.cnt["ms"]

        @block.vector
        def _(v):
            v.wait_ge(k.sem("ms"), ca)
            k.inc("mv", v.tensor_scalar(cst["A_sb"][:], cst["A_sb"][:], -1.0, None, ALU.mult))
        k.barrier()

        with scoped(k) as kp:
            phase_ln(kp, x, out_bf_hbm=xn_bf, eps=1e-5, tag="ln1")
        k.barrier()

        for b in range(2):
            pio = dict(xn_bf=xn_bf, u_hbm=u_hbm, gate_hbm=gate_hbm, u2_hbm=u2_hbm,
                       ipT_tiled=ipT_tiled, convw_tiled=convw_tiled,
                       xpT_tiled=xpT_tiled, opT_tiled=opT_tiled, zeros4=zeros4,
                       s_hbm=s_out[b * D:(b + 1) * D, :])
            build_l2a_mamba(k, b, pio, cst)
    return nc


def prep_l2_mamba(xc, w):
    out = {"x": np.ascontiguousarray(xc, dtype=np.float32)}
    ip = w["L2_attn_in_proj_w"]
    ipT = ip.T
    out["ipT_tiled"] = np.ascontiguousarray(
        ipT.reshape(16, 128, 64, 128).transpose(2, 1, 0, 3).reshape(64, 128, 2048))
    cw = w["L2_attn_conv_w"].reshape(4096, 4)
    out["convw_tiled"] = np.ascontiguousarray(
        cw.reshape(32, 128, 4).transpose(1, 0, 2).reshape(128, 128))
    xp = w["L2_attn_x_proj_w"]
    xpT = np.zeros((4096, 256), dtype=np.float32)
    xpT[:, 0:160] = xp.T
    out["xpT_tiled"] = np.ascontiguousarray(
        xpT.reshape(32, 128, 256).transpose(1, 0, 2).reshape(128, 32 * 256))
    out["dtwT_h"] = np.ascontiguousarray(w["L2_attn_dt_proj_w"].T)
    out["dtb_h"] = np.ascontiguousarray(
        w["L2_attn_dt_proj_b"].reshape(32, 128).T)
    out["alog_h"] = np.ascontiguousarray(
        w["L2_attn_A_log"].reshape(32, 128, 16).transpose(1, 0, 2).reshape(128, 512))
    out["dp_h"] = np.ascontiguousarray(w["L2_attn_D_param"].reshape(32, 128).T)
    out["convb_h"] = np.ascontiguousarray(w["L2_attn_conv_b"].reshape(32, 128).T)
    op = w["L2_attn_out_proj_w"]
    opT = op.T
    out["opT_tiled"] = np.ascontiguousarray(
        opT.reshape(32, 128, 16, 128).transpose(2, 1, 0, 3).reshape(16, 128, 4096))
    ohB = np.zeros((32, 16 * 128), dtype=bf)
    ohC = np.zeros((32, 16 * 128), dtype=bf)
    for n in range(16):
        ohB[n, n * 128:(n + 1) * 128] = 1
        ohC[16 + n, n * 128:(n + 1) * 128] = 1
    out["ohB"] = ohB
    out["ohC"] = ohC
    out["zeros4"] = np.zeros((128, 4), dtype=bf)
    out["ident_bf"] = np.eye(128, dtype=bf)
    out["ident_f32"] = np.eye(128, dtype=np.float32)
    return out


def phase_router(k, io, cst, tag="rt"):
    nc, block = k.nc, k.block

    with scoped(k) as kp:
        sc_sb = cst["scores_sb"]
        stg = [kp.sb(tag + f"_stg{j}", [128, 16], F32) for j in range(2)]
        st_hist = {}

        def evac(k, gi, ncg, ps, mm_cnt, group_pss):
            ib = gi % 2

            @block.scalar
            def _(s):
                s.wait_ge(k.sem("mt"), mm_cnt)
                if ("st", ib) in st_hist:
                    s.wait_ge(k.sem("qs"), st_hist[("st", ib)])
                s.copy(stg[ib][:], ps[:, 0:16])
                k.inc("ms", s.copy(sc_sb[:, gi:gi + 1], ps[:, 0:1]))
            c1 = k.cnt["ms"]

            @block.sync
            def _(sy):
                sy.wait_ge(k.sem("ms"), c1)
                k.dma(sy, io["scores_hbm"][gi * 128:(gi + 1) * 128, :], stg[ib][:], "qs")
            st_hist[("st", ib)] = k.cnt["qs"]

        phase_proj(kp, io["xn_bf"], io["wrT"], 512, evac, tag + "p", evac_sem="ms")
    k.barrier()

    with scoped(k) as kp:
        lo = kp.sb(tag + "_lo", [128, 2], F32)
        hi = kp.sb(tag + "_hi", [128, 2], F32)
        lo2 = kp.sb(tag + "_lo2", [128, 2], F32)
        hi2 = kp.sb(tag + "_hi2", [128, 2], F32)
        thrsum = kp.sb(tag + "_thrs", [128, 2], F32)
        thr2 = kp.sb(tag + "_thr2", [128, 2], F32)
        gt = kp.sb(tag + "_gt", [128, 32], BF16)
        cnt2 = kp.sb(tag + "_cnt2", [128, 2], F32)
        acc_t = kp.sb(tag + "_acct", [128, 16], F32)
        ge = kp.sb(tag + "_ge", [128, 2], F32)
        nge = kp.sb(tag + "_nge", [128, 2], F32)
        ta = kp.sb(tag + "_ta", [128, 4], F32)
        tb = kp.sb(tag + "_tb", [128, 4], F32)
        ta2 = kp.sb(tag + "_ta2", [128, 4], F32)
        tb2 = kp.sb(tag + "_tb2", [128, 4], F32)
        tc2 = kp.sb(tag + "_tc2", [128, 4], F32)
        ps_c = kp.ps(tag + "_psc", [128, 512])

        @block.sync
        def _(sy):
            k.dma(sy, lo[:], io["init_lo"], "qs")
            k.dma(sy, hi[:], io["init_hi"], "qs")
        k.barrier()

        for it in range(52):
            @block.scalar
            def _(s, it=it):
                s.copy(lo2[:], lo[:])
                k.inc("ms", s.copy(hi2[:], hi[:]))
            c0 = k.cnt["ms"]

            @block.vector
            def _(v, it=it):
                v.wait_ge(k.sem("ms"), c0)
                k.inc("mv", v.tensor_tensor(thrsum[:], lo2[:], hi2[:], ALU.add))
            c1 = k.cnt["mv"]

            @block.scalar
            def _(s, it=it):
                s.wait_ge(k.sem("mv"), c1)
                k.inc("ms", s.activation(thr2[:], thrsum[:], AF.Copy, scale=0.5))
            c2 = k.cnt["ms"]

            @block.vector
            def _(v, it=it):
                v.wait_ge(k.sem("ms"), c2)
                for b in range(2):
                    inst = v.tensor_scalar(gt[:, b * 16:(b + 1) * 16],
                                           sc_sb[:, b * 16:(b + 1) * 16],
                                           thr2[:, b:b + 1], None, ALU.is_gt)
                k.inc("mv", inst)
            c3 = k.cnt["mv"]

            @block.tensor
            def _(t, it=it):
                t.wait_ge(k.sem("mv"), c3)
                k.inc("mt", t.matmul(ps_c[:, 0:32], cst["ones128_bf"][:], gt[:],
                                     start=True, stop=True))
            c4 = k.cnt["mt"]

            @block.scalar
            def _(s, it=it):
                s.wait_ge(k.sem("mt"), c4)
                for b in range(2):
                    inst = s.activation(acc_t[:], ps_c[:, b * 16:(b + 1) * 16],
                                        AF.Copy, accum_out=cnt2[:, b:b + 1])
                k.inc("ms", inst)
            c5 = k.cnt["ms"]

            @block.vector
            def _(v, it=it):
                v.wait_ge(k.sem("ms"), c5)
                v.tensor_scalar(ge[:], cnt2[:], 256.0, None, ALU.is_ge)
                k.inc("mv", v.tensor_scalar(nge[:], cnt2[:], 256.0, None, ALU.is_lt))
            c6 = k.cnt["mv"]

            @block.scalar
            def _(s, it=it):
                s.wait_ge(k.sem("mv"), c6)
                s.copy(ta[:, 0:2], ge[:])
                k.inc("ms", s.copy(ta[:, 2:4], nge[:]))
            c7 = k.cnt["ms"]

            @block.vector
            def _(v, it=it):
                v.wait_ge(k.sem("ms"), c7)
                v.tensor_tensor(tb[:, 0:2], ta[:, 0:2], thr2[:], ALU.mult)
                v.tensor_tensor(tb[:, 2:4], ta[:, 2:4], lo2[:], ALU.mult)
                v.tensor_tensor(ta2[:, 0:2], ta[:, 2:4], thr2[:], ALU.mult)
                k.inc("mv", v.tensor_tensor(ta2[:, 2:4], ta[:, 0:2], hi2[:], ALU.mult))
            c8 = k.cnt["mv"]

            @block.scalar
            def _(s, it=it):
                s.wait_ge(k.sem("mv"), c8)
                s.copy(tb2[:], tb[:])
                k.inc("ms", s.copy(tc2[:], ta2[:]))
            c9 = k.cnt["ms"]

            @block.vector
            def _(v, it=it):
                v.wait_ge(k.sem("ms"), c9)
                v.tensor_tensor(lo[:], tb2[:, 0:2], tb2[:, 2:4], ALU.add)
                k.inc("mv", v.tensor_tensor(hi[:], tc2[:, 0:2], tc2[:, 2:4], ALU.add))
        k.barrier()

        selm = kp.sb(tag + "_selm", [128, 32], F32)
        selmb = kp.sb(tag + "_selmb", [128, 32], BF16)
        selm2 = kp.sb(tag + "_selm2", [128, 32], F32)
        pos = kp.sb(tag + "_pos", [128, 32], F32)
        pos2 = kp.sb(tag + "_pos2", [128, 32], F32)
        pos2b = kp.sb(tag + "_pos2b", [128, 32], F32)
        toff = kp.sb(tag + "_toff", [128, 32], F32)
        toff2 = kp.sb(tag + "_toff2", [128, 32], F32)
        cutm = kp.sb(tag + "_cutm", [128, 32], F32)
        cutm2 = kp.sb(tag + "_cutm2", [128, 32], F32)
        selmf = kp.sb(tag + "_selmf", [128, 32], F32)
        selm2f = kp.sb(tag + "_selm2f", [128, 32], F32)
        tot_row = kp.sb(tag + "_tot", [128, 32], F32)
        off_sb = kp.sb(tag + "_off", [128, 32], F32)
        offr = kp.sb(tag + "_offr", [1, 32], F32)
        Mt = kp.sb(tag + "_M", [128, 256], F32)
        idxf = cst["idxf"]
        ps_p = kp.ps(tag + "_psp", [128, 512])
        ps_i = [kp.ps(tag + f"_psi{j}", [128, 512]) for j in range(4)]

        @block.scalar
        def _(s):
            k.inc("ms", s.copy(lo2[:], lo[:]))
        c5 = k.cnt["ms"]

        @block.vector
        def _(v):
            v.wait_ge(k.sem("ms"), c5)
            for b in range(2):
                inst = v.tensor_scalar(selm[:, b * 16:(b + 1) * 16],
                                       sc_sb[:, b * 16:(b + 1) * 16],
                                       lo2[:, b:b + 1], None, ALU.is_gt)
            k.inc("mv", inst)
        c6 = k.cnt["mv"]

        @block.scalar
        def _(s):
            s.wait_ge(k.sem("mv"), c6)
            s.copy(selm2[:], selm[:])
            k.inc("ms", s.copy(selmb[:], selm[:]))
        c6b = k.cnt["ms"]

        @block.tensor
        def _(t):
            t.wait_ge(k.sem("ms"), c6b)
            k.inc("mt", t.matmul(ps_p[:, 0:32], cst["ltstrict_bf"][:], selmb[:],
                                 start=True, stop=True))
            k.inc("mt", t.matmul(ps_p[:, 32:64], cst["ones128_bf"][:], selmb[:],
                                 start=True, stop=True))
        c7 = k.cnt["mt"]

        @block.scalar
        def _(s):
            s.wait_ge(k.sem("mt"), c7)
            s.copy(pos[:], ps_p[:, 0:32])
            k.inc("ms", s.copy(tot_row[:], ps_p[:, 32:64]))
        c8 = k.cnt["ms"]

        @block.vector
        def _(v):
            v.wait_ge(k.sem("ms"), c8)
            for b in range(2):
                inst = v.tensor_tensor_scan(off_sb[0:1, b * 16:(b + 1) * 16],
                                            cst["onesrow32"][0:1, 0:16],
                                            tot_row[0:1, b * 16:(b + 1) * 16], 0.0,
                                            ALU.mult, ALU.add)
            k.inc("mv", inst)
        c9 = k.cnt["mv"]

        @block.scalar
        def _(s):
            s.wait_ge(k.sem("mv"), c9)
            k.inc("ms", s.copy(offr[:], off_sb[0:1, 0:32]))
        c9b = k.cnt["ms"]

        @block.tensor
        def _(t):
            t.wait_ge(k.sem("ms"), c9b)
            k.inc("mt", t.matmul(ps_p[:, 64:96], cst["ones_rowf"][0:1, :],
                                 offr[0:1, 0:32], start=True, stop=True))
        c10 = k.cnt["mt"]

        @block.vector
        def _(v):
            v.wait_ge(k.sem("mt"), c10)
            k.inc("mv", v.tensor_tensor(toff[:], ps_p[:, 64:96], tot_row[:],
                                        ALU.subtract))
        c10b = k.cnt["mv"]

        @block.scalar
        def _(s):
            s.wait_ge(k.sem("mv"), c10b)
            k.inc("ms", s.copy(toff2[:], toff[:]))
        c10c = k.cnt["ms"]

        @block.vector
        def _(v):
            v.wait_ge(k.sem("ms"), c10c)
            k.inc("mv", v.tensor_tensor(pos2[:], toff2[:], pos[:], ALU.add))
        c11 = k.cnt["mv"]

        @block.scalar
        def _(s):
            s.wait_ge(k.sem("mv"), c11)
            k.inc("ms", s.copy(pos2b[:], pos2[:]))
        c11b = k.cnt["ms"]

        @block.vector
        def _(v):
            v.wait_ge(k.sem("ms"), c11b)
            k.inc("mv", v.tensor_scalar(cutm[:], pos2b[:], 256.0, None, ALU.is_lt))
        c11c = k.cnt["mv"]

        @block.scalar
        def _(s):
            s.wait_ge(k.sem("mv"), c11c)
            k.inc("ms", s.copy(cutm2[:], cutm[:]))
        c11d = k.cnt["ms"]

        @block.vector
        def _(v):
            v.wait_ge(k.sem("ms"), c11d)
            k.inc("mv", v.tensor_tensor(selmf[:], selm2[:], cutm2[:], ALU.mult))
        c12 = k.cnt["mv"]

        @block.scalar
        def _(s):
            s.wait_ge(k.sem("mv"), c12)
            k.inc("ms", s.copy(selm2f[:], selmf[:]))
        c13 = k.cnt["ms"]

        idx_hist = {}
        for g_ in range(32):
            b = g_ // 16
            gi = g_ % 16

            @block.vector
            def _(v, g_=g_):
                v.wait_ge(k.sem("ms"), c13)
                if g_ >= 1:
                    v.wait_ge(k.sem("mt"), idx_hist[g_ - 1])
                k.inc("mv", v.tensor_scalar(Mt[:], cst["iota256f"][:],
                                            pos2b[:, g_:g_ + 1], selm2f[:, g_:g_ + 1],
                                            ALU.is_equal, ALU.mult))
            cM = k.cnt["mv"]

            @block.tensor
            def _(t, g_=g_, b=b, gi=gi, cM=cM):
                t.wait_ge(k.sem("mv"), cM)
                for jb in range(2):
                    mm = t.matmul(ps_i[b * 2 + jb][:, 0:1],
                                  Mt[:, jb * 128:(jb + 1) * 128],
                                  cst["tvals"][:, g_:g_ + 1],
                                  start=(gi == 0), stop=(gi == 15),
                                  skip_group_check=True)
                k.inc("mt", mm)
            idx_hist[g_] = k.cnt["mt"]
        idx_done = k.cnt["mt"]

        @block.scalar
        def _(s):
            s.wait_ge(k.sem("mt"), idx_done)
            for c in range(4):
                s.copy(idxf[:, c:c + 1], ps_i[c][:, 0:1])
            s.activation(cst["poswf"][:, 0:2], idxf[:, 0:2], AF.Copy)
            k.inc("ms", s.activation(cst["poswf"][:, 2:4], idxf[:, 2:4], AF.Copy,
                                     bias=-2048.0))
        c14 = k.cnt["ms"]

        @block.vector
        def _(v):
            v.wait_ge(k.sem("ms"), c14)
            v.tensor_copy(cst["idxu"][:], idxf[:])
            v.tensor_copy(cst["poswu"][:], cst["poswf"][:])
            k.inc("mv", v.tensor_scalar(Mt[:], Mt[:], 1.0, None, ALU.mult))
        c15 = k.cnt["mv"]

        @block.sync
        def _(sy):
            sy.wait_ge(k.sem("mv"), c15)
            k.dma(sy, io["idx_hbm"], cst["idxu"][:], "qs")
            k.dma(sy, io["posw_hbm"], cst["poswu"][:], "qs")
            k.dma(sy, io["selm_hbm"], selm2f[:], "qs")
            if "dbg2_hbm" in io:
                k.dma(sy, io["dbg2_hbm"][:, 0:32], pos2b[:], "qs")
                k.dma(sy, io["dbg2_hbm"][:, 32:64], cutm2[:], "qs")
                k.dma(sy, io["dbg2_hbm"][:, 64:66], lo2[:], "qs")
                k.dma(sy, io["dbg2_hbm"][:, 66:98], tot_row[:], "qs")
                k.dma(sy, io["dbg2_hbm"][:, 98:130], toff2[:], "qs")
    k.barrier()


def phase_sel_gather(k, io, cst, tag="sg"):
    nc, block = k.nc, k.block
    with scoped(k) as kp:
        gx = kp.sb(tag + "_gx", [128, D], BF16)
        gc = kp.sb(tag + "_gc", [128, 1024], F32)
        gs = kp.sb(tag + "_gs", [128, 1024], F32)
        gsc = kp.sb(tag + "_gsc", [128, 16], F32)
        gcb = kp.sb(tag + "_gcb", [128, 1024], BF16)
        gsb = kp.sb(tag + "_gsb", [128, 1024], BF16)
        hist = {}
        for c in range(4):
            @block.gpsimd
            def _(g, c=c):
                if c >= 1:
                    g.wait_ge(k.sem("qs"), hist[c - 1])
                k.inc("qg", g.indirect_dma_start(
                    out=gx[:], out_offset=None, in_=io["xn_bf"],
                    in_offset=bass.IndirectOffsetOnAxis(ap=cst["idxu"][:, c:c + 1], axis=0)), 16)
                k.inc("qg", g.indirect_dma_start(
                    out=gc[:], out_offset=None, in_=io["cos_tab"],
                    in_offset=bass.IndirectOffsetOnAxis(ap=cst["poswu"][:, c:c + 1], axis=0)), 16)
                k.inc("qg", g.indirect_dma_start(
                    out=gs[:], out_offset=None, in_=io["sin_tab"],
                    in_offset=bass.IndirectOffsetOnAxis(ap=cst["poswu"][:, c:c + 1], axis=0)), 16)
                k.inc("qg", g.indirect_dma_start(
                    out=gsc[:], out_offset=None, in_=io["scores_hbm"],
                    in_offset=bass.IndirectOffsetOnAxis(ap=cst["idxu"][:, c:c + 1], axis=0)), 16)
            gl = k.cnt["qg"]

            @block.vector
            def _(v, c=c):
                v.wait_ge(k.sem("qg"), gl)
                v.tensor_copy(gcb[:], gc[:])
                k.inc("mv", v.tensor_copy(gsb[:], gs[:]))
            cv1 = k.cnt["mv"]

            @block.scalar
            def _(s, c=c):
                s.wait_ge(k.sem("qg"), gl)
                k.inc("ms", s.copy(cst["selsc"][:, c:c + 1], gsc[:, 0:1]))
            cs1 = k.cnt["ms"]

            @block.sync
            def _(sy, c=c):
                sy.wait_ge(k.sem("mv"), cv1)
                sy.wait_ge(k.sem("ms"), cs1)
                sy.wait_ge(k.sem("qg"), gl)
                k.dma(sy, io["selx_hbm"][c * 128:(c + 1) * 128, :], gx[:], "qs")
                k.dma(sy, io["cossel_hbm"][c * 128:(c + 1) * 128, :], gcb[:], "qs")
                k.dma(sy, io["sinsel_hbm"][c * 128:(c + 1) * 128, :], gsb[:], "qs")
            hist[c] = k.cnt["qs"]
        cs_all = k.cnt["ms"]

        @block.sync
        def _(sy):
            sy.wait_ge(k.sem("ms"), cs_all)
            k.dma(sy, io["selsc_hbm"], cst["selsc"][:], "qs")
    k.barrier()


def make_selv_evac(k, out_hbm, brep, tag, nch_total=2):
    nc, block = k.nc, k.block
    vst = [k.sb(tag + f"_vst{j}", [128, 512 * nch_total], BF16) for j in range(2)]
    out_t = out_hbm.rearrange("(i p) d -> i p d", p=128)
    st_hist = {}

    def evac(k, m, ncg, ps, mm_cnt, group_pss):
        nc4 = ncg % 4
        buf = m % 2

        @block.vector
        def _(v):
            v.wait_ge(k.sem("mt"), mm_cnt)
            if nc4 == 0 and m >= 2:
                v.wait_ge(k.sem("qs"), st_hist[m - 2])
            k.inc("mv", v.tensor_tensor(vst[buf][:, nc4 * 512:(nc4 + 1) * 512], ps[:],
                                        brep[:, nc4 * 512:(nc4 + 1) * 512], ALU.add))
        cp_done = k.cnt["mv"]

        if nc4 == nch_total - 1:
            @block.sync
            def _(sy):
                sy.wait_ge(k.sem("mv"), cp_done)
                k.dma(sy, out_t[m][:, 0:512 * nch_total], vst[buf][:, 0:512 * nch_total], "qs")
            st_hist[m] = k.cnt["qs"]

    return evac


def phase_rms(k, in_hbm, out_hbm, wrep, ntiles, eps, tag):
    nc, block = k.nc, k.block
    with scoped(k) as kp:
        xb = [kp.sb(tag + f"_x{j}", [128, D], BF16) for j in range(ntiles)]
        sq = kp.sb(tag + "_sq", [128, D], F32)
        ssq = kp.sb(tag + "_ssq", [128, ntiles], F32)
        veps = kp.sb(tag + "_veps", [128, ntiles], F32)
        lnc = kp.sb(tag + "_lnc", [128, ntiles], F32)
        rstd = kp.sb(tag + "_rstd", [128, ntiles], F32)
        t1 = kp.sb(tag + "_t1", [128, D], F32)
        ob = [kp.sb(tag + f"_o{j}", [128, D], BF16) for j in range(2)]
        hist = {}
        for i in range(ntiles):
            @block.sync
            def _(sy, i=i):
                k.dma(sy, xb[i][:], in_hbm[i * 128:(i + 1) * 128, :], "qs")
            ld = k.cnt["qs"]

            @block.scalar
            def _(s, i=i, ld=ld):
                s.wait_ge(k.sem("qs"), ld)
                k.inc("ms", s.activation(sq[:], xb[i][:], AF.Square,
                                         accum_out=ssq[:, i:i + 1]))
        csq = k.cnt["ms"]

        @block.vector
        def _(v):
            v.wait_ge(k.sem("ms"), csq)
            k.inc("mv", v.tensor_scalar(veps[:], ssq[:], 1.0 / D, eps, ALU.mult, ALU.add))
        c1 = k.cnt["mv"]

        @block.scalar
        def _(s):
            s.wait_ge(k.sem("mv"), c1)
            k.inc("ms", s.activation(lnc[:], veps[:], AF.Ln))
        c2 = k.cnt["ms"]

        @block.vector
        def _(v):
            v.wait_ge(k.sem("ms"), c2)
            k.inc("mv", v.tensor_scalar(lnc[:], lnc[:], -0.5, None, ALU.mult))
        c3 = k.cnt["mv"]

        @block.scalar
        def _(s):
            s.wait_ge(k.sem("mv"), c3)
            k.inc("ms", s.activation(rstd[:], lnc[:], AF.Exp))
        c4 = k.cnt["ms"]

        for i in range(ntiles):
            @block.vector
            def _(v, i=i):
                v.wait_ge(k.sem("ms"), c4)
                if i >= 2:
                    v.wait_ge(k.sem("qs"), hist[("st", i - 2)])
                v.tensor_scalar(t1[:], xb[i][:], rstd[:, i:i + 1], None, ALU.mult)
                k.inc("mv", v.tensor_tensor(ob[i % 2][:], t1[:], wrep[:], ALU.mult))
            cv = k.cnt["mv"]

            @block.sync
            def _(sy, i=i, cv=cv):
                sy.wait_ge(k.sem("mv"), cv)
                k.dma(sy, out_hbm[i * 128:(i + 1) * 128, :], ob[i % 2][:], "qs")
            hist[("st", i)] = k.cnt["qs"]
    k.barrier()


def build_l2b1(nc, dbg=False, k=None, io=None):
    ext = {}

    def ein(name, shape, dt=F32):
        if io is not None:
            ext[name] = io[name]
            return ext[name]
        ext[name] = nc.dram_tensor(name, shape, dt, kind="ExternalInput").ap()
        return ext[name]

    def edecl(name, shape, dt=F32, kind="Internal"):
        if io is not None:
            return io[name]
        return nc.dram_tensor(name, shape, dt, kind=kind).ap()

    xn_bf = edecl("xn_bf", [TOK, D], BF16, "ExternalInput")
    wrT = ein("wrT", [D, 512])
    cos_tab = ein("cos_tab", [T, 1024])
    sin_tab = ein("sin_tab", [T, 1024])
    init_lo = ein("init_lo", [128, 2])
    init_hi = ein("init_hi", [128, 2])
    c_ones128 = edecl("c_ones128", [128, 128], BF16, "ExternalInput")
    c_lt = edecl("c_lt", [128, 128], BF16, "ExternalInput")
    c_onesrow = ein("c_onesrow", [128, 32])
    c_onesrowf = ein("c_onesrowf", [1, 128])
    c_iota = ein("c_iota", [128, 256])
    c_tvals = ein("c_tvals", [128, 32])
    ln1rep_h = ein("ln1rep", [128, D])
    bqrep_h = ein("bqrep", [128, 2048])
    bkrep_h = ein("bkrep", [128, 1024])
    bvrep_h = ein("bvrep", [128, 1024])
    wq_h = ein("wq", [D, 2048])
    wk_h = ein("wk", [D, 1024])
    wv_h = ein("wv", [D, 1024])

    scores_hbm = edecl("scores", [TOK, 16], F32, "ExternalOutput")
    idx_hbm = edecl("idx", [128, 4], U32, "ExternalOutput")
    posw_hbm = edecl("posw", [128, 4], U32, "ExternalOutput")
    selm_hbm = edecl("selm", [128, 32], F32, "ExternalOutput")
    dbg2_hbm = (nc.dram_tensor("dbg2", [128, 130], F32, kind="ExternalOutput").ap()
                if dbg else None)
    selx_hbm = edecl("selx", [512, D], BF16, "ExternalOutput")
    selsc_hbm = edecl("selsc", [128, 4], F32, "ExternalOutput")
    qsel_hbm = edecl("qsel", [512, 2048], BF16, "ExternalOutput")
    ksel_hbm = edecl("ksel", [512, 1024], BF16, "ExternalOutput")
    vsel_hbm = edecl("vsel", [512, 1024], BF16, "ExternalOutput")
    ikind = "ExternalOutput" if dbg else "Internal"
    cossel_hbm = edecl("cossel", [512, 1024], BF16, ikind)
    sinsel_hbm = edecl("sinsel", [512, 1024], BF16, ikind)
    hsel_hbm = edecl("hsel", [512, D], BF16, ikind)

    es = contextlib.ExitStack()
    with es:
        if k is None:
            block = es.enter_context(nc.Block())
            k = K(nc, block, es)
            for s_ in ["qs", "qg", "mt", "mv", "ms", "mg"]:
                k.sem(s_)
        else:
            block = k.block
            k = K.sub(k, es)
        cst = {}
        cst["scores_sb"] = k.sb("c_scsb", [128, 32], F32)
        cst["idxf"] = k.sb("c_idxf", [128, 4], F32)
        cst["idxu"] = k.sb("c_idxu", [128, 4], U32)
        cst["poswf"] = k.sb("c_poswf", [128, 4], F32)
        cst["poswu"] = k.sb("c_poswu", [128, 4], U32)
        cst["selsc"] = k.sb("c_selsc", [128, 4], F32)
        cst["ones128_bf"] = k.sb("c_o128", [128, 128], BF16)
        cst["ltstrict_bf"] = k.sb("c_ltst", [128, 128], BF16)
        cst["onesrow32"] = k.sb("c_or32", [128, 32], F32)
        cst["ones_rowf"] = k.sb("c_orf", [1, 128], F32)
        cst["iota256f"] = k.sb("s_iota", [128, 256], F32)
        cst["tvals"] = k.sb("s_tvals", [128, 32], F32)
        ln1rep = k.sb("c_ln1r", [128, D], F32)
        bqrep = k.sb("c_bqr", [128, 2048], F32)
        bkrep = k.sb("c_bkr", [128, 1024], F32)
        bvrep = k.sb("c_bvr", [128, 1024], F32)

        @block.sync
        def _(sy):
            k.dma(sy, cst["ones128_bf"][:], c_ones128, "qs")
            k.dma(sy, cst["ltstrict_bf"][:], c_lt, "qs")
            k.dma(sy, cst["onesrow32"][:], c_onesrow, "qs")
            k.dma(sy, cst["ones_rowf"][:], c_onesrowf, "qs")
            k.dma(sy, cst["iota256f"][:], c_iota, "qs")
            k.dma(sy, cst["tvals"][:], c_tvals, "qs")
            k.dma(sy, ln1rep[:], ln1rep_h, "qs")
            k.dma(sy, bqrep[:], bqrep_h, "qs")
            k.dma(sy, bkrep[:], bkrep_h, "qs")
            k.dma(sy, bvrep[:], bvrep_h, "qs")
        k.barrier()

        pio = dict(xn_bf=xn_bf, wrT=wrT, scores_hbm=scores_hbm, init_lo=init_lo,
                   init_hi=init_hi, idx_hbm=idx_hbm, posw_hbm=posw_hbm,
                   selm_hbm=selm_hbm, cos_tab=cos_tab, sin_tab=sin_tab,
                   selx_hbm=selx_hbm, cossel_hbm=cossel_hbm, sinsel_hbm=sinsel_hbm,
                   selsc_hbm=selsc_hbm)
        if dbg2_hbm is not None:
            pio["dbg2_hbm"] = dbg2_hbm
        phase_router(k, pio, cst)
        phase_sel_gather(k, pio, cst)
        phase_rms(k, selx_hbm, hsel_hbm, ln1rep, 4, 1e-6, "rms1")

        with scoped(k) as kp:
            evac = make_qk_evac(kp, None, cossel_hbm, sinsel_hbm, qsel_hbm, 0.0,
                                "sq", rope=True, do_rms=False, nh_total=16, brep=bqrep)
            phase_proj(kp, hsel_hbm, wq_h, 2048, evac, "pq", evac_sem="mv",
                       HT=512, NTOK=512)
        k.barrier()
        with scoped(k) as kp:
            evac = make_qk_evac(kp, None, cossel_hbm, sinsel_hbm, ksel_hbm, 0.0,
                                "sk", rope=True, do_rms=False, nh_total=8, brep=bkrep)
            phase_proj(kp, hsel_hbm, wk_h, 1024, evac, "pk", evac_sem="mv",
                       HT=512, NTOK=512)
        k.barrier()
        with scoped(k) as kp:
            evac = make_selv_evac(kp, vsel_hbm, bvrep, "sv", nch_total=2)
            phase_proj(kp, hsel_hbm, wv_h, 1024, evac, "pv", evac_sem="mv",
                       HT=512, NTOK=512)
        k.barrier()
    return nc


def prep_l2b1_inputs(xn_bf16, w):
    out = {"xn_bf": np.ascontiguousarray(xn_bf16, dtype=bf)}
    wrT = np.zeros((D, 512), np.float32)
    wrT[:, 0] = w["L2_ffn_w_router"]
    out["wrT"] = wrT
    inv = 1.0 / 10000.0 ** (np.arange(0, 128, 2, dtype=np.float32) / 128)
    ang = np.arange(T, dtype=np.float32)[:, None] * inv[None, :]
    out["cos_tab"] = np.ascontiguousarray(np.tile(np.cos(ang), (1, 16)), np.float32)
    out["sin_tab"] = np.ascontiguousarray(np.tile(np.sin(ang), (1, 16)), np.float32)
    out["init_lo"] = np.full((128, 2), -1e5, np.float32)
    out["init_hi"] = np.full((128, 2), 1e5, np.float32)
    out["c_ones128"] = np.ones((128, 128), bf)
    out["c_lt"] = (np.arange(128)[:, None] < np.arange(128)[None, :]).astype(bf)
    out["c_onesrow"] = np.ones((128, 32), np.float32)
    out["c_onesrowf"] = np.ones((1, 128), np.float32)
    out["c_iota"] = np.tile(np.arange(256, dtype=np.float32)[None, :], (128, 1))
    out["c_tvals"] = np.ascontiguousarray(
        (np.arange(32)[None, :] * 128 + np.arange(128)[:, None]).astype(np.float32))
    out["ln1rep"] = np.tile(w["L2_ffn_ln1_w"][None, :], (128, 1)).astype(np.float32)
    out["bqrep"] = np.tile(w["L2_ffn_bq"][None, :], (128, 1)).astype(np.float32)
    out["bkrep"] = np.tile(w["L2_ffn_bk"][None, :], (128, 1)).astype(np.float32)
    out["bvrep"] = np.tile(w["L2_ffn_bv"][None, :], (128, 1)).astype(np.float32)
    out["wq"] = np.ascontiguousarray(w["L2_ffn_wq"], np.float32)
    out["wk"] = np.ascontiguousarray(w["L2_ffn_wk"], np.float32)
    out["wv"] = np.ascontiguousarray(w["L2_ffn_wv"], np.float32)
    return out


def phase_sel_attn(k, io, cst, tag="sa"):
    nc, block = k.nc, k.block
    nh, nkv = 16, 8
    scale = 1.0 / np.sqrt(128.0)
    SK = 4096
    NKT = SK // 128
    NKC = SK // 512
    with scoped(k) as kp:
        amq = [kp.sb(f"{tag}_amq{j}", [128, SK], BF16) for j in range(4)]
        amk = kp.sb(f"{tag}_amk", [128, NKT * 512], BF16)
        kt_sb = kp.sb(f"{tag}_kt", [128, SK], BF16)
        v_sb = kp.sb(f"{tag}_v", [128, NKT * 128], BF16)
        qt_sb = kp.sb(f"{tag}_qt", [128, 512], BF16)
        p_sb = [kp.sb(f"{tag}_p{j}", [128, 512], BF16) for j in range(2)]
        negm = kp.sb(f"{tag}_negm", [1, 512], BF16)
        drow = kp.sb(f"{tag}_drow", [1, 512], F32)
        mcol = kp.sb(f"{tag}_mcol", [128, 4], F32)
        mbig = kp.sb(f"{tag}_mbig", [128, 8], F32)
        mbig2 = kp.sb(f"{tag}_mbig2", [128, 8], F32)
        lnd = kp.sb(f"{tag}_lnd", [128, 4], F32)
        dhat = kp.sb(f"{tag}_dhat", [128, 4], F32)
        ao_sb = [kp.sb(f"{tag}_ao{j}", [128, 2048], BF16) for j in range(4)]
        ps_s = [kp.ps(f"{tag}_pss{j}", [128, 512]) for j in range(2)]
        ps_o = [kp.ps(f"{tag}_pso{j}", [128, 512]) for j in range(4)]
        ps_d = kp.ps(f"{tag}_psd", [128, 512])
        ps_sp = ps_o
        ident_bf, ident_f32 = cst["ident_bf"], cst["ident_f32"]
        ones1, ones_row1 = cst["ones1"], cst["ones_row1"]

        @block.sync
        def _(sy):
            for j in range(4):
                k.dma(sy, amq[j][:], io["amq_hbm"][j * 128:(j + 1) * 128, :], "qs")
            k.dma(sy, amk.ap().rearrange("p (j q) -> p j q", q=512),
                  io["amk_hbm"].rearrange("(j p) q -> p j q", p=128), "qs")
        k.barrier()

        hist = {}
        for h in range(nh):
            hk = h // 2

            @block.sync
            def _(sy, h=h, hk=hk):
                if h >= 1:
                    sy.wait_ge(k.sem("mt"), hist[("hmm", h - 1)])
                if h % 2 == 0:
                    k.inc("qs", sy.dma_start_transpose(
                        kt_sb[:], io["kall_hbm"][:, hk * 128:(hk + 1) * 128]), 16)
                    k.dma(sy, v_sb.ap().rearrange("p (j x) -> p j x", x=128),
                          io["vall_hbm"][:, hk * 128:(hk + 1) * 128]
                          .rearrange("(j p) x -> p j x", p=128), "qs")
                k.inc("qs", sy.dma_start_transpose(
                    qt_sb[:], io["qsel_hbm"][:, h * 128:(h + 1) * 128]), 16)
            qk_ld = k.cnt["qs"]

            for qsub in range(4):
                for kc in range(NKC):
                    pb = (qsub * NKC + kc) % 4

                    @block.tensor
                    def _(t, h=h, qsub=qsub, kc=kc, pb=pb):
                        t.wait_ge(k.sem("qs"), qk_ld)
                        if ("ofree",) in hist and qsub == 0 and kc == 0:
                            t.wait_ge(k.sem("mv"), hist[("ofree",)])
                        if ("spfree", pb) in hist:
                            t.wait_ge(k.sem("mv"), hist[("spfree", pb)])
                        mm = t.matmul(ps_sp[pb][:, 0:512],
                                      qt_sb[:, qsub * 128:(qsub + 1) * 128],
                                      kt_sb[:, kc * 512:(kc + 1) * 512],
                                      start=True, stop=False, skip_group_check=True)
                        mm = t.matmul(ps_sp[pb][:, 0:512], ident_bf[:],
                                      amq[qsub][:, kc * 512:(kc + 1) * 512],
                                      start=False, stop=True, skip_group_check=True)
                        k.inc("mt", mm)
                    sp_done = k.cnt["mt"]

                    @block.vector
                    def _(v, qsub=qsub, kc=kc, pb=pb, sp_done=sp_done):
                        v.wait_ge(k.sem("mt"), sp_done)
                        k.inc("mv", v.tensor_reduce(
                            mbig[:, kc:kc + 1], ps_sp[pb][:, 0:512], AXL.X, ALU.max))
                    hist[("spfree", pb)] = k.cnt["mv"]
                mb_done = k.cnt["mv"]

                @block.scalar
                def _(s, qsub=qsub, mb_done=mb_done):
                    s.wait_ge(k.sem("mv"), mb_done)
                    k.inc("ms", s.copy(mbig2[:], mbig[:]))
                mb2_done = k.cnt["ms"]

                @block.vector
                def _(v, qsub=qsub, mb2_done=mb2_done):
                    v.wait_ge(k.sem("ms"), mb2_done)
                    k.inc("mv", v.tensor_reduce(
                        mcol[:, qsub:qsub + 1], mbig2[:], AXL.X, ALU.max))
            mcol_done = k.cnt["mv"]

            @block.tensor
            def _(t, mcol_done=mcol_done):
                t.wait_ge(k.sem("mv"), mcol_done)
                if ("dtfree",) in hist:
                    t.wait_ge(k.sem("ms"), hist[("dtfree",)])
                if ("dtfree2",) in hist:
                    t.wait_ge(k.sem("ms"), hist[("dtfree2",)])
                for j in range(4):
                    mm = t.transpose(ps_d[0:1, j * 128:(j + 1) * 128],
                                     mcol[:, j:j + 1], ident_f32[:])
                k.inc("mt", mm)
            mt_done = k.cnt["mt"]

            @block.scalar
            def _(s, mt_done=mt_done):
                s.wait_ge(k.sem("mt"), mt_done)
                k.inc("ms", s.activation(negm[0:1, 0:512], ps_d[0:1, 0:512],
                                         AF.Copy, scale=-1.0))
            hist[("dtfree",)] = k.cnt["ms"]
            negm_done = k.cnt["ms"]

            for kt in range(NKT):
                pb = kt % 2

                @block.tensor
                def _(t, h=h, kt=kt, pb=pb, negm_done=negm_done):
                    t.wait_ge(k.sem("ms"), negm_done)
                    if ("sfree", pb) in hist:
                        t.wait_ge(k.sem("ms"), hist[("sfree", pb)])
                    mm = t.matmul(ps_s[pb][:], kt_sb[:, kt * 128:(kt + 1) * 128],
                                  qt_sb[:, 0:512],
                                  start=True, stop=False, skip_group_check=True)
                    mm = t.matmul(ps_s[pb][:], ident_bf[:],
                                  amk[:, kt * 512:(kt + 1) * 512],
                                  start=False, stop=False, skip_group_check=True)
                    mm = t.matmul(ps_s[pb][:], ones_row1[0:1, :], negm[0:1, 0:512],
                                  start=False, stop=True, skip_group_check=True)
                    k.inc("mt", mm)
                sbar_done = k.cnt["mt"]

                @block.scalar
                def _(s, kt=kt, pb=pb, sbar_done=sbar_done):
                    s.wait_ge(k.sem("mt"), sbar_done)
                    if ("pfree", pb) in hist:
                        s.wait_ge(k.sem("mt"), hist[("pfree", pb)])
                    k.inc("ms", s.activation(p_sb[pb][:], ps_s[pb][:], AF.Exp,
                                             scale=scale))
                hist[("sfree", pb)] = k.cnt["ms"]
                p_done = k.cnt["ms"]

                @block.tensor
                def _(t, kt=kt, pb=pb, p_done=p_done):
                    t.wait_ge(k.sem("ms"), p_done)
                    if kt == 0 and ("ofree",) in hist:
                        t.wait_ge(k.sem("mv"), hist[("ofree",)])
                    if kt == 0 and ("dfree",) in hist:
                        t.wait_ge(k.sem("ms"), hist[("dfree",)])
                    if kt == 0 and ("dtfree2",) in hist:
                        t.wait_ge(k.sem("ms"), hist[("dtfree2",)])
                    mm = t.matmul(ps_d[0:1, :], ones1[:, 0:1], p_sb[pb][:],
                                  start=(kt == 0), stop=(kt == NKT - 1),
                                  skip_group_check=True)
                    for sub in range(4):
                        mm = t.matmul(ps_o[sub][:, 0:128],
                                      p_sb[pb][:, sub * 128:(sub + 1) * 128],
                                      v_sb[:, kt * 128:(kt + 1) * 128],
                                      start=(kt == 0), stop=(kt == NKT - 1),
                                      skip_group_check=True)
                    k.inc("mt", mm)
                hist[("pfree", pb)] = k.cnt["mt"]
            pv_done = k.cnt["mt"]

            @block.scalar
            def _(s, pv_done=pv_done):
                s.wait_ge(k.sem("mt"), pv_done)
                k.inc("ms", s.copy(drow[0:1, :], ps_d[0:1, 0:512]))
            hist[("dfree",)] = k.cnt["ms"]
            drow_done = k.cnt["ms"]

            @block.tensor
            def _(t, drow_done=drow_done):
                t.wait_ge(k.sem("ms"), drow_done)
                for j in range(4):
                    mm = t.transpose(ps_d[:, 448 + j: 449 + j],
                                     drow[0:1, j * 128:(j + 1) * 128],
                                     ident_f32[0:1, 0:1])
                k.inc("mt", mm)
            dt_done = k.cnt["mt"]

            @block.scalar
            def _(s, dt_done=dt_done):
                s.wait_ge(k.sem("mt"), dt_done)
                s.activation(lnd[:, 0:4], ps_d[:, 448:452], AF.Ln)
                k.inc("ms", s.activation(dhat[:, 0:4], lnd[:, 0:4], AF.Exp,
                                         scale=-1.0))
            hist[("dtfree2",)] = k.cnt["ms"]
            dhat_done = k.cnt["ms"]

            @block.vector
            def _(v, h=h, dhat_done=dhat_done):
                v.wait_ge(k.sem("ms"), dhat_done)
                for sub in range(4):
                    inst = v.tensor_scalar(ao_sb[sub][:, h * 128:(h + 1) * 128],
                                           ps_o[sub][:, 0:128],
                                           dhat[:, sub:sub + 1], None, ALU.mult)
                k.inc("mv", inst)
            hist[("ofree",)] = k.cnt["mv"]
            hist[("hmm", h)] = k.cnt["mt"]
        ao_done = k.cnt["mv"]

        @block.sync
        def _(sy):
            sy.wait_ge(k.sem("mv"), ao_done)
            for qt in range(4):
                k.dma(sy, io["attnout_hbm"][qt * 128:(qt + 1) * 128, :],
                      ao_sb[qt][:], "qs")
    k.barrier()


def phase_silu_tiles(k, in_hbm, out_hbm, ntiles, tag):
    nc, block = k.nc, k.block
    with scoped(k) as kp:
        xb = [kp.sb(tag + f"_x{j}", [128, D], BF16) for j in range(2)]
        ob = [kp.sb(tag + f"_o{j}", [128, D], BF16) for j in range(2)]
        hist = {}
        for i in range(ntiles):
            @block.sync
            def _(sy, i=i):
                if i >= 2:
                    sy.wait_ge(k.sem("ms"), hist[("sl", i - 2)])
                k.dma(sy, xb[i % 2][:], in_hbm[i * 128:(i + 1) * 128, :], "qs")
            ld = k.cnt["qs"]

            @block.scalar
            def _(s, i=i, ld=ld):
                s.wait_ge(k.sem("qs"), ld)
                if i >= 2:
                    s.wait_ge(k.sem("qs"), hist[("st", i - 2)])
                k.inc("ms", s.activation(ob[i % 2][:], xb[i % 2][:], AF.Silu))
            hist[("sl", i)] = k.cnt["ms"]
            sl = k.cnt["ms"]

            @block.sync
            def _(sy, i=i, sl=sl):
                sy.wait_ge(k.sem("ms"), sl)
                k.dma(sy, out_hbm[i * 128:(i + 1) * 128, :], ob[i % 2][:], "qs")
            hist[("st", i)] = k.cnt["qs"]
    k.barrier()


def make_x1_evac(k, selx_hbm, x1_hbm, x1b_hbm, tag):
    nc, block = k.nc, k.block
    sxt = k.sb(tag + "_sx", [128, D], F32)
    x1t = [k.sb(tag + f"_x1{j}", [128, D], F32) for j in range(2)]
    x1bt = [k.sb(tag + f"_x1b{j}", [128, D], BF16) for j in range(2)]
    ld_hist = {}
    st_hist = {}

    def evac(k, m, ncg, ps, mm_cnt, group_pss):
        nc4 = ncg % 4
        buf = m % 2
        if nc4 == 0:
            @block.gpsimd
            def _(g, m=m):
                if m >= 1:
                    g.wait_ge(k.sem("mv"), ld_hist[m - 1])
                k.dma(g, sxt[:], selx_hbm[m * 128:(m + 1) * 128, :], "qg")
        sx_ld = k.cnt["qg"]

        @block.vector
        def _(v, m=m, nc4=nc4, buf=buf, sx_ld=sx_ld, mm_cnt=mm_cnt):
            v.wait_ge(k.sem("mt"), mm_cnt)
            v.wait_ge(k.sem("qg"), sx_ld)
            if nc4 == 0 and m >= 2:
                v.wait_ge(k.sem("qs"), st_hist[m - 2])
            v.tensor_tensor(x1t[buf][:, nc4 * 512:(nc4 + 1) * 512], ps[:],
                            sxt[:, nc4 * 512:(nc4 + 1) * 512], ALU.add)
            k.inc("mv", v.tensor_scalar(x1bt[buf][:, nc4 * 512:(nc4 + 1) * 512],
                                        x1t[buf][:, nc4 * 512:(nc4 + 1) * 512],
                                        1.0, None, ALU.mult))
        ld_hist[m] = k.cnt["mv"]
        cp_done = k.cnt["mv"]

        if nc4 == 3:
            @block.sync
            def _(sy, m=m, buf=buf, cp_done=cp_done):
                sy.wait_ge(k.sem("mv"), cp_done)
                k.dma(sy, x1_hbm[m * 128:(m + 1) * 128, :], x1t[buf][:], "qs")
                k.dma(sy, x1b_hbm[m * 128:(m + 1) * 128, :], x1bt[buf][:], "qs")
            st_hist[m] = k.cnt["qs"]

    return evac


def make_gate_evac_sel(k, out_hbm, tag):
    nc, block = k.nc, k.block
    st = [k.sb(tag + f"_st{j}", [128, 512], BF16) for j in range(2)]
    st_hist = {}

    def evac(k, m, ncg, ps, mm_cnt, group_pss):
        ib = (m * 16 + ncg) % 2

        @block.scalar
        def _(s, m=m, ncg=ncg, ib=ib, mm_cnt=mm_cnt):
            s.wait_ge(k.sem("mt"), mm_cnt)
            if ("st", ib) in st_hist:
                s.wait_ge(k.sem("qs"), st_hist[("st", ib)])
            k.inc("ms", s.activation(st[ib][:], ps[:], AF.Silu))
        c1 = k.cnt["ms"]

        @block.sync
        def _(sy, m=m, ncg=ncg, ib=ib, c1=c1):
            sy.wait_ge(k.sem("ms"), c1)
            k.dma(sy, out_hbm[m * 128:(m + 1) * 128, ncg * 512:(ncg + 1) * 512],
                  st[ib][:], "qs")
        st_hist[("st", ib)] = k.cnt["qs"]

    return evac


def make_up_evac_sel(k, gate_hbm, out_hbm, tag):
    nc, block = k.nc, k.block
    gld = [k.sb(tag + f"_g{j}", [128, 512], BF16) for j in range(2)]
    st = [k.sb(tag + f"_st{j}", [128, 512], BF16) for j in range(2)]
    hist = {}

    def evac(k, m, ncg, ps, mm_cnt, group_pss):
        ib = (m * 16 + ncg) % 2

        @block.gpsimd
        def _(g, m=m, ncg=ncg, ib=ib):
            if ("mul", ib) in hist:
                g.wait_ge(k.sem("mv"), hist[("mul", ib)])
            k.dma(g, gld[ib][:],
                  gate_hbm[m * 128:(m + 1) * 128, ncg * 512:(ncg + 1) * 512], "qg")
        g_ld = k.cnt["qg"]

        @block.vector
        def _(v, m=m, ncg=ncg, ib=ib, mm_cnt=mm_cnt, g_ld=g_ld):
            v.wait_ge(k.sem("mt"), mm_cnt)
            v.wait_ge(k.sem("qg"), g_ld)
            if ("st", ib) in hist:
                v.wait_ge(k.sem("qs"), hist[("st", ib)])
            k.inc("mv", v.tensor_tensor(st[ib][:], ps[:], gld[ib][:], ALU.mult))
        hist[("mul", ib)] = k.cnt["mv"]
        c1 = k.cnt["mv"]

        @block.sync
        def _(sy, m=m, ncg=ncg, ib=ib, c1=c1):
            sy.wait_ge(k.sem("mv"), c1)
            k.dma(sy, out_hbm[m * 128:(m + 1) * 128, ncg * 512:(ncg + 1) * 512],
                  st[ib][:], "qs")
        hist[("st", ib)] = k.cnt["qs"]

    return evac


def make_proc_evac(k, x1_hbm, proc_hbm, tag):
    nc, block = k.nc, k.block
    x1t = [k.sb(tag + f"_x1{j}", [128, D], F32) for j in range(4)]
    pt = [k.sb(tag + f"_p{j}", [128, 512], F32) for j in range(2)]
    hist = {}

    @block.gpsimd
    def _(g):
        for m in range(4):
            k.dma(g, x1t[m][:], x1_hbm[m * 128:(m + 1) * 128, :], "qg")
    x1_ld = k.cnt["qg"]

    def evac(k, m, ncg, ps, mm_cnt, group_pss):
        ib = (m + ncg) % 2

        @block.vector
        def _(v, m=m, ncg=ncg, ib=ib, mm_cnt=mm_cnt):
            v.wait_ge(k.sem("mt"), mm_cnt)
            v.wait_ge(k.sem("qg"), x1_ld)
            if ("st", ib) in hist:
                v.wait_ge(k.sem("qs"), hist[("st", ib)])
            k.inc("mv", v.tensor_tensor(pt[ib][:], ps[:],
                                        x1t[m][:, ncg * 512:(ncg + 1) * 512], ALU.add))
        cp = k.cnt["mv"]

        @block.sync
        def _(sy, m=m, ncg=ncg, ib=ib, cp=cp):
            sy.wait_ge(k.sem("mv"), cp)
            k.dma(sy, proc_hbm[m * 128:(m + 1) * 128, ncg * 512:(ncg + 1) * 512],
                  pt[ib][:], "qs")
        hist[("st", ib)] = k.cnt["qs"]

    return evac


def phase_blend_scatter(k, io, cst, tag="bl"):
    nc, block = k.nc, k.block
    with scoped(k) as kp:
        proct = kp.sb(tag + "_pr", [128, D], F32)
        sxt = kp.sb(tag + "_sx", [128, D], BF16)
        sxf = kp.sb(tag + "_sxf", [128, D], F32)
        dt_ = kp.sb(tag + "_d", [128, D], F32)
        dg = kp.sb(tag + "_dg", [128, D], F32)
        gated = kp.sb(tag + "_gt", [128, D], F32)
        gs_t = kp.sb(tag + "_gs", [128, D], BF16)
        th = kp.sb(tag + "_th", [128, 4], F32)
        gate_c = kp.sb(tag + "_gc", [128, 4], F32)
        gate2 = kp.sb(tag + "_g2", [128, 4], F32)

        @block.scalar
        def _(s):
            k.inc("ms", s.activation(th[:], cst["selsc"][:], AF.Tanh, scale=0.5))
        c1 = k.cnt["ms"]

        @block.vector
        def _(v):
            v.wait_ge(k.sem("ms"), c1)
            k.inc("mv", v.tensor_scalar(gate_c[:], th[:], 0.5, 0.5, ALU.mult, ALU.add))
        c2 = k.cnt["mv"]

        @block.scalar
        def _(s):
            s.wait_ge(k.sem("mv"), c2)
            k.inc("ms", s.copy(gate2[:], gate_c[:]))
        c3 = k.cnt["ms"]

        hist = {}
        for c in range(4):
            @block.sync
            def _(sy, c=c):
                if c >= 1:
                    sy.wait_ge(k.sem("qg"), hist[c - 1])
                k.dma(sy, proct[:], io["proc_hbm"][c * 128:(c + 1) * 128, :], "qs")
                k.dma(sy, sxt[:], io["selx_hbm"][c * 128:(c + 1) * 128, :], "qs")
            ld = k.cnt["qs"]

            @block.vector
            def _(v, c=c, ld=ld):
                v.wait_ge(k.sem("qs"), ld)
                v.wait_ge(k.sem("ms"), c3)
                v.tensor_scalar(sxf[:], sxt[:], 1.0, None, ALU.mult)
                v.tensor_tensor(dt_[:], proct[:], sxf[:], ALU.subtract)
                v.tensor_scalar(dg[:], dt_[:], gate2[:, c:c + 1], None, ALU.mult)
                k.inc("mv", v.tensor_tensor(gated[:], sxf[:], dg[:], ALU.add))
            c4 = k.cnt["mv"]

            @block.scalar
            def _(s, c=c, c4=c4):
                s.wait_ge(k.sem("mv"), c4)
                k.inc("ms", s.activation(gs_t[:], gated[:], AF.Silu))
            c5 = k.cnt["ms"]

            @block.gpsimd
            def _(g, c=c, c5=c5):
                g.wait_ge(k.sem("ms"), c5)
                k.inc("qg", g.indirect_dma_start(
                    out=io["g_hbm"], out_offset=bass.IndirectOffsetOnAxis(
                        ap=cst["idxu"][:, c:c + 1], axis=0),
                    in_=gs_t[:], in_offset=None), 16)
            hist[c] = k.cnt["qg"]
    k.barrier()


def phase_residual(k, io, tag="rs"):
    nc, block = k.nc, k.block
    with scoped(k) as kp:
        xt = [kp.sb(tag + f"_x{j}", [128, D], F32) for j in range(2)]
        gt_ = [kp.sb(tag + f"_g{j}", [128, D], BF16) for j in range(2)]
        sT = [kp.sb(tag + f"_s{j}", [128, D], BF16) for j in range(2)]
        sg = kp.sb(tag + "_sg", [128, D], F32)
        xo = [kp.sb(tag + f"_o{j}", [128, D], F32) for j in range(2)]
        hist = {}
        for m in range(NTT):
            b, lt = m // 16, m % 16
            buf = m % 2

            @block.sync
            def _(sy, m=m, b=b, lt=lt, buf=buf):
                if m >= 2:
                    sy.wait_ge(k.sem("mv"), hist[("sg", m - 2)])
                k.dma(sy, xt[buf][:], io["x_hbm"][m * 128:(m + 1) * 128, :], "qs")
                k.dma(sy, gt_[buf][:], io["g_hbm"][m * 128:(m + 1) * 128, :], "qs")
                for j in range(16):
                    k.inc("qs", sy.dma_start_transpose(
                        sT[buf][:, j * 128:(j + 1) * 128],
                        io["s_hbm"][b * D + j * 128:b * D + (j + 1) * 128,
                                    lt * 128:(lt + 1) * 128]), 16)
            ld = k.cnt["qs"]

            @block.vector
            def _(v, m=m, buf=buf, ld=ld):
                v.wait_ge(k.sem("qs"), ld)
                if m >= 2:
                    v.wait_ge(k.sem("qs"), hist[("st", m - 2)])
                v.tensor_tensor(sg[:], sT[buf][:], gt_[buf][:], ALU.mult)
                k.inc("mv", v.tensor_tensor(xo[buf][:], sg[:], xt[buf][:], ALU.add))
            hist[("sg", m)] = k.cnt["mv"]
            cv = k.cnt["mv"]

            @block.sync
            def _(sy, m=m, buf=buf, cv=cv):
                sy.wait_ge(k.sem("mv"), cv)
                k.dma(sy, io["xout_hbm"][m * 128:(m + 1) * 128, :], xo[buf][:], "qs")
            hist[("st", m)] = k.cnt["qs"]
    k.barrier()


def build_l2b2(nc, core_id=0, dbg=False, k=None, io=None):
    ext = {}

    def ein(name, shape, dt=F32):
        if io is not None:
            ext[name] = io[name]
            return ext[name]
        ext[name] = nc.dram_tensor(name, shape, dt, kind="ExternalInput").ap()
        return ext[name]

    def edecl(name, shape, dt=F32, kind="Internal"):
        if io is not None:
            return io[name]
        return nc.dram_tensor(name, shape, dt, kind=kind).ap()

    xn_bf = edecl("xn_bf", [TOK, D], BF16, "ExternalInput")
    x_h = ein("x", [TOK, D])
    s_h = edecl("s_in", [2 * D, T], BF16, "ExternalInput")
    qsel = edecl("qsel", [512, 2048], BF16, "ExternalInput")
    kall = edecl("kall", [4096, 1024], BF16, "ExternalInput")
    vall = edecl("vall", [4096, 1024], BF16, "ExternalInput")
    selx = edecl("selx", [512, D], BF16, "ExternalInput")
    idx_in = edecl("idx", [128, 4], U32, "ExternalInput")
    selsc_in = ein("selsc", [128, 4])
    amq_h = edecl("amq", [512, 4096], BF16, "ExternalInput")
    amk_h = edecl("amk", [4096, 512], BF16, "ExternalInput")
    wo_h = ein("wo", [2048, 2048])
    wg_h = ein("wg", [2048, 8192])
    wu_h = ein("wu", [2048, 8192])
    wd_h = ein("wd", [8192, 2048])
    ln2rep_h = ein("ln2rep", [128, D])
    ident_bf_h = edecl("ident_bf", [128, 128], BF16, "ExternalInput")
    ident_f32_h = ein("ident_f32", [128, 128])
    ones1_h = edecl("ones1", [128, 1], BF16, "ExternalInput")
    ones_row1_h = edecl("ones_row1", [1, 128], BF16, "ExternalInput")

    xout = edecl("xout", [TOK, D], F32, "ExternalOutput")
    ikind = "ExternalOutput" if dbg else "Internal"
    g_hbm = edecl("g_sc", [TOK, D], BF16, ikind)
    attnout = edecl("attnout", [512, 2048], BF16, ikind)
    x1_hbm = edecl("x1_sc", [512, D], F32, ikind)
    x1b_hbm = edecl("x1b_sc", [512, D], BF16, ikind)
    h2_hbm = edecl("h2_sc", [512, D], BF16, ikind)
    gsel_hbm = edecl("gsel_sc", [512, 8192], BF16, ikind)
    gu_hbm = edecl("gu_sc", [512, 8192], BF16, ikind)
    proc_hbm = edecl("proc_sc", [512, D], F32, ikind)

    es = contextlib.ExitStack()
    with es:
        if k is None:
            block = es.enter_context(nc.Block())
            k = K(nc, block, es)
            for s_ in ["qs", "qg", "mt", "mv", "ms", "mg"]:
                k.sem(s_)
        else:
            block = k.block
            k = K.sub(k, es)
        cst = {}
        cst["ident_bf"] = k.sb("c_identbf", [128, 128], BF16)
        cst["ident_f32"] = k.sb("c_identf32", [128, 128], F32)
        cst["ones1"] = k.sb("c_ones1", [128, 1], BF16)
        cst["ones_row1"] = k.sb("c_onesrow1", [1, 128], BF16)
        cst["idxu"] = k.sb("c_idxu", [128, 4], U32)
        cst["selsc"] = k.sb("c_selsc", [128, 4], F32)
        ln2rep = k.sb("c_ln2r", [128, D], F32)

        @block.sync
        def _(sy):
            k.dma(sy, cst["ident_bf"][:], ident_bf_h, "qs")
            k.dma(sy, cst["ident_f32"][:], ident_f32_h, "qs")
            k.dma(sy, cst["ones1"][:], ones1_h, "qs")
            k.dma(sy, cst["ones_row1"][:], ones_row1_h, "qs")
            k.dma(sy, cst["idxu"][:], idx_in, "qs")
            k.dma(sy, cst["selsc"][:], selsc_in, "qs")
            k.dma(sy, ln2rep[:], ln2rep_h, "qs")
        k.barrier()

        pio = dict(qsel_hbm=qsel, kall_hbm=kall, vall_hbm=vall, amq_hbm=amq_h,
                   amk_hbm=amk_h, attnout_hbm=attnout, selx_hbm=selx,
                   proc_hbm=proc_hbm, g_hbm=g_hbm, x_hbm=x_h, s_hbm=s_h,
                   xout_hbm=xout)

        phase_silu_tiles(k, xn_bf, g_hbm, NTT, "gb")
        phase_sel_attn(k, pio, cst)

        with scoped(k) as kp:
            evac = make_x1_evac(kp, selx, x1_hbm, x1b_hbm, "x1")
            phase_proj(kp, attnout, wo_h, 2048, evac, "po", evac_sem="mv",
                       HT=512, NTOK=512)
        k.barrier()

        phase_rms(k, x1b_hbm, h2_hbm, ln2rep, 4, 1e-6, "rms2")

        with scoped(k) as kp:
            evac = make_gate_evac_sel(kp, gsel_hbm, "mg")
            phase_proj(kp, h2_hbm, wg_h, 8192, evac, "pg", evac_sem="ms",
                       HT=512, NTOK=512)
        k.barrier()

        with scoped(k) as kp:
            evac = make_up_evac_sel(kp, gsel_hbm, gu_hbm, "mu")
            phase_proj(kp, h2_hbm, wu_h, 8192, evac, "pu", evac_sem="mv",
                       HT=512, NTOK=512)
        k.barrier()

        with scoped(k) as kp:
            evac = make_proc_evac(kp, x1_hbm, proc_hbm, "pd")
            phase_proj(kp, gu_hbm, wd_h, 2048, evac, "pdn", evac_sem="mv",
                       KSTR=64, HT=512, NTOK=512, PASSW=512, nbuf=1)
        k.barrier()

        phase_blend_scatter(k, pio, cst)
        phase_residual(k, pio)
    return nc


def prep_l2b2_inputs(xn_bf16, x_f32, s_bf16, qsel, kall, vall, selx, idx_u32,
                     selsc, w, core_id):
    out = {
        "xn_bf": np.ascontiguousarray(xn_bf16, bf),
        "x": np.ascontiguousarray(x_f32, np.float32),
        "s_in": np.ascontiguousarray(s_bf16, bf),
        "qsel": np.ascontiguousarray(qsel, bf),
        "kall": np.ascontiguousarray(kall, bf),
        "vall": np.ascontiguousarray(vall, bf),
        "selx": np.ascontiguousarray(selx, bf),
        "idx": np.ascontiguousarray(idx_u32, np.uint32),
        "selsc": np.ascontiguousarray(selsc, np.float32),
        "wo": np.ascontiguousarray(w["L2_ffn_wo"], np.float32),
        "wg": np.ascontiguousarray(w["L2_ffn_w_gate"], np.float32),
        "wu": np.ascontiguousarray(w["L2_ffn_w_up"], np.float32),
        "wd": np.ascontiguousarray(w["L2_ffn_w_down"], np.float32),
        "ln2rep": np.tile(w["L2_ffn_ln2_w"][None, :], (128, 1)).astype(np.float32),
        "ident_bf": np.eye(128, dtype=bf),
        "ident_f32": np.eye(128, dtype=np.float32),
        "ones1": np.ones((128, 1), bf),
        "ones_row1": np.ones((1, 128), bf),
    }
    qg = core_id * 512 + np.arange(512)
    kg = np.arange(4096)
    allow = kg[None, :] <= qg[:, None]
    amq = np.where(allow, 0.0, -1e30).astype(bf)
    out["amq"] = amq
    out["amk"] = np.ascontiguousarray(amq.T)
    return out


def kernel_multilaunch(**inputs):
    from concourse.bass_utils import run_bass_kernel_spmd

    w = {n: np.asarray(v) for n, v in inputs.items()}
    x = np.ascontiguousarray(np.asarray(w["L0_attn_x"], dtype=np.float32))
    xs = [np.ascontiguousarray(x[2 * c:2 * c + 2].reshape(TOK, D)) for c in range(8)]
    cores = list(range(8))

    def VB(a):
        a = np.asarray(a)
        return a.view(bf) if a.dtype.kind == 'V' else a.astype(bf)

    def launch(build_fn, prep_fn, **bkw):
        nc = build_fn(bass.Bass("TRN2", target_bir_lowering=False), **bkw)
        ims = [prep_fn(c) for c in cores]
        return run_bass_kernel_spmd(nc, ims, core_ids=cores).results

    r = launch(build_l0, lambda c: prep_l0_inputs(xs[c], w))
    xs = [np.asarray(r[c]["xout"], dtype=np.float32) for c in cores]
    r = launch(build_l1, lambda c: prep_l1_inputs(xs[c], w))
    xs = [np.asarray(r[c]["xout"], dtype=np.float32) for c in cores]
    r = launch(build_l2_mamba_only, lambda c: prep_l2_mamba(xs[c], w), xn_out=True)
    s_bf = [VB(r[c]["s_out"]) for c in cores]
    xnbf = [VB(r[c]["xn_bf"]) for c in cores]
    r = launch(build_l2b1, lambda c: prep_l2b1_inputs(xnbf[c], w))
    kall = np.concatenate([VB(r[c]["ksel"]) for c in cores], axis=0)
    vall = np.concatenate([VB(r[c]["vsel"]) for c in cores], axis=0)
    qsel = [VB(r[c]["qsel"]) for c in cores]
    selx = [VB(r[c]["selx"]) for c in cores]
    idxs = [np.asarray(r[c]["idx"]) for c in cores]
    selscs = [np.asarray(r[c]["selsc"]) for c in cores]
    r = launch(build_l2b2,
               lambda c: prep_l2b2_inputs(xnbf[c], xs[c], s_bf[c], qsel[c], kall,
                                          vall, selx[c], idxs[c], selscs[c], w,
                                          core_id=c))
    xs = [np.asarray(r[c]["xout"], dtype=np.float32) for c in cores]
    r = launch(build_l3, lambda c: prep_l3_inputs(xs[c], w))
    xs = [np.asarray(r[c]["xout"], dtype=np.float32) for c in cores]

    return np.concatenate([xs[c].reshape(2, T, D) for c in cores], axis=0)


def build_full(nc, layers=(0, 1, 2, 3, 4, 5)):
    io = {}
    innames = set()

    def dr(pfx, name, shape, dt=F32, kind="Internal"):
        full = pfx + name
        io[full] = nc.dram_tensor(full, shape, dt, kind=kind).ap()
        if kind == "ExternalInput":
            innames.add(full)
        return io[full]

    EI = "ExternalInput"
    for nm, sh, dt in [
            ("x", [TOK, D], F32), ("wqT", [D, 2048], F32), ("wkT", [D, 2048], F32),
            ("wvT", [D, 2048], F32), ("woT", [2048, D], F32), ("fwT", [D, 2048], F32),
            ("qnw_rep", [128, 2048], F32), ("knw_rep", [128, 2048], F32),
            ("cos_rep", [T, 1024], F32), ("sin_rep", [T, 1024], F32),
            ("lnw_rep", [128, D], F32), ("lnb_rep", [128, D], F32),
            ("gamma_rep", [128, D], F32), ("beta_rep", [128, D], F32),
            ("ident_bf", [128, 128], BF16), ("ident_f32", [128, 128], F32),
            ("ones1", [128, 1], BF16), ("ones_row1", [1, 128], BF16),
            ("ones_rowf", [1, 128], F32), ("masks_f32", [4 * 128, 512], F32)]:
        dr("l0_", nm, sh, dt, EI)
    for nm, sh, dt in [("xout", [TOK, D], F32), ("xn_bf", [TOK, D], BF16),
                       ("xn2_bf", [TOK, D], BF16), ("q_sc", [TOK, 2048], BF16),
                       ("k_sc", [TOK, 2048], BF16), ("v_sc", [TOK, 2048], BF16),
                       ("g_sc", [TOK, D], BF16)]:
        dr("l0_", nm, sh, dt)
    for nm, sh, dt in [
            ("wqT", [D, 2048], F32), ("wkT", [D, 2048], F32), ("wvT", [D, 2048], F32),
            ("woT", [2048, D], F32), ("wgT", [D, 5632], I32), ("wuT", [D, 5632], I32),
            ("wdT", [5632, D], I32), ("sgate_rep", [128, 5632], F32),
            ("sup_rep", [128, 5632], F32), ("sdown_rep", [128, 2048], F32),
            ("qnw_rep", [128, 2048], F32), ("knw_rep", [128, 2048], F32),
            ("ident_bf", [128, 128], BF16), ("ident_f32", [128, 128], F32),
            ("ones1", [128, 1], BF16), ("ones_row1", [1, 128], BF16),
            ("masks_bf", [4 * 128, 512], BF16), ("masks2_bf", [4 * 128, 512], BF16)]:
        dr("l1_", nm, sh, dt, EI)
    for nm, sh, dt in [("xout", [TOK, D], F32), ("xn_bf", [TOK, D], BF16),
                       ("q_sc", [TOK, 2048], BF16), ("k_sc", [TOK, 2048], BF16),
                       ("v_sc", [TOK, 2048], BF16), ("g_sc", [TOK, D], BF16),
                       ("gate_sc", [TOK, 5632], BF16), ("guT_sc", [TOK, 5632], BF16)]:
        dr("l1_", nm, sh, dt)
    io["l1_x"] = io["l0_xout"]
    for nm, sh, dt in [
            ("ipT_tiled", [64, 128, 2048], F32), ("convw_tiled", [128, 128], F32),
            ("xpT_tiled", [128, 32 * 256], F32), ("dtwT_h", [128, 4096], F32),
            ("dtb_h", [128, 32], F32), ("alog_h", [128, 512], F32),
            ("dp_h", [128, 32], F32), ("convb_h", [128, 32], F32),
            ("opT_tiled", [16, 128, 4096], F32), ("ohB", [32, 16 * 128], BF16),
            ("ohC", [32, 16 * 128], BF16), ("zeros4", [128, 4], BF16),
            ("ident_bf", [128, 128], BF16), ("ident_f32", [128, 128], F32)]:
        dr("l2a_", nm, sh, dt, EI)
    for nm, sh, dt in [("s_out", [2 * D, T], BF16), ("xn_bf", [TOK, D], BF16),
                       ("u_sc", [4096, T], BF16), ("gate_sc", [4096, T], BF16),
                       ("u2_sc", [4096, T], BF16)]:
        dr("l2a_", nm, sh, dt)
    io["l2a_x"] = io["l1_xout"]
    for nm, sh, dt in [
            ("wrT", [D, 512], F32), ("cos_tab", [T, 1024], F32),
            ("sin_tab", [T, 1024], F32), ("init_lo", [128, 2], F32),
            ("init_hi", [128, 2], F32), ("c_ones128", [128, 128], BF16),
            ("c_lt", [128, 128], BF16), ("c_onesrow", [128, 32], F32),
            ("c_onesrowf", [1, 128], F32), ("c_iota", [128, 256], F32),
            ("c_tvals", [128, 32], F32), ("ln1rep", [128, D], F32),
            ("bqrep", [128, 2048], F32), ("bkrep", [128, 1024], F32),
            ("bvrep", [128, 1024], F32), ("wq", [D, 2048], F32),
            ("wk", [D, 1024], F32), ("wv", [D, 1024], F32)]:
        dr("l2b1_", nm, sh, dt, EI)
    for nm, sh, dt in [("scores", [TOK, 16], F32), ("idx", [128, 4], U32),
                       ("posw", [128, 4], U32), ("selm", [128, 32], F32),
                       ("selx", [512, D], BF16), ("selsc", [128, 4], F32),
                       ("qsel", [512, 2048], BF16), ("cossel", [512, 1024], BF16),
                       ("sinsel", [512, 1024], BF16), ("hsel", [512, D], BF16)]:
        dr("l2b1_", nm, sh, dt)
    io["l2b1_xn_bf"] = io["l2a_xn_bf"]
    cc_in = nc.dram_tensor("cc_in", [512, 2048], BF16, kind="Internal").ap()
    cc_out = nc.dram_tensor("cc_out", [4096, 2048], BF16, kind="Internal").ap()
    io["l2b1_ksel"] = cc_in[:, 0:1024]
    io["l2b1_vsel"] = cc_in[:, 1024:2048]
    for nm, sh, dt in [
            ("selsc_dummy", [1, 1], F32)]:
        pass
    for nm, sh, dt in [
            ("amq", [512, 4096], BF16), ("amk", [4096, 512], BF16),
            ("wo", [2048, 2048], F32), ("wg", [2048, 8192], F32),
            ("wu", [2048, 8192], F32), ("wd", [8192, 2048], F32),
            ("ln2rep", [128, D], F32), ("ident_bf", [128, 128], BF16),
            ("ident_f32", [128, 128], F32), ("ones1", [128, 1], BF16),
            ("ones_row1", [1, 128], BF16)]:
        dr("l2b2_", nm, sh, dt, EI)
    for nm, sh, dt in [("xout", [TOK, D], F32), ("g_sc", [TOK, D], BF16),
                       ("attnout", [512, 2048], BF16), ("x1_sc", [512, D], F32),
                       ("x1b_sc", [512, D], BF16), ("h2_sc", [512, D], BF16),
                       ("gsel_sc", [512, 8192], BF16), ("gu_sc", [512, 8192], BF16),
                       ("proc_sc", [512, D], F32)]:
        dr("l2b2_", nm, sh, dt)
    io["l2b2_xn_bf"] = io["l2a_xn_bf"]
    io["l2b2_x"] = io["l1_xout"]
    io["l2b2_s_in"] = io["l2a_s_out"]
    io["l2b2_qsel"] = io["l2b1_qsel"]
    io["l2b2_kall"] = cc_out[:, 0:1024]
    io["l2b2_vall"] = cc_out[:, 1024:2048]
    io["l2b2_selx"] = io["l2b1_selx"]
    io["l2b2_idx"] = io["l2b1_idx"]
    io["l2b2_selsc"] = io["l2b1_selsc"]
    for nm, sh, dt in [
            ("wqT", [D, 2048], F32), ("wkT", [D, 512], F32), ("wvT", [D, 512], F32),
            ("woT", [2048, D], F32), ("awT", [D, 1024], F32), ("bwT", [512, D], F32),
            ("cos_repq", [T, 1024], F32), ("sin_repq", [T, 1024], F32),
            ("cos_repk", [T, 256], F32), ("sin_repk", [T, 256], F32),
            ("ident_bf", [128, 128], BF16), ("ident_f32", [128, 128], F32),
            ("ones1", [128, 1], BF16), ("ones_row1", [1, 128], BF16),
            ("masks_bf", [4 * 128, 512], BF16), ("masks2_bf", [4 * 128, 512], BF16)]:
        dr("l3_", nm, sh, dt, EI)
    for nm, sh, dt in [("xn_bf", [TOK, D], BF16), ("q_sc", [TOK, 2048], BF16),
                       ("k_sc", [TOK, 512], BF16), ("v_sc", [TOK, 512], BF16),
                       ("g_sc", [TOK, D], BF16), ("h_sc", [TOK, 1024], BF16),
                       ("hlg_sc", [TOK, 8], F32)]:
        dr("l3_", nm, sh, dt)
    io["l3_x"] = io["l2b2_xout"]
    io["l3_xout"] = nc.dram_tensor("xout", [TOK, D], F32, kind="ExternalOutput").ap()

    def sub_io(pfx):
        n = len(pfx)
        return {kk[n:]: v for kk, v in io.items() if kk.startswith(pfx)}

    es = contextlib.ExitStack()
    with es:
        block = es.enter_context(nc.Block())
        kroot = K(nc, block, es)
        for s_ in ["qs", "qg", "mt", "mv", "ms", "mg", "cc"]:
            kroot.sem(s_)

        if 0 in layers:
            build_l0(nc, k=K.sub(kroot, None, "l0_"), io=sub_io("l0_"))
            kroot.barrier()
        if 1 in layers:
            build_l1(nc, k=K.sub(kroot, None, "l1_"), io=sub_io("l1_"))
            kroot.barrier()
        if 2 in layers:
            build_l2_mamba_only(nc, k=K.sub(kroot, None, "l2a_"), io=sub_io("l2a_"))
            kroot.barrier()
        if 3 in layers:
            build_l2b1(nc, k=K.sub(kroot, None, "l2b1_"), io=sub_io("l2b1_"))
            kroot.barrier()

        if 4 in layers:
            @block.gpsimd
            def _(g):
                kroot.inc("cc", g.collective_compute(
                    "AllGather", ALU.bypass,
                    replica_groups=[list(range(NCORES))],
                    ins=[cc_in.opt()], outs=[cc_out.opt()]), 1)
            kroot.barrier()

            build_l2b2(nc, k=K.sub(kroot, None, "l2b2_"), io=sub_io("l2b2_"))
            kroot.barrier()
        if 5 in layers:
            build_l3(nc, k=K.sub(kroot, None, "l3_"), io=sub_io("l3_"))
        if 5 not in layers:
            src = io["l0_x"]

            @block.sync
            def _(sy):
                for m_ in range(NTT):
                    kroot.dma(sy, io["l3_xout"][m_ * 128:(m_ + 1) * 128, :],
                              src[m_ * 128:(m_ + 1) * 128, :], "qs")
            kroot.barrier()
    return nc, innames


def prep_full_inputs(xc, w, core_id):
    zf = np.zeros((TOK, D), np.float32)
    zb = zf.astype(bf)
    out = {}
    for pfx, d in [
            ("l0_", prep_l0_inputs(xc, w)),
            ("l1_", prep_l1_inputs(zf, w)),
            ("l2a_", prep_l2_mamba(zf, w)),
            ("l2b1_", prep_l2b1_inputs(zb, w)),
            ("l2b2_", prep_l2b2_inputs(zb, zf, np.zeros((2 * D, T), bf),
                                       np.zeros((512, 2048), bf),
                                       np.zeros((4096, 1024), bf),
                                       np.zeros((4096, 1024), bf),
                                       np.zeros((512, D), bf),
                                       np.zeros((128, 4), np.uint32),
                                       np.zeros((128, 4), np.float32), w, core_id)),
            ("l3_", prep_l3_inputs(zf, w))]:
        for kk, v in d.items():
            out[pfx + kk] = v
    return out


def kernel(**inputs):
    from concourse.bass_utils import run_bass_kernel_spmd

    w = {n: np.asarray(v) for n, v in inputs.items()}
    x = np.ascontiguousarray(np.asarray(w["L0_attn_x"], dtype=np.float32))
    cores = list(range(8))
    nc, innames = build_full(bass.Bass("TRN2", target_bir_lowering=False))
    ims = []
    for c in cores:
        xc = np.ascontiguousarray(x[2 * c:2 * c + 2].reshape(TOK, D))
        im = prep_full_inputs(xc, w, c)
        ims.append({kk: v for kk, v in im.items() if kk in innames})
    res = run_bass_kernel_spmd(nc, ims, core_ids=cores).results
    return np.concatenate(
        [np.asarray(res[c]["xout"], dtype=np.float32).reshape(2, T, D)
         for c in cores], axis=0)
